# version 02
# baseline (speedup 1.0000x reference)
import numpy as np
import ml_dtypes
from contextlib import ExitStack

import concourse.bass as bass
import concourse.mybir as mybir
from concourse.bass import IndirectOffsetOnAxis
from concourse.bass_utils import run_bass_kernel_spmd

F32 = mybir.dt.float32
BF16 = mybir.dt.bfloat16
U32 = mybir.dt.uint32
AF = mybir.ActivationFunctionType
ALU = mybir.AluOpType
AX = mybir.AxisListType

NCHUNK = 32
NEXP = 16
SLOTS = 384
NSLOT = NEXP * SLOTS
CAP = 2560.0
EPS = 1e-5
OOBB = 8192.0
CH = 1024


class KB:
    def __init__(self, nc):
        self.nc = nc
        self.cnt = {}
        self.sems = {}
        self.ops = {k: [] for k in ("tensor", "vector", "scalar", "gpsimd", "sync")}
        self.sem_stack = ExitStack()

    def sem(self, name):
        if name not in self.sems:
            self.sems[name] = self.sem_stack.enter_context(self.nc.semaphore(name))
        return self.sems[name]

    def get(self, name):
        return self.cnt.get(name, 0)

    def op(self, eng, fn, waits=(), inc=None, inc_n=1):
        wait_h = [(self.sem(s), int(v)) for (s, v) in waits]
        ev = None
        sem_h = None
        if inc:
            sem_h = self.sem(inc)
            ev = self.cnt[inc] = self.cnt.get(inc, 0) + inc_n

        def play(e, fn=fn, wait_h=wait_h, sem_h=sem_h, inc_n=inc_n):
            for h, v in wait_h:
                e.wait_ge(h, v)
            r = fn(e)
            if sem_h is not None:
                r.then_inc(sem_h, inc_n)
        self.ops[eng].append(play)
        return ev

    def play(self, block):
        for eng in ("tensor", "vector", "scalar", "gpsimd", "sync"):
            ops = self.ops[eng]
            self.ops[eng] = []
            if not ops:
                continue
            sec = getattr(block, eng)

            def body(e, _ops=ops):
                for f in _ops:
                    f(e)
            sec(body)

    def barrier(self, name, extra_waits=()):
        target = self.get(name) + 5
        for eng in self.ops:
            w = list(extra_waits) if eng == "sync" else []
            self.op(eng, lambda e: e.nop(), waits=w, inc=name)
        for eng in self.ops:
            self.op(eng, lambda e: e.nop(), waits=[(name, target)])


def alloc(ctx, nc, kind, name, shape, dtype):
    if kind == "sbuf":
        return ctx.enter_context(nc.sbuf_tensor(name, shape, dtype))
    return ctx.enter_context(nc.psum_tensor(name, shape, dtype))



def alloc_ffn_sbuf(ctx, nc, tagc):
    t = {}
    sb = lambda n, s, d: alloc(ctx, nc, "sbuf", n + tagc, s, d)
    t["xstage"] = [sb(f"xstage{i}", [128, 1024], F32) for i in range(2)]
    t["bn6"] = [sb(f"bn6{i}", [128, 12], F32) for i in range(2)]
    t["mv"] = [sb(f"mv{i}", [128, 2], F32) for i in range(2)]
    t["sqv"] = [sb(f"sqv{i}", [128, 1], F32) for i in range(2)]
    t["rstd"] = [sb(f"rstd{i}", [128, 1], F32) for i in range(2)]
    t["lnbf"] = sb("lnbf", [128, NCHUNK * CH], BF16)
    t["lnt_sc"] = [sb(f"lntsc{i}", [128, 1024], BF16) for i in range(2)]
    t["lg_sc"] = [sb(f"lgsc{i}", [16, 128], F32) for i in range(2)]
    t["nm"] = [sb(f"nm{i}", [128, 1], F32) for i in range(2)]
    t["et"] = [sb(f"et{i}", [128, 16], F32) for i in range(2)]
    t["S"] = [sb(f"S{i}", [128, 1], F32) for i in range(2)]
    t["p_big"] = sb("p_big", [128, 32], F32)
    t["D0"] = [sb(f"D0{i}", [128, 16], F32) for i in range(2)]
    t["Csc"] = [sb(f"Csc{i}", [128, 16], F32) for i in range(2)]
    t["Dm"] = [sb(f"Dm{i}", [128, 16], F32) for i in range(2)]
    t["D_big"] = sb("D_big", [128, NCHUNK * 16], F32)
    t["route"] = sb("route", [16, 2 * 4096], F32)
    t["D_T"] = t["route"][:, 0:4096]
    t["incl"] = t["route"][:, 4096:8192]
    t["ranksT"] = t["route"][:, 0:4096]
    t["hist_all"] = sb("hist_all", [8, 16], F32)
    t["offs_bc"] = sb("offs_bc", [128, 16], F32)
    t["slot_big"] = sb("slot_big", [128, 32], F32)
    t["fits_big"] = sb("fits_big", [128, 32], F32)
    t["coef_big"] = sb("coef_big", [128, 32], F32)
    t["g_big"] = sb("g_big", [128, 32], U32)
    for nm_ in ("sc16a", "sc16b", "sc16c"):
        t[nm_] = [sb(f"{nm_}{i}", [128, 16], F32) for i in range(2)]
    for nm_ in ("sc1a", "sc1b", "sc1c", "sc1d", "sc1e", "sc1f", "sc1g", "sc1h",
                "oobv"):
        t[nm_] = [sb(f"{nm_}{i}", [128, 1], F32) for i in range(2)]
    t["xet"] = [sb(f"xet{i}", [128, 8 * 384], BF16) for i in range(2)]
    t["w1b"] = [sb(f"w1b{i}", [128, 2048], BF16) for i in range(4)]
    t["w2b"] = [sb(f"w2b{i}", [128, 2048], BF16) for i in range(4)]
    t["ht"] = [sb(f"ht{i}", [128, 384], BF16) for i in range(2)]
    t["yev"] = [sb("yev0", [128, 3072], F32)] * 2
    t["outg"] = [sb(f"outg{i}", [128, 1024], F32) for i in range(2)]
    t["xstage2"] = [sb(f"xstage2{i}", [128, 1024], F32) for i in range(2)]
    t["xnew"] = [sb(f"xnew{i}", [128, 1024], F32) for i in range(2)]
    return t


def emit_ffn_p1(kb, c, t, xin, W1R, W2R):
    pf_w1 = []
    pf_w2 = []
    for gs in range(4):
        pf_w1.append(kb.op("gpsimd", lambda e, gs=gs: e.dma_start(
            t["w1b"][gs][:], W1R[gs:gs + 1], max_dma_last_dim=8192),
            inc="f_w1", inc_n=16))
        pf_w2.append(kb.op("gpsimd", lambda e, gs=gs: e.dma_start(
            t["w2b"][gs][:], W2R[gs:gs + 1], max_dma_last_dim=8192),
            inc="f_w2", inc_n=16))
    ev_ln = [None] * NCHUNK
    ev_lnT = [None] * NCHUNK
    ev_gate = [None] * NCHUNK
    ev_lg = [None] * NCHUNK
    ev_D = [None] * NCHUNK
    ev_DT = [None] * NCHUNK

    for j in range(NCHUNK):
        q = j % 2
        w = [("f_ln", ev_ln[j - 2])] if j >= 2 else []
        ev_dx = kb.op("sync", lambda e, q=q, j=j: e.dma_start(
            t["xstage"][q][:], xin[j * 128:(j + 1) * 128, :]),
            waits=w, inc="f_dx", inc_n=16)

        kb.op("vector", lambda e, q=q: e.bn_stats(
            t["bn6"][q][:, 0:6], t["xstage"][q][:, 0:512]),
            waits=[("f_dx", ev_dx)])
        e1 = kb.op("vector", lambda e, q=q: e.bn_stats(
            t["bn6"][q][:, 6:12], t["xstage"][q][:, 512:1024]), inc="f_vch")
        e2 = kb.op("vector", lambda e, q=q: e.bn_aggr(
            t["mv"][q][:], t["bn6"][q][:]), waits=[("f_vch", e1)], inc="f_vch")
        e3 = kb.op("scalar", lambda e, q=q: e.activation(
            t["sqv"][q][:], t["mv"][q][:, 1:2], AF.Sqrt,
            bias=c["epsv"][:, 0:1]),
            waits=[("f_vch", e2)], inc="f_sch")
        e4 = kb.op("vector", lambda e, q=q: e.reciprocal(
            t["rstd"][q][:], t["sqv"][q][:]), waits=[("f_sch", e3)], inc="f_vch")
        ev_ln[j] = kb.op("vector", lambda e, q=q, j=j: e.tensor_scalar(
            t["lnbf"][:, j * CH:(j + 1) * CH], t["xstage"][q][:],
            t["mv"][q][:, 0:1], t["rstd"][q][:], ALU.subtract, ALU.mult),
            waits=[("f_vch", e4)], inc="f_ln")

        ev = None
        for ci in range(8):
            w = []
            if ci == 0:
                w.append(("f_ln", ev_ln[j]))
                if j >= 2:
                    w.append(("f_lnT", ev_lnT[j - 2]))
            ev = kb.op("tensor", lambda e, q=q, j=j, ci=ci: e.transpose(
                t["psumT"][q][:, ci * 128:(ci + 1) * 128],
                t["lnbf"][:, j * CH + ci * 128: j * CH + (ci + 1) * 128],
                c["identbf"][:]),
                waits=w, inc="f_tT" if ci == 7 else None)
        w = [("f_tT", ev)]
        if j >= 2:
            w.append(("f_gate", ev_gate[j - 2]))
        ev_lnT[j] = kb.op("scalar", lambda e, q=q: e.copy(
            t["lnt_sc"][q][:], t["psumT"][q][:]), waits=w, inc="f_lnT")

        ev = None
        for ci in range(8):
            w = []
            if ci == 0:
                w.append(("f_lnT", ev_lnT[j]))
                if j >= 2:
                    w.append(("f_lg", ev_lg[j - 2]))
            ev = kb.op("tensor", lambda e, q=q, ci=ci: e.matmul(
                t["psumG"][q][:], c["gw_sb"][:, ci * 16:(ci + 1) * 16],
                t["lnt_sc"][q][:, ci * 128:(ci + 1) * 128],
                start=(ci == 0), stop=(ci == 7)),
                waits=w, inc="f_gate" if ci == 7 else None)
        ev_gate[j] = ev
        ev_lg[j] = kb.op("scalar", lambda e, q=q: e.copy(
            t["lg_sc"][q][:], t["psumG"][q][:]),
            waits=[("f_gate", ev_gate[j])], inc="f_lg")
        w = [("f_lg", ev_lg[j])]
        if j >= 2:
            w.append(("f_D", ev_D[j - 2]))
        ev_lT = kb.op("tensor", lambda e, q=q: e.transpose(
            t["psumL"][q][:], t["lg_sc"][q][:], c["identf"][0:16, 0:16]),
            waits=w, inc="f_lT")

        e5 = kb.op("vector", lambda e, q=q: e.reduce_max(
            t["nm"][q][:], t["psumL"][q][:], axis=AX.X, negate=True),
            waits=[("f_lT", ev_lT)], inc="f_vch")
        e6 = kb.op("scalar", lambda e, q=q: e.activation(
            t["et"][q][:], t["psumL"][q][:], AF.Exp, bias=t["nm"][q][:, 0:1],
            accum_out=t["S"][q][:]), waits=[("f_vch", e5)], inc="f_sch")
        kb.op("vector", lambda e, q=q, j=j: e.reciprocal(
            t["p_big"][:, j:j + 1], t["S"][q][:]), waits=[("f_sch", e6)])
        e7 = kb.op("vector", lambda e, q=q: e.tensor_scalar(
            t["D0"][q][:], t["psumL"][q][:], t["nm"][q][:, 0:1], 0.0,
            ALU.add, ALU.is_ge), inc="f_vch")
        e8 = kb.op("vector", lambda e, q=q: e.tensor_tensor_scan(
            t["Csc"][q][:], t["D0"][q][:], t["D0"][q][:], 0.0,
            ALU.add, ALU.bypass), waits=[("f_vch", e7)], inc="f_vch")
        e9 = kb.op("vector", lambda e, q=q: e.tensor_scalar(
            t["Dm"][q][:], t["Csc"][q][:], 1.0, None, ALU.is_le),
            waits=[("f_vch", e8)], inc="f_vch")
        ev_D[j] = kb.op("vector", lambda e, q=q, j=j: e.tensor_tensor(
            t["D_big"][:, j * 16:(j + 1) * 16], t["D0"][q][:], t["Dm"][q][:],
            ALU.mult), waits=[("f_vch", e9)], inc="f_D")

        w = [("f_D", ev_D[j])]
        if j >= 2:
            w.append(("f_DT", ev_DT[j - 2]))
        evt = kb.op("tensor", lambda e, q=q, j=j: e.transpose(
            t["psumD"][q][:], t["D_big"][:, j * 16:(j + 1) * 16],
            c["identf"][:]), waits=w, inc="f_tD")
        ev_DT[j] = kb.op("scalar", lambda e, q=q, j=j: e.copy(
            t["D_T"][:, j * 128:(j + 1) * 128], t["psumD"][q][:]),
            waits=[("f_tD", evt)], inc="f_DT")

    return {"ev_D": ev_D, "ev_DT": ev_DT, "pf_w1": pf_w1, "pf_w2": pf_w2}


def emit_ffn_p2(kb, c, t, ev1, XeD, ccin, ccout):
    ev_scan = kb.op("vector", lambda e: e.tensor_tensor_scan(
        t["incl"], t["D_T"], t["D_T"], 0.0, ALU.add, ALU.bypass),
        waits=[("f_DT", ev1["ev_DT"][NCHUNK - 1])], inc="f_vch")
    ev_rk = kb.op("vector", lambda e: e.tensor_tensor(
        t["ranksT"], t["incl"], t["D_T"], ALU.subtract),
        waits=[("f_vch", ev_scan)], inc="f_rk")
    ev_h = kb.op("sync", lambda e: e.dma_start(
        ccin, t["incl"][:, 4095:4096]), waits=[("f_rk", ev_rk)],
        inc="f_hist", inc_n=16)
    ev_cc = kb.op("gpsimd", lambda e: e.collective_compute(
        "AllGather", ALU.bypass, replica_groups=[list(range(8))],
        ins=[ccin], outs=[ccout]), waits=[("f_hist", ev_h)], inc="f_cc")
    ev_hb = kb.op("scalar", lambda e: e.dma_start(
        t["hist_all"][:], ccout), waits=[("f_cc", ev_cc)],
        inc="f_ccb", inc_n=16)

    ev_sg = [None] * NCHUNK
    ev_scat = [None] * NCHUNK
    ev_D = ev1["ev_D"]
    for j in range(NCHUNK):
        q = j % 2
        w = [("f_rk", ev_rk)]
        if j >= 2:
            w.append(("f_slot", ev_sg[j - 2]))
        ev_rtp = kb.op("tensor", lambda e, q=q, j=j: e.transpose(
            t["psumR"][q][:], t["ranksT"][:, j * 128:(j + 1) * 128],
            c["identf"][0:16, 0:16]), waits=w, inc="f_rtp")

        Dj = t["D_big"][:, j * 16:(j + 1) * 16]
        e1a = kb.op("vector", lambda e, q=q, Dj=Dj: e.tensor_tensor(
            t["sc16a"][q][:], t["psumR"][q][:], Dj, ALU.mult),
            waits=[("f_rtp", ev_rtp), ("f_D", ev_D[j])], inc="f_vch")
        e1 = kb.op("vector", lambda e, q=q, j=j: e.reduce_sum(
            t["slot_big"][:, j:j + 1], t["sc16a"][q][:], axis=AX.X),
            waits=[("f_vch", e1a)], inc="f_vch")
        e2 = kb.op("vector", lambda e, j=j: e.tensor_scalar(
            t["fits_big"][:, j:j + 1], t["slot_big"][:, j:j + 1],
            float(SLOTS), None, ALU.is_lt), waits=[("f_vch", e1)], inc="f_vch")
        e3a = kb.op("vector", lambda e, q=q, Dj=Dj: e.tensor_tensor(
            t["sc16b"][q][:], c["cvec384"][:], Dj, ALU.mult),
            waits=[("f_vch", e2)], inc="f_vch")
        e3 = kb.op("vector", lambda e, q=q: e.reduce_sum(
            t["sc1a"][q][:], t["sc16b"][q][:], axis=AX.X),
            waits=[("f_vch", e3a)], inc="f_vch")
        e4 = kb.op("vector", lambda e, q=q, j=j: e.tensor_tensor(
            t["sc1b"][q][:], t["sc1a"][q][:], t["slot_big"][:, j:j + 1],
            ALU.add), waits=[("f_vch", e3)], inc="f_vch")
        e5 = kb.op("vector", lambda e, q=q, j=j: e.tensor_scalar(
            t["oobv"][q][:], c["iotap"][:], OOBB + 128.0 * j, None, ALU.add),
            waits=[("f_vch", e4)], inc="f_vch")
        e6 = kb.op("vector", lambda e, q=q: e.tensor_tensor(
            t["sc1c"][q][:], t["sc1b"][q][:], t["oobv"][q][:], ALU.subtract),
            waits=[("f_vch", e5)], inc="f_vch")
        e7 = kb.op("vector", lambda e, q=q, j=j: e.scalar_tensor_tensor(
            t["sc1d"][q][:], t["sc1c"][q][:], t["fits_big"][:, j:j + 1],
            t["oobv"][q][:], ALU.mult, ALU.add), waits=[("f_vch", e6)],
            inc="f_vch")
        ev_sg[j] = kb.op("vector", lambda e, q=q, j=j: e.tensor_copy(
            t["g_big"][:, j:j + 1], t["sc1d"][q][:]),
            waits=[("f_vch", e7)], inc="f_slot")

        ev_scat[j] = kb.op("gpsimd", lambda e, j=j: e.indirect_dma_start(
            XeD, IndirectOffsetOnAxis(t["g_big"][:, j:j + 1], 0),
            t["lnbf"][:, j * CH:(j + 1) * CH], None,
            bounds_check=kb.rbc, oob_is_err=False),
            waits=[("f_slot", ev_sg[j])], inc="f_scat", inc_n=16)

    ev_om = kb.op("tensor", lambda e: e.matmul(
        t["psumO"][:], c["coremask_bc"][:], t["hist_all"][:],
        start=True, stop=True),
        waits=[("f_ccb", ev_hb), ("f_slot", ev_sg[NCHUNK - 1])], inc="f_om")
    ev_offs = kb.op("scalar", lambda e: e.copy(
        t["offs_bc"][:], t["psumO"][:]), waits=[("f_om", ev_om)],
        inc="f_offs")

    return {"ev_scat": ev_scat, "ev_offs": ev_offs, "ev_D": ev_D}


def emit_ffn_p34(kb, c, t, ev2, W1R, W2R, XeD, YD, xin, xout, ev1):
    ev_coef = [None] * NCHUNK
    for j in range(NCHUNK):
        q = j % 2
        Dj = t["D_big"][:, j * 16:(j + 1) * 16]
        w = [("f_offs", ev2["ev_offs"])] if j == 0 else []
        e1a = kb.op("vector", lambda e, q=q, Dj=Dj: e.tensor_tensor(
            t["sc16c"][q][:], t["offs_bc"][:], Dj, ALU.mult),
            waits=w, inc="f_vch")
        e1 = kb.op("vector", lambda e, q=q: e.reduce_sum(
            t["sc1e"][q][:], t["sc16c"][q][:], axis=AX.X),
            waits=[("f_vch", e1a)], inc="f_vch")
        e2 = kb.op("vector", lambda e, q=q, j=j: e.tensor_tensor(
            t["sc1f"][q][:], t["sc1e"][q][:], t["slot_big"][:, j:j + 1],
            ALU.add), waits=[("f_vch", e1)], inc="f_vch")
        e3 = kb.op("vector", lambda e, q=q: e.tensor_scalar(
            t["sc1g"][q][:], t["sc1f"][q][:], CAP, None, ALU.is_lt),
            waits=[("f_vch", e2)], inc="f_vch")
        e4 = kb.op("vector", lambda e, q=q, j=j: e.tensor_tensor(
            t["sc1h"][q][:], t["sc1g"][q][:], t["fits_big"][:, j:j + 1],
            ALU.mult), waits=[("f_vch", e3)], inc="f_vch")
        ev_coef[j] = kb.op("vector", lambda e, q=q, j=j: e.tensor_scalar(
            t["coef_big"][:, j:j + 1], t["sc1h"][q][:],
            t["p_big"][:, j:j + 1], 0.5, ALU.mult, ALU.mult),
            waits=[("f_vch", e4)], inc="f_coef")

    scat_join = ev2["ev_scat"][NCHUNK - 1]
    ev_xet = [None] * NEXP
    ev_w1 = [None] * (NEXP * 16)
    ev_w2 = [None] * (NEXP * 16)
    ev_w1[0:4] = ev1["pf_w1"]
    ev_w2[0:4] = ev1["pf_w2"]
    ev_yev = [None] * NEXP
    ev_ydma = [None] * NEXP
    b_hmm = kb.get("f_hmm")
    b_h = kb.get("f_h")
    b_htu = kb.get("f_htu")
    ev_hmm = [b_hmm + 1 + i for i in range(NEXP * 32)]
    ev_h = [b_h + 1 + i for i in range(NEXP * 32)]
    ev_hu = [b_htu + 1 + i for i in range(NEXP * 32)]

    for ei in range(NEXP):
        q = ei % 2
        ev = None
        for ci in range(8):
            w = []
            if ci == 0:
                w.append(("f_scat", scat_join))
                if ei >= 2:
                    w.append(("f_hmm", ev_hmm[(ei - 2) * 32 + 31]))
            ev = kb.op("sync", lambda e, q=q, ei=ei, ci=ci: e.dma_start(
                t["xet"][q][:, ci * 384:(ci + 1) * 384],
                XeD[ei * SLOTS:(ei + 1) * SLOTS, ci * 128:(ci + 1) * 128],
                transpose=True), waits=w, inc="f_xet", inc_n=16)
        ev_xet[ei] = ev

        for sg in range(16):
            gs = ei * 16 + sg
            if gs < 4:
                continue
            qq = gs % 4
            w = [("f_hmm", ev_hmm[(gs - 4) * 2 + 1])]
            ev_w1[gs] = kb.op("gpsimd", lambda e, qq=qq, gs=gs:
                e.dma_start(t["w1b"][qq][:], W1R[gs:gs + 1],
                            max_dma_last_dim=8192),
                waits=w, inc="f_w1", inc_n=16)
            w = [("f_htu", ev_hu[(gs - 4) * 2 + 1])]
            ev_w2[gs] = kb.op("gpsimd", lambda e, qq=qq, gs=gs:
                e.dma_start(t["w2b"][qq][:], W2R[gs:gs + 1],
                            max_dma_last_dim=8192),
                waits=w, inc="f_w2", inc_n=16)

        for sg in range(16):
            gs = ei * 16 + sg
            qq = gs % 4
            for fj in range(2):
                ft = sg * 2 + fj
                gft = ei * 32 + ft
                hq = ft % 2
                ev = None
                for ci in range(8):
                    w = []
                    if ci == 0:
                        w.append(("f_w1", ev_w1[gs]))
                        if ft == 0:
                            w.append(("f_xet", ev_xet[ei]))
                            if ei == 0:
                                w.append(("f_offs", ev2["ev_offs"]))
                        if gft >= 2:
                            w.append(("f_h", ev_h[gft - 2]))
                    ev = kb.op("tensor",
                        lambda e, q=q, qq=qq, hq=hq, ci=ci, fj=fj: e.matmul(
                            t["psumH"][hq][:],
                            t["w1b"][qq][:, ci * 256 + fj * 128:
                                          ci * 256 + fj * 128 + 128],
                            t["xet"][q][:, ci * 384:(ci + 1) * 384],
                            start=(ci == 0), stop=(ci == 7)),
                        waits=w, inc="f_hmm" if ci == 7 else None)
                assert ev == ev_hmm[gft], (ev, ev_hmm[gft])
                w = [("f_hmm", ev_hmm[gft])]
                if gft >= 2:
                    w.append(("f_htu", ev_hu[gft - 2]))
                evg = kb.op("scalar", lambda e, hq=hq: e.activation(
                    t["ht"][hq][:], t["psumH"][hq][:], AF.Gelu_apprx_tanh),
                    waits=w, inc="f_h")
                assert evg == ev_h[gft], (evg, ev_h[gft])
                ev = None
                for st in range(3):
                    for dh in range(2):
                        w = []
                        if st == 0 and dh == 0:
                            w.append(("f_h", ev_h[gft]))
                            w.append(("f_w2", ev_w2[gs]))
                            if ft == 0 and ei >= 1:
                                w.append(("f_yev", ev_yev[ei - 1]))
                        ev = kb.op("tensor",
                            lambda e, hq=hq, qq=qq, st=st, dh=dh, fj=fj,
                            ft=ft: e.matmul(
                                t["psumY"][:, st * 1024 + dh * 512:
                                           st * 1024 + dh * 512 + 512],
                                t["ht"][hq][:, st * 128:(st + 1) * 128],
                                t["w2b"][qq][:, fj * 1024 + dh * 512:
                                              fj * 1024 + dh * 512 + 512],
                                start=(ft == 0), stop=(ft == 31)),
                            waits=w,
                            inc="f_htu" if (st == 2 and dh == 1) else None)
                assert ev == ev_hu[gft], (ev, ev_hu[gft])

        w = [("f_htu", ev_hu[ei * 32 + 31])]
        if ei >= 1:
            w.append(("f_y", ev_ydma[ei - 1]))
        ev_yev[ei] = kb.op("vector", lambda e, q=q: e.tensor_copy(
            t["yev"][q][:], t["psumY"][:]), waits=w, inc="f_yev")
        ev_ydma[ei] = kb.op("sync", lambda e, q=q, ei=ei: e.dma_start(
            YD[ei * SLOTS:(ei + 1) * SLOTS, :].rearrange(
                "(st sp) d -> sp st d", sp=128),
            t["yev"][q][:].rearrange("p (st d) -> p st d", st=3)),
            waits=[("f_yev", ev_yev[ei])],
            inc="f_y", inc_n=16)

    ev_ms = None
    for q in range(2):
        ev_ms = kb.op("vector", lambda e, q=q: e.memset(
            t["outg"][q][:], 0.0), inc="f_vch")
    y_join = ev_ydma[NEXP - 1]
    ev_comb = [None] * NCHUNK
    ev_out = [None] * NCHUNK
    for j in range(NCHUNK):
        q = j % 2
        w = [("f_y", y_join)]
        if j >= 2:
            w.append(("f_comb", ev_comb[j - 2]))
        else:
            w.append(("f_vch", ev_ms))
        ev_g = kb.op("gpsimd", lambda e, q=q, j=j: e.indirect_dma_start(
            t["outg"][q][:], None,
            YD, IndirectOffsetOnAxis(t["g_big"][:, j:j + 1], 0),
            bounds_check=kb.rbc, oob_is_err=False),
            waits=w, inc="f_g", inc_n=16)
        w = [("f_comb", ev_comb[j - 2])] if j >= 2 else []
        ev_x2 = kb.op("sync", lambda e, q=q, j=j: e.dma_start(
            t["xstage2"][q][:], xin[j * 128:(j + 1) * 128, :]),
            waits=w, inc="f_x2", inc_n=16)
        w = [("f_g", ev_g), ("f_x2", ev_x2), ("f_coef", ev_coef[j])]
        if j >= 2:
            w.append(("f_out", ev_out[j - 2]))
        ev_comb[j] = kb.op("vector", lambda e, q=q, j=j:
            e.scalar_tensor_tensor(
                t["xnew"][q][:], t["outg"][q][:], t["coef_big"][:, j:j + 1],
                t["xstage2"][q][:], ALU.mult, ALU.add),
            waits=w, inc="f_comb")
        ev_out[j] = kb.op("sync", lambda e, q=q, j=j: e.dma_start(
            xout[j * 128:(j + 1) * 128, :], t["xnew"][q][:]),
            waits=[("f_comb", ev_comb[j])], inc="f_out", inc_n=16)
    return ev_out[NCHUNK - 1]


def emit_ffn_call(kb, block, ctx_consts, c, nc, xin, xout, W1R, W2R, XeD, YD,
                  ccin, ccout, tagc):
    with ExitStack() as sctx:
        t = alloc_ffn_sbuf(sctx, nc, tagc)
        with ExitStack() as pctx:
            ps = lambda n, s, d: alloc(pctx, nc, "psum", n + tagc, s, d)
            t["psumT"] = [ps(f"psumT{i}", [128, 1024], BF16) for i in range(2)]
            t["psumG"] = [ps(f"psumG{i}", [16, 128], F32) for i in range(2)]
            t["psumL"] = [ps(f"psumL{i}", [128, 16], F32) for i in range(2)]
            t["psumD"] = [ps(f"psumD{i}", [16, 128], F32) for i in range(2)]
            ev1 = emit_ffn_p1(kb, c, t, xin, W1R, W2R)
            kb.play(block)
        with ExitStack() as pctx:
            ps = lambda n, s, d: alloc(pctx, nc, "psum", n + tagc, s, d)
            t["psumR"] = [ps(f"psumR{i}", [128, 16], F32) for i in range(2)]
            t["psumO"] = ps("psumO", [128, 16], F32)
            ev2 = emit_ffn_p2(kb, c, t, ev1, XeD, ccin, ccout)
            kb.play(block)
        with ExitStack() as pctx:
            ps = lambda n, s, d: alloc(pctx, nc, "psum", n + tagc, s, d)
            t["psumH"] = [ps(f"psumH{i}", [128, 384], F32) for i in range(2)]
            t["psumY"] = ps("psumY", [128, 3072], F32)
            ev_last = emit_ffn_p34(kb, c, t, ev2, W1R, W2R, XeD, YD, xin,
                                    xout, ev1)
            kb.play(block)
    return ev_last




def alloc_attn_sbuf(ctx, nc):
    t = {}
    sb = lambda n, s, d: alloc(ctx, nc, "sbuf", "a_" + n, s, d)
    t["xb"] = sb("xb", [128, 16 * 1024], F32)
    t["bn6"] = [sb(f"bn6{i}", [128, 12], F32) for i in range(2)]
    t["mv"] = [sb(f"mv{i}", [128, 2], F32) for i in range(2)]
    t["sqv"] = [sb(f"sqv{i}", [128, 1], F32) for i in range(2)]
    t["rstd"] = [sb(f"rstd{i}", [128, 1], F32) for i in range(2)]
    t["aln"] = [sb(f"aln{i}", [128, 1024], BF16) for i in range(2)]
    t["lnT"] = sb("lnT", [128, 16384], BF16)
    t["wq"] = [sb(f"wq{i}", [128, 1024], BF16) for i in range(2)]
    t["wk"] = [sb(f"wk{i}", [128, 1024], BF16) for i in range(2)]
    t["wv"] = [sb(f"wv{i}", [128, 1024], BF16) for i in range(2)]
    t["qpair"] = sb("qpair", [128, 2048], BF16)
    t["kpair"] = sb("kpair", [128, 2048], BF16)
    t["vaug"] = [sb(f"vaug{i}", [128, 16 * 65], BF16) for i in range(2)]
    t["expT"] = sb("expT", [128, 16 * 512], BF16)
    t["avst"] = [sb(f"avst{i}", [65, 512], F32) for i in range(2)]
    t["rec"] = [sb(f"rec{i}", [128, 1], F32) for i in range(2)]
    return t


def emit_attn(kb, c, t, xmir):
    lnT_v = t["lnT"][:].rearrange("p (c s) -> p c s", c=8)
    psumTA_v = t["psumTA"][:].rearrange("p (c s) -> p c s", c=8)

    last_nrm = None
    last_exp2 = [None, None]
    last_ave = None
    last_av = None
    last_xbo = None
    last_xp_2 = {0: None, 1: None}
    wq_user = {0: None, 1: None}
    wk_user = {0: None, 1: None}
    wv_user = {0: None, 1: None}

    for b in range(2):
        base_tok = b * 2048
        ev_xb = [None] * 16
        ev_lnT = [None] * 16
        ev_tT = [None] * 16
        for i in range(16):
            w = []
            if b == 1 and i == 0:
                w.append(("a_xbo", last_xbo))
            ev_xb[i] = kb.op("sync", lambda e, i=i, base_tok=base_tok:
                e.dma_start(t["xb"][:, i * 1024:(i + 1) * 1024],
                            xmir[base_tok + i * 128: base_tok + (i + 1) * 128, :]),
                waits=w, inc="a_dxb", inc_n=16)
            q = i % 2
            kb.op("vector", lambda e, q=q, i=i: e.bn_stats(
                t["bn6"][q][:, 0:6], t["xb"][:, i * 1024:i * 1024 + 512]),
                waits=[("a_dxb", ev_xb[i])])
            e1 = kb.op("vector", lambda e, q=q, i=i: e.bn_stats(
                t["bn6"][q][:, 6:12], t["xb"][:, i * 1024 + 512:(i + 1) * 1024]),
                inc="a_vch")
            e2 = kb.op("vector", lambda e, q=q: e.bn_aggr(
                t["mv"][q][:], t["bn6"][q][:]), waits=[("a_vch", e1)],
                inc="a_vch")
            e3 = kb.op("scalar", lambda e, q=q: e.activation(
                t["sqv"][q][:], t["mv"][q][:, 1:2], AF.Sqrt,
            bias=c["epsv"][:, 0:1]),
                waits=[("a_vch", e2)], inc="a_sch")
            e4 = kb.op("vector", lambda e, q=q: e.reciprocal(
                t["rstd"][q][:], t["sqv"][q][:]), waits=[("a_sch", e3)],
                inc="a_vch")
            w = [("a_vch", e4)]
            if i >= 2:
                w.append(("a_tT", ev_tT[i - 2]))
            ev_ln = kb.op("vector", lambda e, q=q, i=i: e.tensor_scalar(
                t["aln"][q][:], t["xb"][:, i * 1024:(i + 1) * 1024],
                t["mv"][q][:, 0:1], t["rstd"][q][:], ALU.subtract, ALU.mult),
                waits=w, inc="a_ln")
            ev = None
            for ci in range(8):
                w = []
                if ci == 0:
                    w.append(("a_ln", ev_ln))
                    if i >= 2:
                        w.append(("a_lnT", ev_lnT[i - 2]))
                    if b == 1 and i == 0 and wv_user[0] is not None:
                        w.append(("a_vmm", max(x for x in wv_user.values()
                                               if x is not None)))
                ev = kb.op("tensor", lambda e, q=q, ci=ci: e.transpose(
                    t["psumTA"][:, ci * 128:(ci + 1) * 128],
                    t["aln"][q][:, ci * 128:(ci + 1) * 128],
                    c["identbf"][:]),
                    waits=w, inc="a_tT" if ci == 7 else None)
            ev_tT[i] = ev
            ev_lnT[i] = kb.op("scalar", lambda e, i=i: e.copy(
                lnT_v[:, :, i * 128:(i + 1) * 128], psumTA_v),
                waits=[("a_tT", ev_tT[i])], inc="a_lnT")

        mh = [None, None]
        for h in range(2):
            w = []
            if h == 0 and last_av is not None:
                w.append(("a_av", last_av))
            mh[h] = kb.op("vector", lambda e, h=h: e.memset(
                t["vaug"][h][:], 1.0), waits=w, inc="a_vch")

        for dt in range(8):
            qd = dt % 2
            for m, nmw, user in ((0, "wq", wq_user), (1, "wk", wk_user),
                                 (2, "wv", wv_user)):
                w = []
                if user[qd] is not None:
                    w.append((("a_qmm", "a_kmm", "a_vmm")[m], user[qd]))
                kb.op("gpsimd", lambda e, qd=qd, dt=dt, m=m, nmw=nmw:
                    e.dma_start(t[nmw][qd][:],
                                c["WQKV"][m * 8 + dt:m * 8 + dt + 1],
                                max_dma_last_dim=8192),
                    waits=w, inc=f"a_dw{m}", inc_n=16)
            ev_dw = [kb.get(f"a_dw{m}") for m in range(3)]

            ev = None
            for g in range(4):
                for ci in range(8):
                    w = []
                    if g == 0 and ci == 0:
                        w.append(("a_dw0", ev_dw[0]))
                        w.append(("a_lnT", ev_lnT[15]))
                        if last_nrm is not None:
                            w.append(("a_nrm", last_nrm))
                        if last_exp2[1] is not None:
                            w.append(("a_exp", last_exp2[1]))
                        if last_ave is not None:
                            w.append(("a_ave", last_ave))
                    ev = kb.op("tensor", lambda e, qd=qd, g=g, ci=ci: e.matmul(
                        t["psumA"][:, g * 512:(g + 1) * 512],
                        t["wq"][qd][:, ci * 128:(ci + 1) * 128],
                        t["lnT"][:, ci * 2048 + g * 512:
                                 ci * 2048 + (g + 1) * 512],
                        start=(ci == 0), stop=(ci == 7)),
                        waits=w, inc="a_qmm" if (g == 3 and ci == 7) else None)
            ev_qmm = ev
            wq_user[qd] = ev_qmm
            ev = None
            for g in range(4):
                ev = kb.op("vector", lambda e, g=g, b=b: e.tensor_scalar(
                    t["qpair"][:, g * 512:(g + 1) * 512],
                    t["psumA"][:, g * 512:(g + 1) * 512],
                    c["rope_sb"][:, 2 * b:2 * b + 1], None, ALU.mult),
                    waits=([("a_qmm", ev_qmm)] if g == 0 else ()),
                    inc="a_qe" if g == 3 else None)
            ev_qe = ev
            ev = None
            for g in range(4):
                for ci in range(8):
                    w = []
                    if g == 0 and ci == 0:
                        w.append(("a_dw1", ev_dw[1]))
                        w.append(("a_qe", ev_qe))
                    ev = kb.op("tensor", lambda e, qd=qd, g=g, ci=ci: e.matmul(
                        t["psumA"][:, g * 512:(g + 1) * 512],
                        t["wk"][qd][:, ci * 128:(ci + 1) * 128],
                        t["lnT"][:, ci * 2048 + g * 512:
                                 ci * 2048 + (g + 1) * 512],
                        start=(ci == 0), stop=(ci == 7)),
                        waits=w, inc="a_kmm" if (g == 3 and ci == 7) else None)
            ev_kmm = ev
            wk_user[qd] = ev_kmm
            ev = None
            for g in range(4):
                ev = kb.op("vector", lambda e, g=g, b=b: e.tensor_scalar(
                    t["kpair"][:, g * 512:(g + 1) * 512],
                    t["psumA"][:, g * 512:(g + 1) * 512],
                    c["rope_sb"][:, 2 * b + 1:2 * b + 2], None, ALU.mult),
                    waits=([("a_kmm", ev_kmm)] if g == 0 else ()),
                    inc="a_ke" if g == 3 else None)
            ev_ke = ev

            ev_vve = [None] * 16
            for i in range(16):
                pb = i % 2
                ev = None
                for ci in range(8):
                    w = []
                    if ci == 0:
                        w.append(("a_dw2", ev_dw[2]))
                        if i == 0:
                            w.append(("a_vch", mh[1]))
                        if i >= 2:
                            w.append(("a_vve", ev_vve[i - 2]))
                    ev = kb.op("tensor", lambda e, qd=qd, pb=pb, i=i, ci=ci:
                        e.matmul(
                            t["psumB"][:, pb * 512:pb * 512 + 128],
                            lnT_v[:, ci, i * 128:(i + 1) * 128],
                            t["wv"][qd][:, ci * 128:(ci + 1) * 128],
                            start=(ci == 0), stop=(ci == 7)),
                        waits=w, inc="a_vmm" if ci == 7 else None)
                ev_vmm_i = ev
                wv_user[qd] = ev_vmm_i
                for h in range(2):
                    ev_vve[i] = kb.op("vector", lambda e, pb=pb, i=i, h=h:
                        e.tensor_copy(
                            t["vaug"][h][:, i * 65:i * 65 + 64],
                            t["psumB"][:, pb * 512 + h * 64:
                                       pb * 512 + (h + 1) * 64]),
                        waits=([("a_vmm", ev_vmm_i)] if h == 0 else ()),
                        inc="a_vve")
            ev_ve = ev_vve[15]

            for h in range(2):
                hb = h * 64
                for g in range(4):
                    ev_exp_l = [None] * 16
                    for kc in range(16):
                        sb_ = kc % 2
                        w = []
                        if kc == 0 and g == 0 and h == 0:
                            w.append(("a_ke", ev_ke))
                        if kc >= 2:
                            w.append(("a_exp", ev_exp_l[kc - 2]))
                        elif last_exp2[sb_] is not None:
                            w.append(("a_exp", last_exp2[sb_]))
                        ev_sc = kb.op("tensor",
                            lambda e, hb=hb, g=g, kc=kc, sb_=sb_: e.matmul(
                                t["psumA"][:, sb_ * 512:(sb_ + 1) * 512],
                                t["kpair"][hb:hb + 64, kc * 128:(kc + 1) * 128],
                                t["qpair"][hb:hb + 64, g * 512:(g + 1) * 512],
                                start=True, stop=True),
                            waits=w, inc="a_sc")
                        w = [("a_sc", ev_sc)]
                        if kc == 0 and last_av is not None:
                            w.append(("a_av", last_av))
                        ev_exp_l[kc] = kb.op("scalar",
                            lambda e, kc=kc, sb_=sb_: e.activation(
                                t["expT"][:, kc * 512:(kc + 1) * 512],
                                t["psumA"][:, sb_ * 512:(sb_ + 1) * 512],
                                AF.Exp),
                            waits=w, inc="a_exp")
                    last_exp2 = [ev_exp_l[14], ev_exp_l[15]]
                    ev = None
                    for kc in range(16):
                        w = [("a_exp", ev_exp_l[kc])]
                        if kc == 0:
                            if last_ave is not None:
                                w.append(("a_ave", last_ave))
                            if g == 0 and h == 0:
                                w.append(("a_vve", ev_ve))
                        ev = kb.op("tensor", lambda e, h=h, kc=kc: e.matmul(
                            t["psumA"][0:65, 1024:1536],
                            t["vaug"][h][:, kc * 65:(kc + 1) * 65],
                            t["expT"][:, kc * 512:(kc + 1) * 512],
                            start=(kc == 0), stop=(kc == 15)),
                            waits=w, inc="a_av" if kc == 15 else None)
                    ev_av = ev
                    last_av = ev_av
                    qg = g % 2
                    w = [("a_av", ev_av)]
                    if last_xp_2[qg] is not None:
                        w.append(("a_xp", last_xp_2[qg]))
                    ev_ave = kb.op("scalar", lambda e, qg=qg: e.copy(
                        t["avst"][qg][:], t["psumA"][0:65, 1024:1536]),
                        waits=w, inc="a_ave")
                    last_ave = ev_ave
                    ev = None
                    for tt in range(4):
                        w = []
                        if tt == 0:
                            w.append(("a_ave", ev_ave))
                            if last_nrm is not None:
                                w.append(("a_nrm", last_nrm))
                        ev = kb.op("tensor", lambda e, qg=qg, tt=tt:
                            e.transpose(
                                t["psumA"][:, 1536 + tt * 65:
                                           1536 + (tt + 1) * 65],
                                t["avst"][qg][:, tt * 128:(tt + 1) * 128],
                                c["identf"][0:65, 0:65]),
                            waits=w, inc="a_xp" if tt == 3 else None)
                    ev_xp = ev
                    last_xp_2[qg] = ev_xp
                    for tt in range(4):
                        rq = tt % 2
                        e1 = kb.op("vector", lambda e, rq=rq, tt=tt:
                            e.reciprocal(
                                t["rec"][rq][:],
                                t["psumA"][:, 1536 + tt * 65 + 64:
                                           1536 + tt * 65 + 65]),
                            waits=([("a_xp", ev_xp)] if tt == 0 else ()),
                            inc="a_vch")
                        col = (g * 4 + tt) * 1024 + (dt * 2 + h) * 64
                        last_nrm = kb.op("vector",
                            lambda e, rq=rq, tt=tt, col=col:
                            e.scalar_tensor_tensor(
                                t["xb"][:, col:col + 64],
                                t["psumA"][:, 1536 + tt * 65:
                                           1536 + tt * 65 + 64],
                                t["rec"][rq][:],
                                t["xb"][:, col:col + 64],
                                ALU.mult, ALU.add),
                            waits=[("a_vch", e1)], inc="a_nrm")

        ev = None
        for i in range(16):
            w = [("a_nrm", last_nrm)] if i == 0 else []
            ev = kb.op("sync", lambda e, i=i, base_tok=base_tok: e.dma_start(
                xmir[base_tok + i * 128: base_tok + (i + 1) * 128, :],
                t["xb"][:, i * 1024:(i + 1) * 1024]),
                waits=w, inc="a_xbo", inc_n=16)
        last_xbo = ev
    return last_xbo


def emit_attn_call(kb, block, c, nc, xmir):
    with ExitStack() as sctx:
        t = alloc_attn_sbuf(sctx, nc)
        with ExitStack() as pctx:
            t["psumA"] = alloc(pctx, nc, "psum", "a_psumA", [128, 2048], F32)
            t["psumB"] = alloc(pctx, nc, "psum", "a_psumB", [128, 1024], F32)
            t["psumTA"] = alloc(pctx, nc, "psum", "a_psumTA", [128, 1024], BF16)
            ev = emit_attn(kb, c, t, xmir)
            kb.play(block)
    return ev


def build_nc(mode="full"):
    nc = bass.Bass("TRN2", target_bir_lowering=False, num_swdge_queues=4)
    kb = KB(nc)

    X = nc.dram_tensor("x", [4096, 1024], F32, kind="ExternalInput").ap()
    OUT = nc.dram_tensor("out", [4096, 1024], F32, kind="ExternalOutput").ap()
    xmir = nc.dram_tensor("xmir", [4096, 1024], F32).ap()
    XeD = nc.dram_tensor("XeD", [NSLOT, 1024], BF16).ap()
    YD = nc.dram_tensor("YD", [NSLOT, 1024], F32).ap()
    ccin = nc.dram_tensor("ccin", [1, 16], F32).ap().opt()
    ccout = nc.dram_tensor("ccout", [8, 16], F32).ap().opt()
    GWT = nc.dram_tensor("gwt", [128, 128], F32).ap()
    CVEC = nc.dram_tensor("cvec", [128, 16], F32, kind="ExternalInput").ap()
    IOTAP = nc.dram_tensor("iotap", [128, 1], F32, kind="ExternalInput").ap()
    CMASK = nc.dram_tensor("cmask", [8, 128], F32, kind="ExternalInput").ap()
    ROPE = nc.dram_tensor("rope", [128, 4], F32, kind="ExternalInput").ap()
    IDB = nc.dram_tensor("identbf", [128, 128], BF16, kind="ExternalInput").ap()
    IDF = nc.dram_tensor("identf", [128, 128], F32, kind="ExternalInput").ap()
    GWI = nc.dram_tensor("gw_in", [128, 128], F32, kind="ExternalInput").ap()
    WQKV = nc.dram_tensor("wqkv", [24, 128, 1024], F32, kind="ExternalInput").ap()
    W1R = nc.dram_tensor("w1r", [256, 128, 2048], F32, kind="ExternalInput").ap()
    W2R = nc.dram_tensor("w2r", [256, 128, 2048], F32, kind="ExternalInput").ap()

    with ExitStack() as cctx, nc.Block() as block:
        c = {}
        sb = lambda n, s, d: alloc(cctx, nc, "sbuf", "c_" + n, s, d)
        c["identbf"] = sb("identbf", [128, 128], BF16)
        c["identf"] = sb("identf", [128, 128], F32)
        c["rope_sb"] = sb("rope", [128, 4], F32)
        c["cvec384"] = sb("cvec", [128, 16], F32)
        c["iotap"] = sb("iotap", [128, 1], F32)
        c["coremask_bc"] = sb("cmask", [8, 128], F32)
        c["gw_sb"] = sb("gw", [128, 128], BF16)
        c["epsv"] = sb("epsv", [128, 1], F32)
        c["WQKV"] = WQKV
        kb.op("vector", lambda e: e.memset(c["epsv"][:], EPS), inc="f_vch")

        for dst, src in ((c["identbf"], IDB), (c["identf"], IDF),
                         (c["rope_sb"], ROPE), (c["cvec384"], CVEC),
                         (c["iotap"], IOTAP), (c["coremask_bc"], CMASK)):
            kb.op("sync", lambda e, dst=dst, src=src: e.dma_start(dst[:], src),
                  inc="c_dma", inc_n=16)
        kb.op("gpsimd", lambda e: setattr(kb, "rbc", e.to_reg(NSLOT - 1)))
        kb.op("gpsimd", lambda e: e.dma_start(c["gw_sb"][:], GWI),
              inc="c_dma", inc_n=16)
        kb.barrier("ph", extra_waits=[("c_dma", kb.get("c_dma"))])
        kb.play(block)

        if mode == "attn":
            kb.op("sync", lambda e: e.dma_start(xmir, X), inc="c_dma", inc_n=16)
            kb.barrier("ph", extra_waits=[("c_dma", kb.get("c_dma"))])
            ev = emit_attn_call(kb, block, c, nc, xmir)
            kb.barrier("ph", extra_waits=[("a_xbo", kb.get("a_xbo"))])
            kb.op("sync", lambda e: e.dma_start(OUT, xmir),
                  inc="f_out", inc_n=16)
            kb.op("sync", lambda e: e.nop(),
                  waits=[("f_out", kb.get("f_out"))])
            kb.play(block)
            return nc

        xo1 = OUT if mode == "ffn" else xmir
        emit_ffn_call(kb, block, None, c, nc, X, xo1, W1R, W2R, XeD, YD,
                      ccin, ccout, "_f1")
        if mode == "ffn":
            kb.op("sync", lambda e: e.nop(),
                  waits=[("f_out", kb.get("f_out"))])
            kb.play(block)
            return nc

        kb.barrier("ph", extra_waits=[("f_out", kb.get("f_out"))])
        kb.play(block)
        emit_attn_call(kb, block, c, nc, xmir)
        kb.barrier("ph", extra_waits=[("a_xbo", kb.get("a_xbo"))])
        kb.play(block)
        emit_ffn_call(kb, block, None, c, nc, xmir, OUT, W1R, W2R, XeD, YD,
                      ccin, ccout, "_f2")
        kb.op("sync", lambda e: e.nop(), waits=[("f_out", kb.get("f_out"))])
        kb.play(block)
    return nc



def host_consts(core):
    identbf = np.eye(128, dtype=ml_dtypes.bfloat16)
    identf = np.eye(128, dtype=np.float32)
    cvec = np.broadcast_to(
        (np.arange(16, dtype=np.float32) * SLOTS)[None, :], (128, 16)).copy()
    iotap = np.arange(128, dtype=np.float32).reshape(128, 1)
    cmask = np.zeros((8, 128), dtype=np.float32)
    cmask[:core, :] = 1.0
    rope = np.zeros((128, 4), dtype=np.float32)
    for bl in range(2):
        bg = 2 * core + bl
        theta = 10000.0 ** (-np.arange(0, 64, 2, dtype=np.float64) / 64.0)
        ang = bg * theta
        r = np.concatenate([np.cos(ang) + np.sin(ang),
                            np.cos(ang) - np.sin(ang)])
        rv = np.tile(r, 2)
        rope[:, 2 * bl] = (rv * 0.125).astype(np.float32)
        rope[:, 2 * bl + 1] = rv.astype(np.float32)
    return {"identbf": identbf, "identf": identf, "cvec": cvec,
            "iotap": iotap, "cmask": cmask, "rope": rope}


def host_weights(Wq, Wk, Wv, gw, w1, w2):
    def wt(W):
        WT = np.ascontiguousarray(W.T)
        return WT.reshape(8, 128, 8, 128).transpose(2, 1, 0, 3).reshape(
            8, 128, 1024)
    wqkv = np.ascontiguousarray(
        np.concatenate([wt(Wq), wt(Wk), wt(Wv)], axis=0))
    gwt = np.ascontiguousarray(
        gw.T.reshape(8, 128, 16).transpose(1, 0, 2).reshape(128, 128))
    w1r = np.ascontiguousarray(
        w1.reshape(16, 8, 128, 16, 256).transpose(0, 3, 2, 1, 4)
        .reshape(256, 128, 2048))
    w2r = np.ascontiguousarray(
        w2.reshape(16, 16, 2, 128, 1024).transpose(0, 1, 3, 2, 4)
        .reshape(256, 128, 2048))
    return {"wqkv": wqkv, "gw_in": gwt, "w1r": w1r, "w2r": w2r}


def make_in_maps(x_full, Wq, Wk, Wv, gw, w1, w2):
    wts = host_weights(Wq, Wk, Wv, gw, w1, w2)
    in_maps = []
    for core in range(8):
        m = dict(wts)
        m.update(host_consts(core))
        m["x"] = np.ascontiguousarray(
            x_full[2 * core:2 * core + 2].reshape(4096, 1024))
        in_maps.append(m)
    return in_maps



_CACHED_NC = None


def kernel(L0_attn_x, L0_attn_Wq, L0_attn_Wk, L0_attn_Wv, L0_ffn_gate_w,
           L0_ffn_w1, L0_ffn_w2):
    global _CACHED_NC
    x = np.asarray(L0_attn_x, dtype=np.float32)
    Wq = np.asarray(L0_attn_Wq, dtype=np.float32)
    Wk = np.asarray(L0_attn_Wk, dtype=np.float32)
    Wv = np.asarray(L0_attn_Wv, dtype=np.float32)
    gw = np.asarray(L0_ffn_gate_w, dtype=np.float32)
    w1 = np.asarray(L0_ffn_w1, dtype=np.float32)
    w2 = np.asarray(L0_ffn_w2, dtype=np.float32)

    if _CACHED_NC is None:
        _CACHED_NC = build_nc("full")
    in_maps = make_in_maps(x, Wq, Wk, Wv, gw, w1, w2)
    res = run_bass_kernel_spmd(_CACHED_NC, in_maps, core_ids=list(range(8)))
    out = np.concatenate(
        [np.asarray(res.results[i]["out"]).reshape(2, 2048, 1024)
         for i in range(8)], axis=0)
    return out


# version 03
# speedup vs baseline: 1.1175x; 1.1175x over previous
import numpy as np
import ml_dtypes
from contextlib import ExitStack

import concourse.bass as bass
import concourse.mybir as mybir
from concourse.bass import IndirectOffsetOnAxis
from concourse.bass_utils import run_bass_kernel_spmd

F32 = mybir.dt.float32
BF16 = mybir.dt.bfloat16
U32 = mybir.dt.uint32
AF = mybir.ActivationFunctionType
ALU = mybir.AluOpType
AX = mybir.AxisListType

NCHUNK = 32
NEXP = 16
SLOTS = 384
NSLOT = NEXP * SLOTS
CAP = 2560.0
EPS = 1e-5
OOBB = 8192.0
CH = 1024


class KB:
    def __init__(self, nc):
        self.nc = nc
        self.cnt = {}
        self.sems = {}
        self.ops = {k: [] for k in ("tensor", "vector", "scalar", "gpsimd", "sync")}
        self.sem_stack = ExitStack()

    def sem(self, name):
        if name not in self.sems:
            self.sems[name] = self.sem_stack.enter_context(self.nc.semaphore(name))
        return self.sems[name]

    def get(self, name):
        return self.cnt.get(name, 0)

    def op(self, eng, fn, waits=(), inc=None, inc_n=1):
        wait_h = [(self.sem(s), int(v)) for (s, v) in waits]
        ev = None
        sem_h = None
        if inc:
            sem_h = self.sem(inc)
            ev = self.cnt[inc] = self.cnt.get(inc, 0) + inc_n

        def play(e, fn=fn, wait_h=wait_h, sem_h=sem_h, inc_n=inc_n):
            for h, v in wait_h:
                e.wait_ge(h, v)
            r = fn(e)
            if sem_h is not None:
                r.then_inc(sem_h, inc_n)
        self.ops[eng].append(play)
        return ev

    def play(self, block):
        for eng in ("tensor", "vector", "scalar", "gpsimd", "sync"):
            ops = self.ops[eng]
            self.ops[eng] = []
            if not ops:
                continue
            sec = getattr(block, eng)

            def body(e, _ops=ops):
                for f in _ops:
                    f(e)
            sec(body)

    def barrier(self, name, extra_waits=()):
        target = self.get(name) + 5
        for eng in self.ops:
            w = list(extra_waits) if eng == "sync" else []
            self.op(eng, lambda e: e.nop(), waits=w, inc=name)
        for eng in self.ops:
            self.op(eng, lambda e: e.nop(), waits=[(name, target)])


def alloc(ctx, nc, kind, name, shape, dtype):
    if kind == "sbuf":
        return ctx.enter_context(nc.sbuf_tensor(name, shape, dtype))
    return ctx.enter_context(nc.psum_tensor(name, shape, dtype))



def alloc_ffn_sbuf(ctx, nc, tagc):
    t = {}
    sb = lambda n, s, d: alloc(ctx, nc, "sbuf", n + tagc, s, d)
    t["xstage"] = [sb(f"xstage{i}", [128, 1024], F32) for i in range(2)]
    t["bn6"] = [sb(f"bn6{i}", [128, 12], F32) for i in range(2)]
    t["mv"] = [sb(f"mv{i}", [128, 2], F32) for i in range(2)]
    t["sqv"] = [sb(f"sqv{i}", [128, 1], F32) for i in range(2)]
    t["rstd"] = [sb(f"rstd{i}", [128, 1], F32) for i in range(2)]
    t["lnbf"] = sb("lnbf", [128, NCHUNK * CH], BF16)
    t["lnt_sc"] = [sb(f"lntsc{i}", [128, 1024], BF16) for i in range(2)]
    t["lg_sc"] = [sb(f"lgsc{i}", [16, 128], F32) for i in range(2)]
    t["nm"] = [sb(f"nm{i}", [128, 1], F32) for i in range(2)]
    t["et"] = [sb(f"et{i}", [128, 16], F32) for i in range(2)]
    t["S"] = [sb(f"S{i}", [128, 1], F32) for i in range(2)]
    t["p_big"] = sb("p_big", [128, 32], F32)
    t["D0"] = [sb(f"D0{i}", [128, 16], F32) for i in range(2)]
    t["Csc"] = [sb(f"Csc{i}", [128, 16], F32) for i in range(2)]
    t["Dm"] = [sb(f"Dm{i}", [128, 16], F32) for i in range(2)]
    t["D_big"] = sb("D_big", [128, NCHUNK * 16], F32)
    t["route"] = sb("route", [16, 2 * 4096], F32)
    t["D_T"] = t["route"][:, 0:4096]
    t["incl"] = t["route"][:, 4096:8192]
    t["ranksT"] = t["route"][:, 0:4096]
    t["hist_all"] = sb("hist_all", [8, 16], F32)
    t["offs_bc"] = sb("offs_bc", [128, 16], F32)
    t["slot_big"] = sb("slot_big", [128, 32], F32)
    t["fits_big"] = sb("fits_big", [128, 32], F32)
    t["coef_big"] = sb("coef_big", [128, 32], F32)
    t["g_big"] = sb("g_big", [128, 32], U32)
    for nm_ in ("sc16a", "sc16b", "sc16c"):
        t[nm_] = [sb(f"{nm_}{i}", [128, 16], F32) for i in range(2)]
    for nm_ in ("sc1a", "sc1b", "sc1c", "sc1d", "sc1e", "sc1f", "sc1g", "sc1h",
                "oobv"):
        t[nm_] = [sb(f"{nm_}{i}", [128, 1], F32) for i in range(2)]
    t["xet"] = [sb(f"xet{i}", [128, 8 * 384], BF16) for i in range(2)]
    t["w1b"] = [sb(f"w1b{i}", [128, 2048], BF16) for i in range(4)]
    t["w2b"] = [sb(f"w2b{i}", [128, 2048], BF16) for i in range(4)]
    t["ht"] = [sb(f"ht{i}", [128, 384], BF16) for i in range(2)]
    t["yev"] = [sb("yev0", [128, 3072], F32)] * 2
    t["outg"] = [sb(f"outg{i}", [128, 1024], F32) for i in range(2)]
    t["xstage2"] = [sb(f"xstage2{i}", [128, 1024], F32) for i in range(2)]
    t["xnew"] = [sb(f"xnew{i}", [128, 1024], F32) for i in range(2)]
    return t


def emit_ffn_p1(kb, c, t, xin, W1R, W2R):
    pf_w1 = []
    pf_w2 = []
    for gs in range(4):
        pf_w1.append(kb.op("gpsimd", lambda e, gs=gs: e.dma_start(
            t["w1b"][gs][:], W1R[gs:gs + 1], max_dma_last_dim=8192),
            inc="f_w1", inc_n=16))
        pf_w2.append(kb.op("gpsimd", lambda e, gs=gs: e.dma_start(
            t["w2b"][gs][:], W2R[gs:gs + 1], max_dma_last_dim=8192),
            inc="f_w2", inc_n=16))
    ev_ln = [None] * NCHUNK
    ev_lnT = [None] * NCHUNK
    ev_gate = [None] * NCHUNK
    ev_lg = [None] * NCHUNK
    ev_D = [None] * NCHUNK
    ev_DT = [None] * NCHUNK

    pend_B = None

    def emit_B(args):
        j, q, e3 = args
        e4 = kb.op("vector", lambda e, q=q: e.reciprocal(
            t["rstd"][q][:], t["sqv"][q][:]), waits=[("f_sch", e3)],
            inc="f_vch")
        ev_ln[j] = kb.op("vector", lambda e, q=q, j=j: e.tensor_scalar(
            t["lnbf"][:, j * CH:(j + 1) * CH], t["xstage"][q][:],
            t["mv"][q][:, 0:1], t["rstd"][q][:], ALU.subtract, ALU.mult),
            waits=[("f_vch", e4)], inc="f_ln")

    for j in range(NCHUNK):
        q = j % 2
        w = [("f_ln", ev_ln[j - 2])] if j >= 2 else []
        ev_dx = kb.op("sync", lambda e, q=q, j=j: e.dma_start(
            t["xstage"][q][:], xin[j * 128:(j + 1) * 128, :]),
            waits=w, inc="f_dx", inc_n=16)
        kb.op("vector", lambda e, q=q: e.bn_stats(
            t["bn6"][q][:, 0:6], t["xstage"][q][:, 0:512]),
            waits=[("f_dx", ev_dx)])
        e1 = kb.op("vector", lambda e, q=q: e.bn_stats(
            t["bn6"][q][:, 6:12], t["xstage"][q][:, 512:1024]), inc="f_vch")
        e2 = kb.op("vector", lambda e, q=q: e.bn_aggr(
            t["mv"][q][:], t["bn6"][q][:]), waits=[("f_vch", e1)], inc="f_vch")
        e3 = kb.op("scalar", lambda e, q=q: e.activation(
            t["sqv"][q][:], t["mv"][q][:, 1:2], AF.Sqrt,
            bias=c["epsv"][:, 0:1]),
            waits=[("f_vch", e2)], inc="f_sch")
        if pend_B is not None:
            emit_B(pend_B)
        pend_B = (j, q, e3)
    emit_B(pend_B)

    for j in range(NCHUNK):
        q = j % 2
        ev = None
        for ci in range(8):
            w = []
            if ci == 0:
                w.append(("f_ln", ev_ln[j]))
                if j >= 2:
                    w.append(("f_lnT", ev_lnT[j - 2]))
            ev = kb.op("tensor", lambda e, q=q, j=j, ci=ci: e.transpose(
                t["psumT"][q][:, ci * 128:(ci + 1) * 128],
                t["lnbf"][:, j * CH + ci * 128: j * CH + (ci + 1) * 128],
                c["identbf"][:]),
                waits=w, inc="f_tT" if ci == 7 else None)
        w = [("f_tT", ev)]
        if j >= 2:
            w.append(("f_gate", ev_gate[j - 2]))
        ev_lnT[j] = kb.op("scalar", lambda e, q=q: e.copy(
            t["lnt_sc"][q][:], t["psumT"][q][:]), waits=w, inc="f_lnT")

        ev = None
        for ci in range(8):
            w = []
            if ci == 0:
                w.append(("f_lnT", ev_lnT[j]))
                if j >= 2:
                    w.append(("f_lg", ev_lg[j - 2]))
            ev = kb.op("tensor", lambda e, q=q, ci=ci: e.matmul(
                t["psumG"][q][:], c["gw_sb"][:, ci * 16:(ci + 1) * 16],
                t["lnt_sc"][q][:, ci * 128:(ci + 1) * 128],
                start=(ci == 0), stop=(ci == 7)),
                waits=w, inc="f_gate" if ci == 7 else None)
        ev_gate[j] = ev
        ev_lg[j] = kb.op("scalar", lambda e, q=q: e.copy(
            t["lg_sc"][q][:], t["psumG"][q][:]),
            waits=[("f_gate", ev_gate[j])], inc="f_lg")
        w = [("f_lg", ev_lg[j])]
        if j >= 2:
            w.append(("f_D", ev_D[j - 2]))
        ev_lT = kb.op("tensor", lambda e, q=q: e.transpose(
            t["psumL"][q][:], t["lg_sc"][q][:], c["identf"][0:16, 0:16]),
            waits=w, inc="f_lT")

        e5 = kb.op("vector", lambda e, q=q: e.reduce_max(
            t["nm"][q][:], t["psumL"][q][:], axis=AX.X, negate=True),
            waits=[("f_lT", ev_lT)], inc="f_vch")
        e6 = kb.op("scalar", lambda e, q=q: e.activation(
            t["et"][q][:], t["psumL"][q][:], AF.Exp, bias=t["nm"][q][:, 0:1],
            accum_out=t["S"][q][:]), waits=[("f_vch", e5)], inc="f_sch")
        kb.op("vector", lambda e, q=q, j=j: e.reciprocal(
            t["p_big"][:, j:j + 1], t["S"][q][:]), waits=[("f_sch", e6)])
        e7 = kb.op("vector", lambda e, q=q: e.tensor_scalar(
            t["D0"][q][:], t["psumL"][q][:], t["nm"][q][:, 0:1], 0.0,
            ALU.add, ALU.is_ge), inc="f_vch")
        e8 = kb.op("vector", lambda e, q=q: e.tensor_tensor_scan(
            t["Csc"][q][:], t["D0"][q][:], t["D0"][q][:], 0.0,
            ALU.add, ALU.bypass), waits=[("f_vch", e7)], inc="f_vch")
        e9 = kb.op("vector", lambda e, q=q: e.tensor_scalar(
            t["Dm"][q][:], t["Csc"][q][:], 1.0, None, ALU.is_le),
            waits=[("f_vch", e8)], inc="f_vch")
        ev_D[j] = kb.op("vector", lambda e, q=q, j=j: e.tensor_tensor(
            t["D_big"][:, j * 16:(j + 1) * 16], t["D0"][q][:], t["Dm"][q][:],
            ALU.mult), waits=[("f_vch", e9)], inc="f_D")

        w = [("f_D", ev_D[j])]
        if j >= 2:
            w.append(("f_DT", ev_DT[j - 2]))
        evt = kb.op("tensor", lambda e, q=q, j=j: e.transpose(
            t["psumD"][q][:], t["D_big"][:, j * 16:(j + 1) * 16],
            c["identf"][:]), waits=w, inc="f_tD")
        ev_DT[j] = kb.op("scalar", lambda e, q=q, j=j: e.copy(
            t["D_T"][:, j * 128:(j + 1) * 128], t["psumD"][q][:]),
            waits=[("f_tD", evt)], inc="f_DT")

    return {"ev_D": ev_D, "ev_DT": ev_DT, "pf_w1": pf_w1, "pf_w2": pf_w2}


def emit_ffn_p2(kb, c, t, ev1, XeD, ccin, ccout):
    ev_scan = kb.op("vector", lambda e: e.tensor_tensor_scan(
        t["incl"], t["D_T"], t["D_T"], 0.0, ALU.add, ALU.bypass),
        waits=[("f_DT", ev1["ev_DT"][NCHUNK - 1])], inc="f_vch")
    ev_rk = kb.op("vector", lambda e: e.tensor_tensor(
        t["ranksT"], t["incl"], t["D_T"], ALU.subtract),
        waits=[("f_vch", ev_scan)], inc="f_rk")
    ev_h = kb.op("sync", lambda e: e.dma_start(
        ccin, t["incl"][:, 4095:4096]), waits=[("f_rk", ev_rk)],
        inc="f_hist", inc_n=16)
    ev_cc = kb.op("gpsimd", lambda e: e.collective_compute(
        "AllGather", ALU.bypass, replica_groups=[list(range(8))],
        ins=[ccin], outs=[ccout]), waits=[("f_hist", ev_h)], inc="f_cc")
    ev_hb = kb.op("scalar", lambda e: e.dma_start(
        t["hist_all"][:], ccout), waits=[("f_cc", ev_cc)],
        inc="f_ccb", inc_n=16)

    ev_sg = [None] * NCHUNK
    ev_scat = [None] * NCHUNK
    ev_D = ev1["ev_D"]
    for j in range(NCHUNK):
        q = j % 2
        w = [("f_rk", ev_rk)]
        if j >= 2:
            w.append(("f_slot", ev_sg[j - 2]))
        ev_rtp = kb.op("tensor", lambda e, q=q, j=j: e.transpose(
            t["psumR"][q][:], t["ranksT"][:, j * 128:(j + 1) * 128],
            c["identf"][0:16, 0:16]), waits=w, inc="f_rtp")

        Dj = t["D_big"][:, j * 16:(j + 1) * 16]
        e1a = kb.op("vector", lambda e, q=q, Dj=Dj: e.tensor_tensor(
            t["sc16a"][q][:], t["psumR"][q][:], Dj, ALU.mult),
            waits=[("f_rtp", ev_rtp), ("f_D", ev_D[j])], inc="f_vch")
        e1 = kb.op("vector", lambda e, q=q, j=j: e.reduce_sum(
            t["slot_big"][:, j:j + 1], t["sc16a"][q][:], axis=AX.X),
            waits=[("f_vch", e1a)], inc="f_vch")
        e2 = kb.op("vector", lambda e, j=j: e.tensor_scalar(
            t["fits_big"][:, j:j + 1], t["slot_big"][:, j:j + 1],
            float(SLOTS), None, ALU.is_lt), waits=[("f_vch", e1)], inc="f_vch")
        e3a = kb.op("vector", lambda e, q=q, Dj=Dj: e.tensor_tensor(
            t["sc16b"][q][:], c["cvec384"][:], Dj, ALU.mult),
            waits=[("f_vch", e2)], inc="f_vch")
        e3 = kb.op("vector", lambda e, q=q: e.reduce_sum(
            t["sc1a"][q][:], t["sc16b"][q][:], axis=AX.X),
            waits=[("f_vch", e3a)], inc="f_vch")
        e4 = kb.op("vector", lambda e, q=q, j=j: e.tensor_tensor(
            t["sc1b"][q][:], t["sc1a"][q][:], t["slot_big"][:, j:j + 1],
            ALU.add), waits=[("f_vch", e3)], inc="f_vch")
        e5 = kb.op("vector", lambda e, q=q, j=j: e.tensor_scalar(
            t["oobv"][q][:], c["iotap"][:], OOBB + 128.0 * j, None, ALU.add),
            waits=[("f_vch", e4)], inc="f_vch")
        e6 = kb.op("vector", lambda e, q=q: e.tensor_tensor(
            t["sc1c"][q][:], t["sc1b"][q][:], t["oobv"][q][:], ALU.subtract),
            waits=[("f_vch", e5)], inc="f_vch")
        e7 = kb.op("vector", lambda e, q=q, j=j: e.scalar_tensor_tensor(
            t["sc1d"][q][:], t["sc1c"][q][:], t["fits_big"][:, j:j + 1],
            t["oobv"][q][:], ALU.mult, ALU.add), waits=[("f_vch", e6)],
            inc="f_vch")
        ev_sg[j] = kb.op("vector", lambda e, q=q, j=j: e.tensor_copy(
            t["g_big"][:, j:j + 1], t["sc1d"][q][:]),
            waits=[("f_vch", e7)], inc="f_slot")

        ev_scat[j] = kb.op("gpsimd", lambda e, j=j: e.indirect_dma_start(
            XeD, IndirectOffsetOnAxis(t["g_big"][:, j:j + 1], 0),
            t["lnbf"][:, j * CH:(j + 1) * CH], None,
            bounds_check=kb.rbc, oob_is_err=False),
            waits=[("f_slot", ev_sg[j])], inc="f_scat", inc_n=16)

    ev_om = kb.op("tensor", lambda e: e.matmul(
        t["psumO"][:], c["coremask_bc"][:], t["hist_all"][:],
        start=True, stop=True),
        waits=[("f_ccb", ev_hb), ("f_slot", ev_sg[NCHUNK - 1])], inc="f_om")
    ev_offs = kb.op("scalar", lambda e: e.copy(
        t["offs_bc"][:], t["psumO"][:]), waits=[("f_om", ev_om)],
        inc="f_offs")

    return {"ev_scat": ev_scat, "ev_offs": ev_offs, "ev_D": ev_D}


def emit_ffn_p34(kb, c, t, ev2, W1R, W2R, XeD, YD, xin, xout, ev1):
    ev_coef = [None] * NCHUNK
    for j in range(NCHUNK):
        q = j % 2
        Dj = t["D_big"][:, j * 16:(j + 1) * 16]
        w = [("f_offs", ev2["ev_offs"])] if j == 0 else []
        e1a = kb.op("vector", lambda e, q=q, Dj=Dj: e.tensor_tensor(
            t["sc16c"][q][:], t["offs_bc"][:], Dj, ALU.mult),
            waits=w, inc="f_vch")
        e1 = kb.op("vector", lambda e, q=q: e.reduce_sum(
            t["sc1e"][q][:], t["sc16c"][q][:], axis=AX.X),
            waits=[("f_vch", e1a)], inc="f_vch")
        e2 = kb.op("vector", lambda e, q=q, j=j: e.tensor_tensor(
            t["sc1f"][q][:], t["sc1e"][q][:], t["slot_big"][:, j:j + 1],
            ALU.add), waits=[("f_vch", e1)], inc="f_vch")
        e3 = kb.op("vector", lambda e, q=q: e.tensor_scalar(
            t["sc1g"][q][:], t["sc1f"][q][:], CAP, None, ALU.is_lt),
            waits=[("f_vch", e2)], inc="f_vch")
        e4 = kb.op("vector", lambda e, q=q, j=j: e.tensor_tensor(
            t["sc1h"][q][:], t["sc1g"][q][:], t["fits_big"][:, j:j + 1],
            ALU.mult), waits=[("f_vch", e3)], inc="f_vch")
        ev_coef[j] = kb.op("vector", lambda e, q=q, j=j: e.tensor_scalar(
            t["coef_big"][:, j:j + 1], t["sc1h"][q][:],
            t["p_big"][:, j:j + 1], 0.5, ALU.mult, ALU.mult),
            waits=[("f_vch", e4)], inc="f_coef")

    scat_join = ev2["ev_scat"][NCHUNK - 1]
    ev_xet = [None] * NEXP
    ev_w1 = [None] * (NEXP * 16)
    ev_w2 = [None] * (NEXP * 16)
    ev_w1[0:4] = ev1["pf_w1"]
    ev_w2[0:4] = ev1["pf_w2"]
    ev_yev = [None] * NEXP
    ev_ydma = [None] * NEXP
    b_hmm = kb.get("f_hmm")
    b_h = kb.get("f_h")
    b_htu = kb.get("f_htu")
    ev_hmm = [b_hmm + 1 + i for i in range(NEXP * 32)]
    ev_h = [b_h + 1 + i for i in range(NEXP * 32)]
    ev_hu = [b_htu + 1 + i for i in range(NEXP * 32)]

    for ei in range(NEXP):
        q = ei % 2
        ev = None
        for ci in range(8):
            w = []
            if ci == 0:
                w.append(("f_scat", scat_join))
                if ei >= 2:
                    w.append(("f_hmm", ev_hmm[(ei - 2) * 32 + 31]))
            ev = kb.op("sync", lambda e, q=q, ei=ei, ci=ci: e.dma_start(
                t["xet"][q][:, ci * 384:(ci + 1) * 384],
                XeD[ei * SLOTS:(ei + 1) * SLOTS, ci * 128:(ci + 1) * 128],
                transpose=True), waits=w, inc="f_xet", inc_n=16)
        ev_xet[ei] = ev

        for sg in range(16):
            gs = ei * 16 + sg
            if gs < 4:
                continue
            qq = gs % 4
            w = [("f_hmm", ev_hmm[(gs - 4) * 2 + 1])]
            ev_w1[gs] = kb.op("gpsimd", lambda e, qq=qq, gs=gs:
                e.dma_start(t["w1b"][qq][:], W1R[gs:gs + 1],
                            max_dma_last_dim=8192),
                waits=w, inc="f_w1", inc_n=16)
            w = [("f_htu", ev_hu[(gs - 4) * 2 + 1])]
            ev_w2[gs] = kb.op("gpsimd", lambda e, qq=qq, gs=gs:
                e.dma_start(t["w2b"][qq][:], W2R[gs:gs + 1],
                            max_dma_last_dim=8192),
                waits=w, inc="f_w2", inc_n=16)

        pend_w2 = None

        def emit_w2(args, ei=ei):
            gs_, gft_, hq_, qq_, fj_, ft_ = args
            ev = None
            for st in range(3):
                for dh in range(2):
                    w = []
                    if st == 0 and dh == 0:
                        w.append(("f_h", ev_h[gft_]))
                        w.append(("f_w2", ev_w2[gs_]))
                        if ft_ == 0 and ei >= 1:
                            w.append(("f_yev", ev_yev[ei - 1]))
                    ev = kb.op("tensor",
                        lambda e, hq=hq_, qq=qq_, st=st, dh=dh, fj=fj_,
                        ft=ft_: e.matmul(
                            t["psumY"][:, st * 1024 + dh * 512:
                                       st * 1024 + dh * 512 + 512],
                            t["ht"][hq][:, st * 128:(st + 1) * 128],
                            t["w2b"][qq][:, fj * 1024 + dh * 512:
                                          fj * 1024 + dh * 512 + 512],
                            start=(ft == 0), stop=(ft == 31)),
                        waits=w,
                        inc="f_htu" if (st == 2 and dh == 1) else None)
            assert ev == ev_hu[gft_], (ev, ev_hu[gft_])

        for sg in range(16):
            gs = ei * 16 + sg
            qq = gs % 4
            for fj in range(2):
                ft = sg * 2 + fj
                gft = ei * 32 + ft
                hq = ft % 2
                ev = None
                for ci in range(8):
                    w = []
                    if ci == 0:
                        w.append(("f_w1", ev_w1[gs]))
                        if ft == 0:
                            w.append(("f_xet", ev_xet[ei]))
                            if ei == 0:
                                w.append(("f_offs", ev2["ev_offs"]))
                        if gft >= 2:
                            w.append(("f_h", ev_h[gft - 2]))
                    ev = kb.op("tensor",
                        lambda e, q=q, qq=qq, hq=hq, ci=ci, fj=fj: e.matmul(
                            t["psumH"][hq][:],
                            t["w1b"][qq][:, ci * 256 + fj * 128:
                                          ci * 256 + fj * 128 + 128],
                            t["xet"][q][:, ci * 384:(ci + 1) * 384],
                            start=(ci == 0), stop=(ci == 7)),
                        waits=w, inc="f_hmm" if ci == 7 else None)
                assert ev == ev_hmm[gft], (ev, ev_hmm[gft])
                w = [("f_hmm", ev_hmm[gft])]
                if gft >= 2:
                    w.append(("f_htu", ev_hu[gft - 2]))
                evg = kb.op("scalar", lambda e, hq=hq: e.activation(
                    t["ht"][hq][:], t["psumH"][hq][:], AF.Gelu_apprx_tanh),
                    waits=w, inc="f_h")
                assert evg == ev_h[gft], (evg, ev_h[gft])
                if pend_w2 is not None:
                    emit_w2(pend_w2)
                pend_w2 = (gs, gft, hq, qq, fj, ft)
        emit_w2(pend_w2)

        w = [("f_htu", ev_hu[ei * 32 + 31])]
        if ei >= 1:
            w.append(("f_y", ev_ydma[ei - 1]))
        ev_yev[ei] = kb.op("vector", lambda e, q=q: e.tensor_copy(
            t["yev"][q][:], t["psumY"][:]), waits=w, inc="f_yev")
        ev_ydma[ei] = kb.op("sync", lambda e, q=q, ei=ei: e.dma_start(
            YD[ei * SLOTS:(ei + 1) * SLOTS, :].rearrange(
                "(st sp) d -> sp st d", sp=128),
            t["yev"][q][:].rearrange("p (st d) -> p st d", st=3)),
            waits=[("f_yev", ev_yev[ei])],
            inc="f_y", inc_n=16)

    ev_ms = None
    for q in range(2):
        ev_ms = kb.op("vector", lambda e, q=q: e.memset(
            t["outg"][q][:], 0.0), inc="f_vch")
    y_join = ev_ydma[NEXP - 1]
    ev_comb = [None] * NCHUNK
    ev_out = [None] * NCHUNK
    for j in range(NCHUNK):
        q = j % 2
        w = [("f_y", y_join)]
        if j >= 2:
            w.append(("f_comb", ev_comb[j - 2]))
        else:
            w.append(("f_vch", ev_ms))
        ev_g = kb.op("gpsimd", lambda e, q=q, j=j: e.indirect_dma_start(
            t["outg"][q][:], None,
            YD, IndirectOffsetOnAxis(t["g_big"][:, j:j + 1], 0),
            bounds_check=kb.rbc, oob_is_err=False),
            waits=w, inc="f_g", inc_n=16)
        w = [("f_comb", ev_comb[j - 2])] if j >= 2 else []
        ev_x2 = kb.op("sync", lambda e, q=q, j=j: e.dma_start(
            t["xstage2"][q][:], xin[j * 128:(j + 1) * 128, :]),
            waits=w, inc="f_x2", inc_n=16)
        w = [("f_g", ev_g), ("f_x2", ev_x2), ("f_coef", ev_coef[j])]
        if j >= 2:
            w.append(("f_out", ev_out[j - 2]))
        ev_comb[j] = kb.op("vector", lambda e, q=q, j=j:
            e.scalar_tensor_tensor(
                t["xnew"][q][:], t["outg"][q][:], t["coef_big"][:, j:j + 1],
                t["xstage2"][q][:], ALU.mult, ALU.add),
            waits=w, inc="f_comb")
        ev_out[j] = kb.op("sync", lambda e, q=q, j=j: e.dma_start(
            xout[j * 128:(j + 1) * 128, :], t["xnew"][q][:]),
            waits=[("f_comb", ev_comb[j])], inc="f_out", inc_n=16)
    return ev_out[NCHUNK - 1]


def emit_ffn_call(kb, block, ctx_consts, c, nc, xin, xout, W1R, W2R, XeD, YD,
                  ccin, ccout, tagc):
    with ExitStack() as sctx:
        t = alloc_ffn_sbuf(sctx, nc, tagc)
        with ExitStack() as pctx:
            ps = lambda n, s, d: alloc(pctx, nc, "psum", n + tagc, s, d)
            t["psumT"] = [ps(f"psumT{i}", [128, 1024], BF16) for i in range(2)]
            t["psumG"] = [ps(f"psumG{i}", [16, 128], F32) for i in range(2)]
            t["psumL"] = [ps(f"psumL{i}", [128, 16], F32) for i in range(2)]
            t["psumD"] = [ps(f"psumD{i}", [16, 128], F32) for i in range(2)]
            ev1 = emit_ffn_p1(kb, c, t, xin, W1R, W2R)
            kb.play(block)
        with ExitStack() as pctx:
            ps = lambda n, s, d: alloc(pctx, nc, "psum", n + tagc, s, d)
            t["psumR"] = [ps(f"psumR{i}", [128, 16], F32) for i in range(2)]
            t["psumO"] = ps("psumO", [128, 16], F32)
            ev2 = emit_ffn_p2(kb, c, t, ev1, XeD, ccin, ccout)
            kb.play(block)
        with ExitStack() as pctx:
            ps = lambda n, s, d: alloc(pctx, nc, "psum", n + tagc, s, d)
            t["psumH"] = [ps(f"psumH{i}", [128, 384], F32) for i in range(2)]
            t["psumY"] = ps("psumY", [128, 3072], F32)
            ev_last = emit_ffn_p34(kb, c, t, ev2, W1R, W2R, XeD, YD, xin,
                                    xout, ev1)
            kb.play(block)
    return ev_last




def alloc_attn_sbuf(ctx, nc):
    t = {}
    sb = lambda n, s, d: alloc(ctx, nc, "sbuf", "a_" + n, s, d)
    t["xb"] = sb("xb", [128, 16 * 1024], F32)
    t["bn6"] = [sb(f"bn6{i}", [128, 12], F32) for i in range(2)]
    t["mv"] = [sb(f"mv{i}", [128, 2], F32) for i in range(2)]
    t["sqv"] = [sb(f"sqv{i}", [128, 1], F32) for i in range(2)]
    t["rstd"] = [sb(f"rstd{i}", [128, 1], F32) for i in range(2)]
    t["aln"] = [sb(f"aln{i}", [128, 1024], BF16) for i in range(2)]
    t["lnT"] = sb("lnT", [128, 16384], BF16)
    t["wq"] = [sb(f"wq{i}", [128, 1024], BF16) for i in range(2)]
    t["wk"] = [sb(f"wk{i}", [128, 1024], BF16) for i in range(2)]
    t["wv"] = [sb(f"wv{i}", [128, 1024], BF16) for i in range(2)]
    t["qpair"] = sb("qpair", [128, 2048], BF16)
    t["kpair"] = sb("kpair", [128, 2048], BF16)
    t["vaug"] = [sb(f"vaug{i}", [128, 16 * 65], BF16) for i in range(2)]
    t["expT"] = sb("expT", [128, 16 * 512], BF16)
    t["avst"] = [sb(f"avst{i}", [65, 512], F32) for i in range(2)]
    t["rec"] = [sb(f"rec{i}", [128, 1], F32) for i in range(2)]
    return t


def emit_attn(kb, c, t, xmir):
    lnT_v = t["lnT"][:].rearrange("p (c s) -> p c s", c=8)
    psumTA_v = t["psumTA"][:].rearrange("p (c s) -> p c s", c=8)

    last_nrm = None
    last_exp2 = [None, None]
    last_ave = None
    last_av = None
    last_xbo = None
    last_xp_2 = {0: None, 1: None}
    wq_user = {0: None, 1: None}
    wk_user = {0: None, 1: None}
    wv_user = {0: None, 1: None}

    for b in range(2):
        base_tok = b * 2048
        ev_xb = [None] * 16
        ev_lnT = [None] * 16
        ev_tT = [None] * 16
        for i in range(16):
            w = []
            if b == 1 and i == 0:
                w.append(("a_xbo", last_xbo))
            ev_xb[i] = kb.op("sync", lambda e, i=i, base_tok=base_tok:
                e.dma_start(t["xb"][:, i * 1024:(i + 1) * 1024],
                            xmir[base_tok + i * 128: base_tok + (i + 1) * 128, :]),
                waits=w, inc="a_dxb", inc_n=16)
            q = i % 2
            kb.op("vector", lambda e, q=q, i=i: e.bn_stats(
                t["bn6"][q][:, 0:6], t["xb"][:, i * 1024:i * 1024 + 512]),
                waits=[("a_dxb", ev_xb[i])])
            e1 = kb.op("vector", lambda e, q=q, i=i: e.bn_stats(
                t["bn6"][q][:, 6:12], t["xb"][:, i * 1024 + 512:(i + 1) * 1024]),
                inc="a_vch")
            e2 = kb.op("vector", lambda e, q=q: e.bn_aggr(
                t["mv"][q][:], t["bn6"][q][:]), waits=[("a_vch", e1)],
                inc="a_vch")
            e3 = kb.op("scalar", lambda e, q=q: e.activation(
                t["sqv"][q][:], t["mv"][q][:, 1:2], AF.Sqrt,
            bias=c["epsv"][:, 0:1]),
                waits=[("a_vch", e2)], inc="a_sch")
            e4 = kb.op("vector", lambda e, q=q: e.reciprocal(
                t["rstd"][q][:], t["sqv"][q][:]), waits=[("a_sch", e3)],
                inc="a_vch")
            w = [("a_vch", e4)]
            if i >= 2:
                w.append(("a_tT", ev_tT[i - 2]))
            ev_ln = kb.op("vector", lambda e, q=q, i=i: e.tensor_scalar(
                t["aln"][q][:], t["xb"][:, i * 1024:(i + 1) * 1024],
                t["mv"][q][:, 0:1], t["rstd"][q][:], ALU.subtract, ALU.mult),
                waits=w, inc="a_ln")
            ev = None
            for ci in range(8):
                w = []
                if ci == 0:
                    w.append(("a_ln", ev_ln))
                    if i >= 2:
                        w.append(("a_lnT", ev_lnT[i - 2]))
                    if b == 1 and i == 0 and wv_user[0] is not None:
                        w.append(("a_vmm", max(x for x in wv_user.values()
                                               if x is not None)))
                ev = kb.op("tensor", lambda e, q=q, ci=ci: e.transpose(
                    t["psumTA"][:, ci * 128:(ci + 1) * 128],
                    t["aln"][q][:, ci * 128:(ci + 1) * 128],
                    c["identbf"][:]),
                    waits=w, inc="a_tT" if ci == 7 else None)
            ev_tT[i] = ev
            ev_lnT[i] = kb.op("scalar", lambda e, i=i: e.copy(
                lnT_v[:, :, i * 128:(i + 1) * 128], psumTA_v),
                waits=[("a_tT", ev_tT[i])], inc="a_lnT")

        mh = [None, None]
        for h in range(2):
            w = []
            if h == 0 and last_av is not None:
                w.append(("a_av", last_av))
            mh[h] = kb.op("vector", lambda e, h=h: e.memset(
                t["vaug"][h][:], 1.0), waits=w, inc="a_vch")

        for dt in range(8):
            qd = dt % 2
            for m, nmw, user in ((0, "wq", wq_user), (1, "wk", wk_user),
                                 (2, "wv", wv_user)):
                w = []
                if user[qd] is not None:
                    w.append((("a_qmm", "a_kmm", "a_vmm")[m], user[qd]))
                kb.op("gpsimd", lambda e, qd=qd, dt=dt, m=m, nmw=nmw:
                    e.dma_start(t[nmw][qd][:],
                                c["WQKV"][m * 8 + dt:m * 8 + dt + 1],
                                max_dma_last_dim=8192),
                    waits=w, inc=f"a_dw{m}", inc_n=16)
            ev_dw = [kb.get(f"a_dw{m}") for m in range(3)]

            ev = None
            for g in range(4):
                for ci in range(8):
                    w = []
                    if g == 0 and ci == 0:
                        w.append(("a_dw0", ev_dw[0]))
                        w.append(("a_lnT", ev_lnT[15]))
                        if last_nrm is not None:
                            w.append(("a_nrm", last_nrm))
                        if last_exp2[1] is not None:
                            w.append(("a_exp", last_exp2[1]))
                        if last_ave is not None:
                            w.append(("a_ave", last_ave))
                    ev = kb.op("tensor", lambda e, qd=qd, g=g, ci=ci: e.matmul(
                        t["psumA"][:, g * 512:(g + 1) * 512],
                        t["wq"][qd][:, ci * 128:(ci + 1) * 128],
                        t["lnT"][:, ci * 2048 + g * 512:
                                 ci * 2048 + (g + 1) * 512],
                        start=(ci == 0), stop=(ci == 7)),
                        waits=w, inc="a_qmm" if (g == 3 and ci == 7) else None)
            ev_qmm = ev
            wq_user[qd] = ev_qmm
            ev = None
            for g in range(4):
                ev = kb.op("vector", lambda e, g=g, b=b: e.tensor_scalar(
                    t["qpair"][:, g * 512:(g + 1) * 512],
                    t["psumA"][:, g * 512:(g + 1) * 512],
                    c["rope_sb"][:, 2 * b:2 * b + 1], None, ALU.mult),
                    waits=([("a_qmm", ev_qmm)] if g == 0 else ()),
                    inc="a_qe" if g == 3 else None)
            ev_qe = ev
            ev = None
            for g in range(4):
                for ci in range(8):
                    w = []
                    if g == 0 and ci == 0:
                        w.append(("a_dw1", ev_dw[1]))
                        w.append(("a_qe", ev_qe))
                    ev = kb.op("tensor", lambda e, qd=qd, g=g, ci=ci: e.matmul(
                        t["psumA"][:, g * 512:(g + 1) * 512],
                        t["wk"][qd][:, ci * 128:(ci + 1) * 128],
                        t["lnT"][:, ci * 2048 + g * 512:
                                 ci * 2048 + (g + 1) * 512],
                        start=(ci == 0), stop=(ci == 7)),
                        waits=w, inc="a_kmm" if (g == 3 and ci == 7) else None)
            ev_kmm = ev
            wk_user[qd] = ev_kmm
            ev = None
            for g in range(4):
                ev = kb.op("vector", lambda e, g=g, b=b: e.tensor_scalar(
                    t["kpair"][:, g * 512:(g + 1) * 512],
                    t["psumA"][:, g * 512:(g + 1) * 512],
                    c["rope_sb"][:, 2 * b + 1:2 * b + 2], None, ALU.mult),
                    waits=([("a_kmm", ev_kmm)] if g == 0 else ()),
                    inc="a_ke" if g == 3 else None)
            ev_ke = ev

            ev_vve = [None] * 16
            for i in range(16):
                pb = i % 2
                ev = None
                for ci in range(8):
                    w = []
                    if ci == 0:
                        w.append(("a_dw2", ev_dw[2]))
                        if i == 0:
                            w.append(("a_vch", mh[1]))
                        if i >= 2:
                            w.append(("a_vve", ev_vve[i - 2]))
                    ev = kb.op("tensor", lambda e, qd=qd, pb=pb, i=i, ci=ci:
                        e.matmul(
                            t["psumB"][:, pb * 512:pb * 512 + 128],
                            lnT_v[:, ci, i * 128:(i + 1) * 128],
                            t["wv"][qd][:, ci * 128:(ci + 1) * 128],
                            start=(ci == 0), stop=(ci == 7)),
                        waits=w, inc="a_vmm" if ci == 7 else None)
                ev_vmm_i = ev
                wv_user[qd] = ev_vmm_i
                for h in range(2):
                    ev_vve[i] = kb.op("vector", lambda e, pb=pb, i=i, h=h:
                        e.tensor_copy(
                            t["vaug"][h][:, i * 65:i * 65 + 64],
                            t["psumB"][:, pb * 512 + h * 64:
                                       pb * 512 + (h + 1) * 64]),
                        waits=([("a_vmm", ev_vmm_i)] if h == 0 else ()),
                        inc="a_vve")
            ev_ve = ev_vve[15]

            for h in range(2):
                hb = h * 64
                for g in range(4):
                    ev_exp_l = [None] * 16
                    for kc in range(16):
                        sb_ = kc % 2
                        w = []
                        if kc == 0 and g == 0 and h == 0:
                            w.append(("a_ke", ev_ke))
                        if kc >= 2:
                            w.append(("a_exp", ev_exp_l[kc - 2]))
                        elif last_exp2[sb_] is not None:
                            w.append(("a_exp", last_exp2[sb_]))
                        ev_sc = kb.op("tensor",
                            lambda e, hb=hb, g=g, kc=kc, sb_=sb_: e.matmul(
                                t["psumA"][:, sb_ * 512:(sb_ + 1) * 512],
                                t["kpair"][hb:hb + 64, kc * 128:(kc + 1) * 128],
                                t["qpair"][hb:hb + 64, g * 512:(g + 1) * 512],
                                start=True, stop=True),
                            waits=w, inc="a_sc")
                        w = [("a_sc", ev_sc)]
                        if kc == 0 and last_av is not None:
                            w.append(("a_av", last_av))
                        ev_exp_l[kc] = kb.op("scalar",
                            lambda e, kc=kc, sb_=sb_: e.activation(
                                t["expT"][:, kc * 512:(kc + 1) * 512],
                                t["psumA"][:, sb_ * 512:(sb_ + 1) * 512],
                                AF.Exp),
                            waits=w, inc="a_exp")
                    last_exp2 = [ev_exp_l[14], ev_exp_l[15]]
                    ev = None
                    for kc in range(16):
                        w = [("a_exp", ev_exp_l[kc])]
                        if kc == 0:
                            if last_ave is not None:
                                w.append(("a_ave", last_ave))
                            if g == 0 and h == 0:
                                w.append(("a_vve", ev_ve))
                        ev = kb.op("tensor", lambda e, h=h, kc=kc: e.matmul(
                            t["psumA"][0:65, 1024:1536],
                            t["vaug"][h][:, kc * 65:(kc + 1) * 65],
                            t["expT"][:, kc * 512:(kc + 1) * 512],
                            start=(kc == 0), stop=(kc == 15)),
                            waits=w, inc="a_av" if kc == 15 else None)
                    ev_av = ev
                    last_av = ev_av
                    qg = g % 2
                    w = [("a_av", ev_av)]
                    if last_xp_2[qg] is not None:
                        w.append(("a_xp", last_xp_2[qg]))
                    ev_ave = kb.op("scalar", lambda e, qg=qg: e.copy(
                        t["avst"][qg][:], t["psumA"][0:65, 1024:1536]),
                        waits=w, inc="a_ave")
                    last_ave = ev_ave
                    ev = None
                    for tt in range(4):
                        w = []
                        if tt == 0:
                            w.append(("a_ave", ev_ave))
                            if last_nrm is not None:
                                w.append(("a_nrm", last_nrm))
                        ev = kb.op("tensor", lambda e, qg=qg, tt=tt:
                            e.transpose(
                                t["psumA"][:, 1536 + tt * 65:
                                           1536 + (tt + 1) * 65],
                                t["avst"][qg][:, tt * 128:(tt + 1) * 128],
                                c["identf"][0:65, 0:65]),
                            waits=w, inc="a_xp" if tt == 3 else None)
                    ev_xp = ev
                    last_xp_2[qg] = ev_xp
                    for tt in range(4):
                        rq = tt % 2
                        e1 = kb.op("vector", lambda e, rq=rq, tt=tt:
                            e.reciprocal(
                                t["rec"][rq][:],
                                t["psumA"][:, 1536 + tt * 65 + 64:
                                           1536 + tt * 65 + 65]),
                            waits=([("a_xp", ev_xp)] if tt == 0 else ()),
                            inc="a_vch")
                        col = (g * 4 + tt) * 1024 + (dt * 2 + h) * 64
                        last_nrm = kb.op("vector",
                            lambda e, rq=rq, tt=tt, col=col:
                            e.scalar_tensor_tensor(
                                t["xb"][:, col:col + 64],
                                t["psumA"][:, 1536 + tt * 65:
                                           1536 + tt * 65 + 64],
                                t["rec"][rq][:],
                                t["xb"][:, col:col + 64],
                                ALU.mult, ALU.add),
                            waits=[("a_vch", e1)], inc="a_nrm")

        ev = None
        for i in range(16):
            w = [("a_nrm", last_nrm)] if i == 0 else []
            ev = kb.op("sync", lambda e, i=i, base_tok=base_tok: e.dma_start(
                xmir[base_tok + i * 128: base_tok + (i + 1) * 128, :],
                t["xb"][:, i * 1024:(i + 1) * 1024]),
                waits=w, inc="a_xbo", inc_n=16)
        last_xbo = ev
    return last_xbo


def emit_attn_call(kb, block, c, nc, xmir):
    with ExitStack() as sctx:
        t = alloc_attn_sbuf(sctx, nc)
        with ExitStack() as pctx:
            t["psumA"] = alloc(pctx, nc, "psum", "a_psumA", [128, 2048], F32)
            t["psumB"] = alloc(pctx, nc, "psum", "a_psumB", [128, 1024], F32)
            t["psumTA"] = alloc(pctx, nc, "psum", "a_psumTA", [128, 1024], BF16)
            ev = emit_attn(kb, c, t, xmir)
            kb.play(block)
    return ev


def build_nc(mode="full"):
    nc = bass.Bass("TRN2", target_bir_lowering=False, num_swdge_queues=4)
    kb = KB(nc)

    X = nc.dram_tensor("x", [4096, 1024], F32, kind="ExternalInput").ap()
    OUT = nc.dram_tensor("out", [4096, 1024], F32, kind="ExternalOutput").ap()
    xmir = nc.dram_tensor("xmir", [4096, 1024], F32).ap()
    XeD = nc.dram_tensor("XeD", [NSLOT, 1024], BF16).ap()
    YD = nc.dram_tensor("YD", [NSLOT, 1024], F32).ap()
    ccin = nc.dram_tensor("ccin", [1, 16], F32).ap().opt()
    ccout = nc.dram_tensor("ccout", [8, 16], F32).ap().opt()
    GWT = nc.dram_tensor("gwt", [128, 128], F32).ap()
    CVEC = nc.dram_tensor("cvec", [128, 16], F32, kind="ExternalInput").ap()
    IOTAP = nc.dram_tensor("iotap", [128, 1], F32, kind="ExternalInput").ap()
    CMASK = nc.dram_tensor("cmask", [8, 128], F32, kind="ExternalInput").ap()
    ROPE = nc.dram_tensor("rope", [128, 4], F32, kind="ExternalInput").ap()
    IDB = nc.dram_tensor("identbf", [128, 128], BF16, kind="ExternalInput").ap()
    IDF = nc.dram_tensor("identf", [128, 128], F32, kind="ExternalInput").ap()
    GWI = nc.dram_tensor("gw_in", [128, 128], F32, kind="ExternalInput").ap()
    WQKV = nc.dram_tensor("wqkv", [24, 128, 1024], F32, kind="ExternalInput").ap()
    W1R = nc.dram_tensor("w1r", [256, 128, 2048], F32, kind="ExternalInput").ap()
    W2R = nc.dram_tensor("w2r", [256, 128, 2048], F32, kind="ExternalInput").ap()

    with ExitStack() as cctx, nc.Block() as block:
        c = {}
        sb = lambda n, s, d: alloc(cctx, nc, "sbuf", "c_" + n, s, d)
        c["identbf"] = sb("identbf", [128, 128], BF16)
        c["identf"] = sb("identf", [128, 128], F32)
        c["rope_sb"] = sb("rope", [128, 4], F32)
        c["cvec384"] = sb("cvec", [128, 16], F32)
        c["iotap"] = sb("iotap", [128, 1], F32)
        c["coremask_bc"] = sb("cmask", [8, 128], F32)
        c["gw_sb"] = sb("gw", [128, 128], BF16)
        c["epsv"] = sb("epsv", [128, 1], F32)
        c["WQKV"] = WQKV
        kb.op("vector", lambda e: e.memset(c["epsv"][:], EPS), inc="f_vch")

        for dst, src in ((c["identbf"], IDB), (c["identf"], IDF),
                         (c["rope_sb"], ROPE), (c["cvec384"], CVEC),
                         (c["iotap"], IOTAP), (c["coremask_bc"], CMASK)):
            kb.op("sync", lambda e, dst=dst, src=src: e.dma_start(dst[:], src),
                  inc="c_dma", inc_n=16)
        kb.op("gpsimd", lambda e: setattr(kb, "rbc", e.to_reg(NSLOT - 1)))
        kb.op("gpsimd", lambda e: e.dma_start(c["gw_sb"][:], GWI),
              inc="c_dma", inc_n=16)
        kb.barrier("ph", extra_waits=[("c_dma", kb.get("c_dma"))])
        kb.play(block)

        if mode == "attn":
            kb.op("sync", lambda e: e.dma_start(xmir, X), inc="c_dma", inc_n=16)
            kb.barrier("ph", extra_waits=[("c_dma", kb.get("c_dma"))])
            ev = emit_attn_call(kb, block, c, nc, xmir)
            kb.barrier("ph", extra_waits=[("a_xbo", kb.get("a_xbo"))])
            kb.op("sync", lambda e: e.dma_start(OUT, xmir),
                  inc="f_out", inc_n=16)
            kb.op("sync", lambda e: e.nop(),
                  waits=[("f_out", kb.get("f_out"))])
            kb.play(block)
            return nc

        xo1 = OUT if mode == "ffn" else xmir
        emit_ffn_call(kb, block, None, c, nc, X, xo1, W1R, W2R, XeD, YD,
                      ccin, ccout, "_f1")
        if mode == "ffn":
            kb.op("sync", lambda e: e.nop(),
                  waits=[("f_out", kb.get("f_out"))])
            kb.play(block)
            return nc

        kb.barrier("ph", extra_waits=[("f_out", kb.get("f_out"))])
        kb.play(block)
        emit_attn_call(kb, block, c, nc, xmir)
        kb.barrier("ph", extra_waits=[("a_xbo", kb.get("a_xbo"))])
        kb.play(block)
        emit_ffn_call(kb, block, None, c, nc, xmir, OUT, W1R, W2R, XeD, YD,
                      ccin, ccout, "_f2")
        kb.op("sync", lambda e: e.nop(), waits=[("f_out", kb.get("f_out"))])
        kb.play(block)
    return nc



def host_consts(core):
    identbf = np.eye(128, dtype=ml_dtypes.bfloat16)
    identf = np.eye(128, dtype=np.float32)
    cvec = np.broadcast_to(
        (np.arange(16, dtype=np.float32) * SLOTS)[None, :], (128, 16)).copy()
    iotap = np.arange(128, dtype=np.float32).reshape(128, 1)
    cmask = np.zeros((8, 128), dtype=np.float32)
    cmask[:core, :] = 1.0
    rope = np.zeros((128, 4), dtype=np.float32)
    for bl in range(2):
        bg = 2 * core + bl
        theta = 10000.0 ** (-np.arange(0, 64, 2, dtype=np.float64) / 64.0)
        ang = bg * theta
        r = np.concatenate([np.cos(ang) + np.sin(ang),
                            np.cos(ang) - np.sin(ang)])
        rv = np.tile(r, 2)
        rope[:, 2 * bl] = (rv * 0.125).astype(np.float32)
        rope[:, 2 * bl + 1] = rv.astype(np.float32)
    return {"identbf": identbf, "identf": identf, "cvec": cvec,
            "iotap": iotap, "cmask": cmask, "rope": rope}


def host_weights(Wq, Wk, Wv, gw, w1, w2):
    def wt(W):
        WT = np.ascontiguousarray(W.T)
        return WT.reshape(8, 128, 8, 128).transpose(2, 1, 0, 3).reshape(
            8, 128, 1024)
    wqkv = np.ascontiguousarray(
        np.concatenate([wt(Wq), wt(Wk), wt(Wv)], axis=0))
    gwt = np.ascontiguousarray(
        gw.T.reshape(8, 128, 16).transpose(1, 0, 2).reshape(128, 128))
    w1r = np.ascontiguousarray(
        w1.reshape(16, 8, 128, 16, 256).transpose(0, 3, 2, 1, 4)
        .reshape(256, 128, 2048))
    w2r = np.ascontiguousarray(
        w2.reshape(16, 16, 2, 128, 1024).transpose(0, 1, 3, 2, 4)
        .reshape(256, 128, 2048))
    return {"wqkv": wqkv, "gw_in": gwt, "w1r": w1r, "w2r": w2r}


def make_in_maps(x_full, Wq, Wk, Wv, gw, w1, w2):
    wts = host_weights(Wq, Wk, Wv, gw, w1, w2)
    in_maps = []
    for core in range(8):
        m = dict(wts)
        m.update(host_consts(core))
        m["x"] = np.ascontiguousarray(
            x_full[2 * core:2 * core + 2].reshape(4096, 1024))
        in_maps.append(m)
    return in_maps



_CACHED_NC = None


def kernel(L0_attn_x, L0_attn_Wq, L0_attn_Wk, L0_attn_Wv, L0_ffn_gate_w,
           L0_ffn_w1, L0_ffn_w2):
    global _CACHED_NC
    x = np.asarray(L0_attn_x, dtype=np.float32)
    Wq = np.asarray(L0_attn_Wq, dtype=np.float32)
    Wk = np.asarray(L0_attn_Wk, dtype=np.float32)
    Wv = np.asarray(L0_attn_Wv, dtype=np.float32)
    gw = np.asarray(L0_ffn_gate_w, dtype=np.float32)
    w1 = np.asarray(L0_ffn_w1, dtype=np.float32)
    w2 = np.asarray(L0_ffn_w2, dtype=np.float32)

    if _CACHED_NC is None:
        _CACHED_NC = build_nc("full")
    in_maps = make_in_maps(x, Wq, Wk, Wv, gw, w1, w2)
    res = run_bass_kernel_spmd(_CACHED_NC, in_maps, core_ids=list(range(8)))
    out = np.concatenate(
        [np.asarray(res.results[i]["out"]).reshape(2, 2048, 1024)
         for i in range(8)], axis=0)
    return out


# version 07 (final)
# speedup vs baseline: 1.1523x; 1.0311x over previous
import numpy as np
import ml_dtypes
from contextlib import ExitStack

import concourse.bass as bass
import concourse.mybir as mybir
from concourse.bass import IndirectOffsetOnAxis
from concourse.bass_utils import run_bass_kernel_spmd

F32 = mybir.dt.float32
BF16 = mybir.dt.bfloat16
U32 = mybir.dt.uint32
AF = mybir.ActivationFunctionType
ALU = mybir.AluOpType
AX = mybir.AxisListType

NCHUNK = 32
NEXP = 16
SLOTS = 384
NSLOT = NEXP * SLOTS
CAP = 2560.0
EPS = 1e-5
OOBB = 8192.0
CH = 1024
PARANOID = 0


class KB:
    def __init__(self, nc):
        self.nc = nc
        self.cnt = {}
        self.sems = {}
        self.ops = {k: [] for k in ("tensor", "vector", "scalar", "gpsimd", "sync")}
        self.sem_stack = ExitStack()

    def sem(self, name):
        if name not in self.sems:
            self.sems[name] = self.sem_stack.enter_context(self.nc.semaphore(name))
        return self.sems[name]

    def get(self, name):
        return self.cnt.get(name, 0)

    def op(self, eng, fn, waits=(), inc=None, inc_n=1):
        wait_h = [(self.sem(s), int(v)) for (s, v) in waits]
        ev = None
        sem_h = None
        if inc:
            sem_h = self.sem(inc)
            ev = self.cnt[inc] = self.cnt.get(inc, 0) + inc_n

        def play(e, fn=fn, wait_h=wait_h, sem_h=sem_h, inc_n=inc_n):
            for h, v in wait_h:
                e.wait_ge(h, v)
            r = fn(e)
            if sem_h is not None:
                r.then_inc(sem_h, inc_n)
        self.ops[eng].append(play)
        return ev

    def play(self, block):
        for eng in ("tensor", "vector", "scalar", "gpsimd", "sync"):
            ops = self.ops[eng]
            self.ops[eng] = []
            if not ops:
                continue
            sec = getattr(block, eng)

            def body(e, _ops=ops):
                for f in _ops:
                    f(e)
            sec(body)

    def barrier(self, name, extra_waits=()):
        target = self.get(name) + 5
        for eng in self.ops:
            w = list(extra_waits) if eng == "sync" else []
            self.op(eng, lambda e: e.nop(), waits=w, inc=name)
        for eng in self.ops:
            self.op(eng, lambda e: e.nop(), waits=[(name, target)])


def alloc(ctx, nc, kind, name, shape, dtype):
    if kind == "sbuf":
        return ctx.enter_context(nc.sbuf_tensor(name, shape, dtype))
    return ctx.enter_context(nc.psum_tensor(name, shape, dtype))



def alloc_ffn_sbuf(ctx, nc, tagc):
    t = {}
    sb = lambda n, s, d: alloc(ctx, nc, "sbuf", n + tagc, s, d)
    t["xstage"] = [sb(f"xstage{i}", [128, 1024], F32) for i in range(2)]
    t["bn6"] = [sb(f"bn6{i}", [128, 12], F32) for i in range(2)]
    t["mv"] = [sb(f"mv{i}", [128, 2], F32) for i in range(2)]
    t["sqv"] = [sb(f"sqv{i}", [128, 1], F32) for i in range(2)]
    t["rstd"] = [sb(f"rstd{i}", [128, 1], F32) for i in range(2)]
    t["lnbf"] = sb("lnbf", [128, NCHUNK * CH], BF16)
    t["lnt_sc"] = [sb(f"lntsc{i}", [128, 1024], BF16) for i in range(2)]
    t["lg_sc"] = [sb(f"lgsc{i}", [16, 128], F32) for i in range(2)]
    t["nm"] = [sb(f"nm{i}", [128, 1], F32) for i in range(2)]
    t["et"] = [sb(f"et{i}", [128, 16], F32) for i in range(2)]
    t["S"] = [sb(f"S{i}", [128, 1], F32) for i in range(2)]
    t["p_big"] = sb("p_big", [128, 32], F32)
    t["D0"] = [sb(f"D0{i}", [128, 16], F32) for i in range(2)]
    t["Csc"] = [sb(f"Csc{i}", [128, 16], F32) for i in range(2)]
    t["Dm"] = [sb(f"Dm{i}", [128, 16], F32) for i in range(2)]
    t["D_big"] = sb("D_big", [128, NCHUNK * 16], F32)
    t["route"] = sb("route", [16, 2 * 4096], F32)
    t["D_T"] = t["route"][:, 0:4096]
    t["incl"] = t["route"][:, 4096:8192]
    t["ranksT"] = t["route"][:, 0:4096]
    t["hist_all"] = sb("hist_all", [8, 16], F32)
    t["offs_bc"] = sb("offs_bc", [128, 16], F32)
    t["slot_big"] = sb("slot_big", [128, 32], F32)
    t["fits_big"] = sb("fits_big", [128, 32], F32)
    t["coef_big"] = sb("coef_big", [128, 32], F32)
    t["g_big"] = sb("g_big", [128, 32], U32)
    for nm_ in ("sc16a", "sc16b", "sc16c"):
        t[nm_] = [sb(f"{nm_}{i}", [128, 16], F32) for i in range(2)]
    for nm_ in ("sc1a", "sc1b", "sc1c", "sc1d", "sc1e", "sc1f", "sc1g", "sc1h",
                "oobv"):
        t[nm_] = [sb(f"{nm_}{i}", [128, 1], F32) for i in range(2)]
    t["xet"] = [sb(f"xet{i}", [128, 8 * 384], BF16) for i in range(2)]
    t["w1b"] = [sb(f"w1b{i}", [128, 2048], BF16) for i in range(4)]
    t["w2b"] = [sb(f"w2b{i}", [128, 2048], BF16) for i in range(4)]
    t["ht"] = [sb(f"ht{i}", [128, 384], BF16) for i in range(2)]
    t["yev"] = [sb(f"yev{i}", [128, 3072], F32) for i in range(2)]
    t["outg"] = [sb(f"outg{i}", [128, 1024], F32) for i in range(2)]
    t["xstage2"] = t["xstage"]
    t["xnew"] = [sb(f"xnew{i}", [128, 1024], F32) for i in range(2)]
    return t


def emit_ffn_p1(kb, c, t, xin, W1R, W2R):
    pf_w1 = []
    pf_w2 = []
    for gs in range(4):
        pf_w1.append(kb.op("gpsimd", lambda e, gs=gs: e.dma_start(
            t["w1b"][gs][:], W1R[gs:gs + 1], max_dma_last_dim=8192),
            inc="f_w1", inc_n=16))
        pf_w2.append(kb.op("gpsimd", lambda e, gs=gs: e.dma_start(
            t["w2b"][gs][:], W2R[gs:gs + 1], max_dma_last_dim=8192),
            inc="f_w2", inc_n=16))
    ev_ln = [None] * NCHUNK
    ev_lnT = [None] * NCHUNK
    ev_gate = [None] * NCHUNK
    ev_lg = [None] * NCHUNK
    ev_D = [None] * NCHUNK
    ev_DT = [None] * NCHUNK

    pend_B = None

    def emit_B(args):
        j, q, e3 = args
        e4 = kb.op("scalar", lambda e, q=q: e.activation(
            t["rstd"][q][:], t["sqv"][q][:], AF.Exp, scale=-0.5),
            waits=[("f_sch", e3)], inc="f_sch")
        ev_ln[j] = kb.op("vector", lambda e, q=q, j=j: e.tensor_scalar(
            t["lnbf"][:, j * CH:(j + 1) * CH], t["xstage"][q][:],
            t["mv"][q][:, 0:1], t["rstd"][q][:], ALU.subtract, ALU.mult),
            waits=[("f_sch", e4)], inc="f_ln")

    for j in range(NCHUNK):
        q = j % 2
        w = [("f_ln", ev_ln[j - 2])] if j >= 2 else []
        ev_dx = kb.op("sync", lambda e, q=q, j=j: e.dma_start(
            t["xstage"][q][:], xin[j * 128:(j + 1) * 128, :]),
            waits=w, inc="f_dx", inc_n=16)
        kb.op("vector", lambda e, q=q: e.bn_stats(
            t["bn6"][q][:, 0:6], t["xstage"][q][:, 0:512]),
            waits=[("f_dx", ev_dx)])
        e1 = kb.op("vector", lambda e, q=q: e.bn_stats(
            t["bn6"][q][:, 6:12], t["xstage"][q][:, 512:1024]), inc="f_vch")
        e2 = kb.op("vector", lambda e, q=q: e.bn_aggr(
            t["mv"][q][:], t["bn6"][q][:]), waits=[("f_vch", e1)], inc="f_vch")
        e3 = kb.op("scalar", lambda e, q=q: e.activation(
            t["sqv"][q][:], t["mv"][q][:, 1:2], AF.Ln,
            bias=c["epsv"][:, 0:1]),
            waits=[("f_vch", e2)], inc="f_sch")
        if pend_B is not None:
            emit_B(pend_B)
        pend_B = (j, q, e3)
    emit_B(pend_B)

    for j in range(NCHUNK):
        q = j % 2
        ev = None
        for ci in range(8):
            w = []
            if ci == 0:
                w.append(("f_ln", ev_ln[j]))
                if j >= 2:
                    w.append(("f_lnT", ev_lnT[j - 2]))
            ev = kb.op("tensor", lambda e, q=q, j=j, ci=ci: e.transpose(
                t["psumT"][q][:, ci * 128:(ci + 1) * 128],
                t["lnbf"][:, j * CH + ci * 128: j * CH + (ci + 1) * 128],
                c["identbf"][:]),
                waits=w, inc="f_tT" if ci == 7 else None)
        w = [("f_tT", ev)]
        if j >= 2:
            w.append(("f_gate", ev_gate[j - 2]))
        ev_lnT[j] = kb.op("scalar", lambda e, q=q: e.copy(
            t["lnt_sc"][q][:], t["psumT"][q][:]), waits=w, inc="f_lnT")

        ev = None
        for ci in range(8):
            w = []
            if ci == 0:
                w.append(("f_lnT", ev_lnT[j]))
                if j >= 2:
                    w.append(("f_lg", ev_lg[j - 2]))
            ev = kb.op("tensor", lambda e, q=q, ci=ci: e.matmul(
                t["psumG"][q][:], c["gw_sb"][:, ci * 16:(ci + 1) * 16],
                t["lnt_sc"][q][:, ci * 128:(ci + 1) * 128],
                start=(ci == 0), stop=(ci == 7)),
                waits=w, inc="f_gate" if ci == 7 else None)
        ev_gate[j] = ev
        ev_lg[j] = kb.op("scalar", lambda e, q=q: e.copy(
            t["lg_sc"][q][:], t["psumG"][q][:]),
            waits=[("f_gate", ev_gate[j])], inc="f_lg")
        w = [("f_lg", ev_lg[j])]
        if j >= 2:
            w.append(("f_D", ev_D[j - 2]))
        ev_lT = kb.op("tensor", lambda e, q=q: e.transpose(
            t["psumL"][q][:], t["lg_sc"][q][:], c["identf"][0:16, 0:16]),
            waits=w, inc="f_lT")

        e5 = kb.op("vector", lambda e, q=q: e.reduce_max(
            t["nm"][q][:], t["psumL"][q][:], axis=AX.X, negate=True),
            waits=[("f_lT", ev_lT)], inc="f_vch")
        e6 = kb.op("scalar", lambda e, q=q: e.activation(
            t["et"][q][:], t["psumL"][q][:], AF.Exp, bias=t["nm"][q][:, 0:1],
            accum_out=t["S"][q][:]), waits=[("f_vch", e5)], inc="f_sch")
        kb.op("vector", lambda e, q=q, j=j: e.reciprocal(
            t["p_big"][:, j:j + 1], t["S"][q][:]), waits=[("f_sch", e6)])
        e7 = kb.op("vector", lambda e, q=q: e.tensor_scalar(
            t["D0"][q][:], t["psumL"][q][:], t["nm"][q][:, 0:1], 0.0,
            ALU.add, ALU.is_ge), inc="f_vch")
        e8 = kb.op("vector", lambda e, q=q: e.tensor_tensor_scan(
            t["Csc"][q][:], t["D0"][q][:], t["D0"][q][:], 0.0,
            ALU.add, ALU.bypass), waits=[("f_vch", e7)], inc="f_vch")
        e9 = kb.op("vector", lambda e, q=q: e.tensor_scalar(
            t["Dm"][q][:], t["Csc"][q][:], 1.0, None, ALU.is_le),
            waits=[("f_vch", e8)], inc="f_vch")
        ev_D[j] = kb.op("vector", lambda e, q=q, j=j: e.tensor_tensor(
            t["D_big"][:, j * 16:(j + 1) * 16], t["D0"][q][:], t["Dm"][q][:],
            ALU.mult), waits=[("f_vch", e9)], inc="f_D")

        w = [("f_D", ev_D[j])]
        if j >= 2:
            w.append(("f_DT", ev_DT[j - 2]))
        evt = kb.op("tensor", lambda e, q=q, j=j: e.transpose(
            t["psumD"][q][:], t["D_big"][:, j * 16:(j + 1) * 16],
            c["identf"][:]), waits=w, inc="f_tD")
        ev_DT[j] = kb.op("scalar", lambda e, q=q, j=j: e.copy(
            t["D_T"][:, j * 128:(j + 1) * 128], t["psumD"][q][:]),
            waits=[("f_tD", evt)], inc="f_DT")

    return {"ev_D": ev_D, "ev_DT": ev_DT, "pf_w1": pf_w1, "pf_w2": pf_w2}


def emit_ffn_p2(kb, c, t, ev1, XeD, ccin, ccout):
    ev_scan = kb.op("vector", lambda e: e.tensor_tensor_scan(
        t["incl"], t["D_T"], t["D_T"], 0.0, ALU.add, ALU.bypass),
        waits=[("f_DT", ev1["ev_DT"][NCHUNK - 1])], inc="f_vch")
    ev_rk = kb.op("vector", lambda e: e.tensor_tensor(
        t["ranksT"], t["incl"], t["D_T"], ALU.subtract),
        waits=[("f_vch", ev_scan)], inc="f_rk")
    ev_h = kb.op("sync", lambda e: e.dma_start(
        ccin, t["incl"][:, 4095:4096]), waits=[("f_rk", ev_rk)],
        inc="f_hist", inc_n=16)
    ev_cc = kb.op("gpsimd", lambda e: e.collective_compute(
        "AllGather", ALU.bypass, replica_groups=[list(range(8))],
        ins=[ccin], outs=[ccout]), waits=[("f_hist", ev_h)], inc="f_cc")
    ev_hb = kb.op("scalar", lambda e: e.dma_start(
        t["hist_all"][:], ccout), waits=[("f_cc", ev_cc)],
        inc="f_ccb", inc_n=16)

    ev_sg = [None] * NCHUNK
    ev_scat = [None] * NCHUNK
    ev_D = ev1["ev_D"]
    for j in range(NCHUNK):
        q = j % 2
        w = [("f_rk", ev_rk)]
        if j >= 2:
            w.append(("f_slot", ev_sg[j - 2]))
        ev_rtp = kb.op("tensor", lambda e, q=q, j=j: e.transpose(
            t["psumR"][q][:], t["ranksT"][:, j * 128:(j + 1) * 128],
            c["identf"][0:16, 0:16]), waits=w, inc="f_rtp")

        Dj = t["D_big"][:, j * 16:(j + 1) * 16]
        e1a = kb.op("vector", lambda e, q=q, Dj=Dj: e.tensor_tensor(
            t["sc16a"][q][:], t["psumR"][q][:], Dj, ALU.mult),
            waits=[("f_rtp", ev_rtp), ("f_D", ev_D[j])], inc="f_vch")
        e1 = kb.op("vector", lambda e, q=q, j=j: e.reduce_sum(
            t["slot_big"][:, j:j + 1], t["sc16a"][q][:], axis=AX.X),
            waits=[("f_vch", e1a)], inc="f_vch")
        e2 = kb.op("vector", lambda e, j=j: e.tensor_scalar(
            t["fits_big"][:, j:j + 1], t["slot_big"][:, j:j + 1],
            float(SLOTS), None, ALU.is_lt), waits=[("f_vch", e1)], inc="f_vch")
        e3a = kb.op("vector", lambda e, q=q, Dj=Dj: e.tensor_tensor(
            t["sc16b"][q][:], c["cvec384"][:], Dj, ALU.mult),
            waits=[("f_vch", e2)], inc="f_vch")
        e3 = kb.op("vector", lambda e, q=q: e.reduce_sum(
            t["sc1a"][q][:], t["sc16b"][q][:], axis=AX.X),
            waits=[("f_vch", e3a)], inc="f_vch")
        e4 = kb.op("vector", lambda e, q=q, j=j: e.tensor_tensor(
            t["sc1b"][q][:], t["sc1a"][q][:], t["slot_big"][:, j:j + 1],
            ALU.add), waits=[("f_vch", e3)], inc="f_vch")
        e5 = kb.op("vector", lambda e, q=q, j=j: e.tensor_scalar(
            t["oobv"][q][:], c["iotap"][:], OOBB + 128.0 * j, None, ALU.add),
            waits=[("f_vch", e4)], inc="f_vch")
        e6 = kb.op("vector", lambda e, q=q: e.tensor_tensor(
            t["sc1c"][q][:], t["sc1b"][q][:], t["oobv"][q][:], ALU.subtract),
            waits=[("f_vch", e5)], inc="f_vch")
        e7 = kb.op("vector", lambda e, q=q, j=j: e.scalar_tensor_tensor(
            t["sc1d"][q][:], t["sc1c"][q][:], t["fits_big"][:, j:j + 1],
            t["oobv"][q][:], ALU.mult, ALU.add), waits=[("f_vch", e6)],
            inc="f_vch")
        ev_sg[j] = kb.op("vector", lambda e, q=q, j=j: e.tensor_copy(
            t["g_big"][:, j:j + 1], t["sc1d"][q][:]),
            waits=[("f_vch", e7)], inc="f_slot")

        ev_scat[j] = kb.op("gpsimd", lambda e, j=j: e.indirect_dma_start(
            XeD, IndirectOffsetOnAxis(t["g_big"][:, j:j + 1], 0),
            t["lnbf"][:, j * CH:(j + 1) * CH], None,
            bounds_check=kb.rbc, oob_is_err=False),
            waits=[("f_slot", ev_sg[j])], inc="f_scat", inc_n=16)

    ev_om = kb.op("tensor", lambda e: e.matmul(
        t["psumO"][:], c["coremask_bc"][:], t["hist_all"][:],
        start=True, stop=True),
        waits=[("f_ccb", ev_hb), ("f_slot", ev_sg[NCHUNK - 1])], inc="f_om")
    ev_offs = kb.op("scalar", lambda e: e.copy(
        t["offs_bc"][:], t["psumO"][:]), waits=[("f_om", ev_om)],
        inc="f_offs")

    return {"ev_scat": ev_scat, "ev_offs": ev_offs, "ev_D": ev_D}


def emit_ffn_p34(kb, c, t, ev2, W1R, W2R, XeD, YD, xin, xout, ev1):
    ev_coef = [None] * NCHUNK
    for j in range(NCHUNK):
        q = j % 2
        Dj = t["D_big"][:, j * 16:(j + 1) * 16]
        w = [("f_offs", ev2["ev_offs"])] if j == 0 else []
        e1a = kb.op("vector", lambda e, q=q, Dj=Dj: e.tensor_tensor(
            t["sc16c"][q][:], t["offs_bc"][:], Dj, ALU.mult),
            waits=w, inc="f_vch")
        e1 = kb.op("vector", lambda e, q=q: e.reduce_sum(
            t["sc1e"][q][:], t["sc16c"][q][:], axis=AX.X),
            waits=[("f_vch", e1a)], inc="f_vch")
        e2 = kb.op("vector", lambda e, q=q, j=j: e.tensor_tensor(
            t["sc1f"][q][:], t["sc1e"][q][:], t["slot_big"][:, j:j + 1],
            ALU.add), waits=[("f_vch", e1)], inc="f_vch")
        e3 = kb.op("vector", lambda e, q=q: e.tensor_scalar(
            t["sc1g"][q][:], t["sc1f"][q][:], CAP, None, ALU.is_lt),
            waits=[("f_vch", e2)], inc="f_vch")
        e4 = kb.op("vector", lambda e, q=q, j=j: e.tensor_tensor(
            t["sc1h"][q][:], t["sc1g"][q][:], t["fits_big"][:, j:j + 1],
            ALU.mult), waits=[("f_vch", e3)], inc="f_vch")
        ev_coef[j] = kb.op("vector", lambda e, q=q, j=j: e.tensor_scalar(
            t["coef_big"][:, j:j + 1], t["sc1h"][q][:],
            t["p_big"][:, j:j + 1], 0.5, ALU.mult, ALU.mult),
            waits=[("f_vch", e4)], inc="f_coef")

    scat_join = ev2["ev_scat"][NCHUNK - 1]
    if PARANOID & 1:
        kb.barrier("pb1", extra_waits=[("f_scat", kb.get("f_scat"))])
        scat_join = 0
    ev_xet = [None] * NEXP
    ev_w1 = [None] * (NEXP * 16)
    ev_w2 = [None] * (NEXP * 16)
    ev_w1[0:4] = ev1["pf_w1"]
    ev_w2[0:4] = ev1["pf_w2"]
    ev_yev = [None] * NEXP
    ev_ydma = [None] * NEXP
    b_hmm = kb.get("f_hmm")
    b_h = kb.get("f_h")
    b_htu = kb.get("f_htu")
    ev_hmm = [b_hmm + 1 + i for i in range(NEXP * 32)]
    ev_h = [b_h + 1 + i for i in range(NEXP * 32)]
    ev_hu = [b_htu + 1 + i for i in range(NEXP * 32)]

    for ei in range(NEXP):
        q = ei % 2
        ev = None
        for ci in range(8):
            w = []
            if ci == 0:
                w.append(("f_scat", scat_join))
                if ei >= 2:
                    w.append(("f_hmm", ev_hmm[(ei - 2) * 32 + 31]))
            ev = kb.op("sync", lambda e, q=q, ei=ei, ci=ci: e.dma_start(
                t["xet"][q][:, ci * 384:(ci + 1) * 384],
                XeD[ei * SLOTS:(ei + 1) * SLOTS, ci * 128:(ci + 1) * 128],
                transpose=True), waits=w, inc="f_xet", inc_n=16)
        ev_xet[ei] = ev

        for sg in range(16):
            gs = ei * 16 + sg
            if gs < 4:
                continue
            qq = gs % 4
            w = [("f_hmm", ev_hmm[(gs - 4) * 2 + 1])]
            ev_w1[gs] = kb.op("gpsimd", lambda e, qq=qq, gs=gs:
                e.dma_start(t["w1b"][qq][:], W1R[gs:gs + 1],
                            max_dma_last_dim=8192),
                waits=w, inc="f_w1", inc_n=16)
            w = [("f_htu", ev_hu[(gs - 4) * 2 + 1])]
            ev_w2[gs] = kb.op("gpsimd", lambda e, qq=qq, gs=gs:
                e.dma_start(t["w2b"][qq][:], W2R[gs:gs + 1],
                            max_dma_last_dim=8192),
                waits=w, inc="f_w2", inc_n=16)

        pend_w2 = None

        def emit_w2(args, ei=ei):
            gs_, gft_, hq_, qq_, fj_, ft_ = args
            ev = None
            for st in range(3):
                for dh in range(2):
                    w = []
                    if st == 0 and dh == 0:
                        w.append(("f_h", ev_h[gft_]))
                        w.append(("f_w2", ev_w2[gs_]))
                        if ft_ == 0 and ei >= 1:
                            w.append(("f_yev", ev_yev[ei - 1]))
                    ev = kb.op("tensor",
                        lambda e, hq=hq_, qq=qq_, st=st, dh=dh, fj=fj_,
                        ft=ft_: e.matmul(
                            t["psumY"][:, st * 1024 + dh * 512:
                                       st * 1024 + dh * 512 + 512],
                            t["ht"][hq][:, st * 128:(st + 1) * 128],
                            t["w2b"][qq][:, fj * 1024 + dh * 512:
                                          fj * 1024 + dh * 512 + 512],
                            start=(ft == 0), stop=(ft == 31)),
                        waits=w,
                        inc="f_htu" if (st == 2 and dh == 1) else None)
            assert ev == ev_hu[gft_], (ev, ev_hu[gft_])

        for sg in range(16):
            gs = ei * 16 + sg
            qq = gs % 4
            for fj in range(2):
                ft = sg * 2 + fj
                gft = ei * 32 + ft
                hq = ft % 2
                ev = None
                for ci in range(8):
                    w = []
                    if ci == 0:
                        w.append(("f_w1", ev_w1[gs]))
                        if ft == 0:
                            w.append(("f_xet", ev_xet[ei]))
                            if ei == 0:
                                w.append(("f_offs", ev2["ev_offs"]))
                        if gft >= 2:
                            w.append(("f_h", ev_h[gft - 2]))
                    ev = kb.op("tensor",
                        lambda e, q=q, qq=qq, hq=hq, ci=ci, fj=fj: e.matmul(
                            t["psumH"][hq][:],
                            t["w1b"][qq][:, ci * 256 + fj * 128:
                                          ci * 256 + fj * 128 + 128],
                            t["xet"][q][:, ci * 384:(ci + 1) * 384],
                            start=(ci == 0), stop=(ci == 7)),
                        waits=w, inc="f_hmm" if ci == 7 else None)
                assert ev == ev_hmm[gft], (ev, ev_hmm[gft])
                w = [("f_hmm", ev_hmm[gft])]
                if gft >= 2:
                    w.append(("f_htu", ev_hu[gft - 2]))
                evg = kb.op("scalar", lambda e, hq=hq: e.activation(
                    t["ht"][hq][:], t["psumH"][hq][:], AF.Gelu_apprx_tanh),
                    waits=w, inc="f_h")
                assert evg == ev_h[gft], (evg, ev_h[gft])
                if pend_w2 is not None:
                    emit_w2(pend_w2)
                pend_w2 = (gs, gft, hq, qq, fj, ft)
        emit_w2(pend_w2)

        w = [("f_htu", ev_hu[ei * 32 + 31])]
        if ei >= 1:
            w.append(("f_y", ev_ydma[ei - 1]))
        ev_yev[ei] = kb.op("vector", lambda e, q=q: e.tensor_copy(
            t["yev"][q][:], t["psumY"][:]), waits=w, inc="f_yev")
        ev_ydma[ei] = kb.op("gpsimd", lambda e, q=q, ei=ei: e.dma_start(
            YD[ei * SLOTS:(ei + 1) * SLOTS, :].rearrange(
                "(st sp) d -> sp st d", sp=128),
            t["yev"][q][:].rearrange("p (st d) -> p st d", st=3)),
            waits=[("f_yev", ev_yev[ei])],
            inc="f_y", inc_n=16)

    ev_ms = None
    for q in range(2):
        ev_ms = kb.op("vector", lambda e, q=q: e.memset(
            t["outg"][q][:], 0.0), inc="f_vch")
    y_join = ev_ydma[NEXP - 1]
    if PARANOID & 2:
        kb.barrier("pb2", extra_waits=[("f_y", y_join)])
        y_join = 0
    ev_comb = [None] * NCHUNK
    ev_out = [None] * NCHUNK
    for j in range(NCHUNK):
        q = j % 2
        w = [("f_y", y_join)]
        if j >= 2:
            w.append(("f_comb", ev_comb[j - 2]))
        else:
            w.append(("f_vch", ev_ms))
        ev_g = kb.op("gpsimd", lambda e, q=q, j=j: e.indirect_dma_start(
            t["outg"][q][:], None,
            YD, IndirectOffsetOnAxis(t["g_big"][:, j:j + 1], 0),
            bounds_check=kb.rbc, oob_is_err=False),
            waits=w, inc="f_g", inc_n=16)
        w = [("f_comb", ev_comb[j - 2])] if j >= 2 else []
        ev_x2 = kb.op("sync", lambda e, q=q, j=j: e.dma_start(
            t["xstage2"][q][:], xin[j * 128:(j + 1) * 128, :]),
            waits=w, inc="f_x2", inc_n=16)
        w = [("f_g", ev_g), ("f_x2", ev_x2), ("f_coef", ev_coef[j])]
        if j >= 2:
            w.append(("f_out", ev_out[j - 2]))
        ev_comb[j] = kb.op("vector", lambda e, q=q, j=j:
            e.scalar_tensor_tensor(
                t["xnew"][q][:], t["outg"][q][:], t["coef_big"][:, j:j + 1],
                t["xstage2"][q][:], ALU.mult, ALU.add),
            waits=w, inc="f_comb")
        ev_out[j] = kb.op("sync", lambda e, q=q, j=j: e.dma_start(
            xout[j * 128:(j + 1) * 128, :], t["xnew"][q][:]),
            waits=[("f_comb", ev_comb[j])], inc="f_out", inc_n=16)
    return ev_out[NCHUNK - 1]


def emit_ffn_call(kb, block, ctx_consts, c, nc, xin, xout, W1R, W2R, XeD, YD,
                  ccin, ccout, tagc):
    with ExitStack() as sctx:
        t = alloc_ffn_sbuf(sctx, nc, tagc)
        with ExitStack() as pctx:
            ps = lambda n, s, d: alloc(pctx, nc, "psum", n + tagc, s, d)
            t["psumT"] = [ps(f"psumT{i}", [128, 1024], BF16) for i in range(2)]
            t["psumG"] = [ps(f"psumG{i}", [16, 128], F32) for i in range(2)]
            t["psumL"] = [ps(f"psumL{i}", [128, 16], F32) for i in range(2)]
            t["psumD"] = [ps(f"psumD{i}", [16, 128], F32) for i in range(2)]
            ev1 = emit_ffn_p1(kb, c, t, xin, W1R, W2R)
            kb.play(block)
        with ExitStack() as pctx:
            ps = lambda n, s, d: alloc(pctx, nc, "psum", n + tagc, s, d)
            t["psumR"] = [ps(f"psumR{i}", [128, 16], F32) for i in range(2)]
            t["psumO"] = ps("psumO", [128, 16], F32)
            ev2 = emit_ffn_p2(kb, c, t, ev1, XeD, ccin, ccout)
            kb.play(block)
        with ExitStack() as pctx:
            ps = lambda n, s, d: alloc(pctx, nc, "psum", n + tagc, s, d)
            t["psumH"] = [ps(f"psumH{i}", [128, 384], F32) for i in range(2)]
            t["psumY"] = ps("psumY", [128, 3072], F32)
            ev_last = emit_ffn_p34(kb, c, t, ev2, W1R, W2R, XeD, YD, xin,
                                    xout, ev1)
            kb.play(block)
    return ev_last




def alloc_attn_sbuf(ctx, nc):
    t = {}
    sb = lambda n, s, d: alloc(ctx, nc, "sbuf", "a_" + n, s, d)
    t["xb"] = sb("xb", [128, 16 * 1024], F32)
    t["bn6"] = [sb(f"bn6{i}", [128, 12], F32) for i in range(2)]
    t["mv"] = [sb(f"mv{i}", [128, 2], F32) for i in range(2)]
    t["sqv"] = [sb(f"sqv{i}", [128, 1], F32) for i in range(2)]
    t["rstd"] = [sb(f"rstd{i}", [128, 1], F32) for i in range(2)]
    t["aln"] = [sb(f"aln{i}", [128, 1024], BF16) for i in range(2)]
    t["lnT"] = sb("lnT", [128, 16384], BF16)
    t["wq"] = [sb(f"wq{i}", [128, 1024], BF16) for i in range(2)]
    t["wk"] = [sb(f"wk{i}", [128, 1024], BF16) for i in range(2)]
    t["wv"] = [sb(f"wv{i}", [128, 1024], BF16) for i in range(2)]
    t["qpair"] = sb("qpair", [128, 2048], BF16)
    t["kpair"] = sb("kpair", [128, 2048], BF16)
    t["vaug"] = [sb(f"vaug{i}", [128, 16 * 65], BF16) for i in range(2)]
    t["expT"] = sb("expT", [128, 16 * 512], BF16)
    t["avst"] = [sb(f"avst{i}", [65, 512], F32) for i in range(2)]
    t["rec"] = [sb(f"rec{i}", [128, 1], F32) for i in range(2)]
    return t


def emit_attn(kb, c, t, xmir):
    lnT_v = t["lnT"][:].rearrange("p (c s) -> p c s", c=8)
    psumTA_v = t["psumTA"][:].rearrange("p (c s) -> p c s", c=8)

    last_nrm = None
    prev_ep_last = [None]
    prev_av = [None]
    last_ave = None
    last_av = None
    last_xbo = None
    last_xp_2 = {0: None, 1: None}
    wq_user = {0: None, 1: None}
    wk_user = {0: None, 1: None}
    wv_user = {0: None, 1: None}

    for b in range(2):
        base_tok = b * 2048
        ev_xb = [None] * 16
        ev_lnT = [None] * 16
        ev_tT = [None] * 16
        for i in range(16):
            w = []
            if b == 1 and i == 0:
                w.append(("a_xbo", last_xbo))
            ev_xb[i] = kb.op("sync", lambda e, i=i, base_tok=base_tok:
                e.dma_start(t["xb"][:, i * 1024:(i + 1) * 1024],
                            xmir[base_tok + i * 128: base_tok + (i + 1) * 128, :]),
                waits=w, inc="a_dxb", inc_n=16)
            q = i % 2
            kb.op("vector", lambda e, q=q, i=i: e.bn_stats(
                t["bn6"][q][:, 0:6], t["xb"][:, i * 1024:i * 1024 + 512]),
                waits=[("a_dxb", ev_xb[i])])
            e1 = kb.op("vector", lambda e, q=q, i=i: e.bn_stats(
                t["bn6"][q][:, 6:12], t["xb"][:, i * 1024 + 512:(i + 1) * 1024]),
                inc="a_vch")
            e2 = kb.op("vector", lambda e, q=q: e.bn_aggr(
                t["mv"][q][:], t["bn6"][q][:]), waits=[("a_vch", e1)],
                inc="a_vch")
            e3 = kb.op("scalar", lambda e, q=q: e.activation(
                t["sqv"][q][:], t["mv"][q][:, 1:2], AF.Ln,
                bias=c["epsv"][:, 0:1]),
                waits=[("a_vch", e2)], inc="a_sch")
            e4 = kb.op("scalar", lambda e, q=q: e.activation(
                t["rstd"][q][:], t["sqv"][q][:], AF.Exp, scale=-0.5),
                waits=[("a_sch", e3)], inc="a_sch")
            w = [("a_sch", e4)]
            if i >= 2:
                w.append(("a_tT", ev_tT[i - 2]))
            ev_ln = kb.op("vector", lambda e, q=q, i=i: e.tensor_scalar(
                t["aln"][q][:], t["xb"][:, i * 1024:(i + 1) * 1024],
                t["mv"][q][:, 0:1], t["rstd"][q][:], ALU.subtract, ALU.mult),
                waits=w, inc="a_ln")
            ev = None
            for ci in range(8):
                w = []
                if ci == 0:
                    w.append(("a_ln", ev_ln))
                    if i >= 2:
                        w.append(("a_lnT", ev_lnT[i - 2]))
                    if b == 1 and i == 0 and wv_user[0] is not None:
                        w.append(("a_vmm", max(x for x in wv_user.values()
                                               if x is not None)))
                ev = kb.op("tensor", lambda e, q=q, ci=ci: e.transpose(
                    t["psumTA"][:, ci * 128:(ci + 1) * 128],
                    t["aln"][q][:, ci * 128:(ci + 1) * 128],
                    c["identbf"][:]),
                    waits=w, inc="a_tT" if ci == 7 else None)
            ev_tT[i] = ev
            ev_lnT[i] = kb.op("scalar", lambda e, i=i: e.copy(
                lnT_v[:, :, i * 128:(i + 1) * 128], psumTA_v),
                waits=[("a_tT", ev_tT[i])], inc="a_lnT")

        mh = [None, None]
        for h in range(2):
            w = []
            if h == 0 and last_av is not None:
                w.append(("a_av", last_av))
            mh[h] = kb.op("vector", lambda e, h=h: e.memset(
                t["vaug"][h][:], 1.0), waits=w, inc="a_vch")

        for dt in range(8):
            qd = dt % 2
            for m, nmw, user in ((0, "wq", wq_user), (1, "wk", wk_user),
                                 (2, "wv", wv_user)):
                w = []
                if user[qd] is not None:
                    w.append((("a_qmm", "a_kmm", "a_vmm")[m], user[qd]))
                kb.op("gpsimd", lambda e, qd=qd, dt=dt, m=m, nmw=nmw:
                    e.dma_start(t[nmw][qd][:],
                                c["WQKV"][m * 8 + dt:m * 8 + dt + 1],
                                max_dma_last_dim=8192),
                    waits=w, inc=f"a_dw{m}", inc_n=16)
            ev_dw = [kb.get(f"a_dw{m}") for m in range(3)]

            ev = None
            for g in range(4):
                for ci in range(8):
                    w = []
                    if g == 0 and ci == 0:
                        w.append(("a_dw0", ev_dw[0]))
                        w.append(("a_lnT", ev_lnT[15]))
                        if last_nrm is not None:
                            w.append(("a_nrm", last_nrm))
                        if prev_ep_last[0] is not None:
                            w.append(("a_exp", prev_ep_last[0]))
                        if last_ave is not None:
                            w.append(("a_ave", last_ave))
                    ev = kb.op("tensor", lambda e, qd=qd, g=g, ci=ci: e.matmul(
                        t["psumA"][:, g * 512:(g + 1) * 512],
                        t["wq"][qd][:, ci * 128:(ci + 1) * 128],
                        t["lnT"][:, ci * 2048 + g * 512:
                                 ci * 2048 + (g + 1) * 512],
                        start=(ci == 0), stop=(ci == 7)),
                        waits=w, inc="a_qmm" if (g == 3 and ci == 7) else None)
            ev_qmm = ev
            wq_user[qd] = ev_qmm
            ev = None
            for g in range(4):
                ev = kb.op("vector", lambda e, g=g, b=b: e.tensor_scalar(
                    t["qpair"][:, g * 512:(g + 1) * 512],
                    t["psumA"][:, g * 512:(g + 1) * 512],
                    c["rope_sb"][:, 2 * b:2 * b + 1], None, ALU.mult),
                    waits=([("a_qmm", ev_qmm)] if g == 0 else ()),
                    inc="a_qe" if g == 3 else None)
            ev_qe = ev
            ev = None
            for g in range(4):
                for ci in range(8):
                    w = []
                    if g == 0 and ci == 0:
                        w.append(("a_dw1", ev_dw[1]))
                        w.append(("a_qe", ev_qe))
                    ev = kb.op("tensor", lambda e, qd=qd, g=g, ci=ci: e.matmul(
                        t["psumA"][:, g * 512:(g + 1) * 512],
                        t["wk"][qd][:, ci * 128:(ci + 1) * 128],
                        t["lnT"][:, ci * 2048 + g * 512:
                                 ci * 2048 + (g + 1) * 512],
                        start=(ci == 0), stop=(ci == 7)),
                        waits=w, inc="a_kmm" if (g == 3 and ci == 7) else None)
            ev_kmm = ev
            wk_user[qd] = ev_kmm
            ev = None
            for g in range(4):
                ev = kb.op("vector", lambda e, g=g, b=b: e.tensor_scalar(
                    t["kpair"][:, g * 512:(g + 1) * 512],
                    t["psumA"][:, g * 512:(g + 1) * 512],
                    c["rope_sb"][:, 2 * b + 1:2 * b + 2], None, ALU.mult),
                    waits=([("a_kmm", ev_kmm)] if g == 0 else ()),
                    inc="a_ke" if g == 3 else None)
            ev_ke = ev

            ev_vve = [None] * 16
            for i in range(16):
                pb = i % 2
                ev = None
                for ci in range(8):
                    w = []
                    if ci == 0:
                        w.append(("a_dw2", ev_dw[2]))
                        if i == 0:
                            w.append(("a_vch", mh[1]))
                            if last_ave is not None:
                                w.append(("a_ave", last_ave))
                            if last_nrm is not None:
                                w.append(("a_nrm", last_nrm))
                        if i >= 2:
                            w.append(("a_vve", ev_vve[i - 2]))
                    ev = kb.op("tensor", lambda e, qd=qd, pb=pb, i=i, ci=ci:
                        e.matmul(
                            t["psumB"][:, pb * 512:pb * 512 + 128],
                            lnT_v[:, ci, i * 128:(i + 1) * 128],
                            t["wv"][qd][:, ci * 128:(ci + 1) * 128],
                            start=(ci == 0), stop=(ci == 7)),
                        waits=w, inc="a_vmm" if ci == 7 else None)
                ev_vmm_i = ev
                wv_user[qd] = ev_vmm_i
                for h in range(2):
                    ev_vve[i] = kb.op("vector", lambda e, pb=pb, i=i, h=h:
                        e.tensor_copy(
                            t["vaug"][h][:, i * 65:i * 65 + 64],
                            t["psumB"][:, pb * 512 + h * 64:
                                       pb * 512 + (h + 1) * 64]),
                        waits=([("a_vmm", ev_vmm_i)] if h == 0 else ()),
                        inc="a_vve")
            ev_ve = ev_vve[15]

            for h in range(2):
                hb = h * 64
                for g in range(4):
                    ev_sc = [None] * 16
                    ev_ep = [None] * 8
                    av_ev = [None] * 16

                    def emit_sc(kc, h=h, g=g, hb=hb, ev_sc=ev_sc, ev_ep=ev_ep):
                        bnk = kc % 4
                        w = []
                        if kc == 0 and g == 0 and h == 0:
                            w.append(("a_ke", ev_ke))
                        if kc >= 4:
                            w.append(("a_exp", ev_ep[(kc - 4) // 2]))
                        elif prev_ep_last[0] is not None:
                            w.append(("a_exp", prev_ep_last[0]))
                        ev_sc[kc] = kb.op("tensor",
                            lambda e, hb=hb, g=g, kc=kc, bnk=bnk: e.matmul(
                                t["psumA"][:, bnk * 512:(bnk + 1) * 512],
                                t["kpair"][hb:hb + 64, kc * 128:(kc + 1) * 128],
                                t["qpair"][hb:hb + 64, g * 512:(g + 1) * 512],
                                start=True, stop=True),
                            waits=w, inc="a_sc")

                    def emit_ep(p, ev_sc=ev_sc, ev_ep=ev_ep):
                        w = [("a_sc", ev_sc[2 * p + 1])]
                        if prev_av[0] is not None:
                            w.append(("a_av", prev_av[0][2 * p + 1]))
                        half = p % 2
                        ev_ep[p] = kb.op("scalar",
                            lambda e, p=p, half=half: e.activation(
                                t["expT"][:, p * 1024:(p + 1) * 1024],
                                t["psumA"][:, half * 1024:(half + 1) * 1024],
                                AF.Exp),
                            waits=w, inc="a_exp")

                    def emit_av(kc, h=h, g=g, ev_ep=ev_ep, av_ev=av_ev,
                                w_first=()):
                        w = [("a_exp", ev_ep[kc // 2])] + list(w_first)
                        av_ev[kc] = kb.op("tensor", lambda e, h=h, kc=kc:
                            e.matmul(
                                t["psumB"][0:65, 0:512],
                                t["vaug"][h][:, kc * 65:(kc + 1) * 65],
                                t["expT"][:, kc * 512:(kc + 1) * 512],
                                start=(kc == 0), stop=(kc == 15)),
                            waits=w, inc="a_av")

                    for kc in range(4):
                        emit_sc(kc)
                    for p in range(8):
                        emit_ep(p)
                        if p < 6:
                            emit_sc(2 * p + 4)
                            emit_sc(2 * p + 5)
                        for kc in (2 * p, 2 * p + 1):
                            wf = []
                            if kc == 0:
                                if last_ave is not None:
                                    wf.append(("a_ave", last_ave))
                                if g == 0 and h == 0:
                                    wf.append(("a_vve", ev_ve))
                            emit_av(kc, w_first=wf)
                    prev_ep_last[0] = ev_ep[7]
                    prev_av[0] = av_ev
                    last_av = av_ev[15]

                    qg = g % 2
                    w = [("a_av", av_ev[15])]
                    if last_xp_2[qg] is not None:
                        w.append(("a_xp", last_xp_2[qg]))
                    ev_ave = kb.op("vector", lambda e, qg=qg: e.tensor_copy(
                        t["avst"][qg][:], t["psumB"][0:65, 0:512]),
                        waits=w, inc="a_ave")
                    last_ave = ev_ave
                    ev = None
                    for tt in range(4):
                        w = []
                        if tt == 0:
                            w.append(("a_ave", ev_ave))
                            if last_nrm is not None:
                                w.append(("a_nrm", last_nrm))
                        ev = kb.op("tensor", lambda e, qg=qg, tt=tt:
                            e.transpose(
                                t["psumB"][:, 512 + tt * 65:
                                           512 + (tt + 1) * 65],
                                t["avst"][qg][:, tt * 128:(tt + 1) * 128],
                                c["identf"][0:65, 0:65]),
                            waits=w, inc="a_xp" if tt == 3 else None)
                    ev_xp = ev
                    last_xp_2[qg] = ev_xp
                    for tt in range(4):
                        rq = tt % 2
                        e1 = kb.op("vector", lambda e, rq=rq, tt=tt:
                            e.reciprocal(
                                t["rec"][rq][:],
                                t["psumB"][:, 512 + tt * 65 + 64:
                                           512 + tt * 65 + 65]),
                            waits=([("a_xp", ev_xp)] if tt == 0 else ()),
                            inc="a_vch")
                        col = (g * 4 + tt) * 1024 + (dt * 2 + h) * 64
                        last_nrm = kb.op("vector",
                            lambda e, rq=rq, tt=tt, col=col:
                            e.scalar_tensor_tensor(
                                t["xb"][:, col:col + 64],
                                t["psumB"][:, 512 + tt * 65:
                                           512 + tt * 65 + 64],
                                t["rec"][rq][:],
                                t["xb"][:, col:col + 64],
                                ALU.mult, ALU.add),
                            waits=[("a_vch", e1)], inc="a_nrm")

        ev = None
        for i in range(16):
            w = [("a_nrm", last_nrm)] if i == 0 else []
            ev = kb.op("sync", lambda e, i=i, base_tok=base_tok: e.dma_start(
                xmir[base_tok + i * 128: base_tok + (i + 1) * 128, :],
                t["xb"][:, i * 1024:(i + 1) * 1024]),
                waits=w, inc="a_xbo", inc_n=16)
        last_xbo = ev
    return last_xbo


def emit_attn_call(kb, block, c, nc, xmir):
    with ExitStack() as sctx:
        t = alloc_attn_sbuf(sctx, nc)
        with ExitStack() as pctx:
            t["psumA"] = alloc(pctx, nc, "psum", "a_psumA", [128, 2048], F32)
            t["psumB"] = alloc(pctx, nc, "psum", "a_psumB", [128, 1024], F32)
            t["psumTA"] = alloc(pctx, nc, "psum", "a_psumTA", [128, 1024], BF16)
            ev = emit_attn(kb, c, t, xmir)
            kb.play(block)
    return ev


def build_nc(mode="full"):
    nc = bass.Bass("TRN2", target_bir_lowering=False, num_swdge_queues=4)
    kb = KB(nc)

    X = nc.dram_tensor("x", [4096, 1024], F32, kind="ExternalInput").ap()
    OUT = nc.dram_tensor("out", [4096, 1024], F32, kind="ExternalOutput").ap()
    xmir = nc.dram_tensor("xmir", [4096, 1024], F32).ap()
    XeD = nc.dram_tensor("XeD", [NSLOT, 1024], BF16).ap()
    YD = nc.dram_tensor("YD", [NSLOT, 1024], F32).ap()
    ccin = nc.dram_tensor("ccin", [1, 16], F32).ap().opt()
    ccout = nc.dram_tensor("ccout", [8, 16], F32).ap().opt()
    GWT = nc.dram_tensor("gwt", [128, 128], F32).ap()
    CVEC = nc.dram_tensor("cvec", [128, 16], F32, kind="ExternalInput").ap()
    IOTAP = nc.dram_tensor("iotap", [128, 1], F32, kind="ExternalInput").ap()
    CMASK = nc.dram_tensor("cmask", [8, 128], F32, kind="ExternalInput").ap()
    ROPE = nc.dram_tensor("rope", [128, 4], F32, kind="ExternalInput").ap()
    IDB = nc.dram_tensor("identbf", [128, 128], BF16, kind="ExternalInput").ap()
    IDF = nc.dram_tensor("identf", [128, 128], F32, kind="ExternalInput").ap()
    GWI = nc.dram_tensor("gw_in", [128, 128], F32, kind="ExternalInput").ap()
    WQKV = nc.dram_tensor("wqkv", [24, 128, 1024], F32, kind="ExternalInput").ap()
    W1R = nc.dram_tensor("w1r", [256, 128, 2048], F32, kind="ExternalInput").ap()
    W2R = nc.dram_tensor("w2r", [256, 128, 2048], F32, kind="ExternalInput").ap()

    with ExitStack() as cctx, nc.Block() as block:
        c = {}
        sb = lambda n, s, d: alloc(cctx, nc, "sbuf", "c_" + n, s, d)
        c["identbf"] = sb("identbf", [128, 128], BF16)
        c["identf"] = sb("identf", [128, 128], F32)
        c["rope_sb"] = sb("rope", [128, 4], F32)
        c["cvec384"] = sb("cvec", [128, 16], F32)
        c["iotap"] = sb("iotap", [128, 1], F32)
        c["coremask_bc"] = sb("cmask", [8, 128], F32)
        c["gw_sb"] = sb("gw", [128, 128], BF16)
        c["epsv"] = sb("epsv", [128, 1], F32)
        c["WQKV"] = WQKV
        kb.op("vector", lambda e: e.memset(c["epsv"][:], EPS), inc="f_vch")

        for dst, src in ((c["identbf"], IDB), (c["identf"], IDF),
                         (c["rope_sb"], ROPE), (c["cvec384"], CVEC),
                         (c["iotap"], IOTAP), (c["coremask_bc"], CMASK)):
            kb.op("sync", lambda e, dst=dst, src=src: e.dma_start(dst[:], src),
                  inc="c_dma", inc_n=16)
        kb.op("gpsimd", lambda e: setattr(kb, "rbc", e.to_reg(NSLOT - 1)))
        kb.op("gpsimd", lambda e: e.dma_start(c["gw_sb"][:], GWI),
              inc="c_dma", inc_n=16)
        kb.barrier("ph", extra_waits=[("c_dma", kb.get("c_dma"))])
        kb.play(block)

        if mode == "attn":
            kb.op("sync", lambda e: e.dma_start(xmir, X), inc="c_dma", inc_n=16)
            kb.barrier("ph", extra_waits=[("c_dma", kb.get("c_dma"))])
            ev = emit_attn_call(kb, block, c, nc, xmir)
            kb.barrier("ph", extra_waits=[("a_xbo", kb.get("a_xbo"))])
            kb.op("sync", lambda e: e.dma_start(OUT, xmir),
                  inc="f_out", inc_n=16)
            kb.op("sync", lambda e: e.nop(),
                  waits=[("f_out", kb.get("f_out"))])
            kb.play(block)
            return nc

        xo1 = OUT if mode == "ffn" else xmir
        emit_ffn_call(kb, block, None, c, nc, X, xo1, W1R, W2R, XeD, YD,
                      ccin, ccout, "_f1")
        if mode == "ffn":
            kb.op("sync", lambda e: e.nop(),
                  waits=[("f_out", kb.get("f_out"))])
            kb.play(block)
            return nc

        kb.barrier("ph", extra_waits=[("f_out", kb.get("f_out"))])
        kb.play(block)
        emit_attn_call(kb, block, c, nc, xmir)
        kb.barrier("ph", extra_waits=[("a_xbo", kb.get("a_xbo"))])
        kb.play(block)
        emit_ffn_call(kb, block, None, c, nc, xmir, OUT, W1R, W2R, XeD, YD,
                      ccin, ccout, "_f2")
        kb.op("sync", lambda e: e.nop(), waits=[("f_out", kb.get("f_out"))])
        kb.play(block)
    return nc



def host_consts(core):
    identbf = np.eye(128, dtype=ml_dtypes.bfloat16)
    identf = np.eye(128, dtype=np.float32)
    cvec = np.broadcast_to(
        (np.arange(16, dtype=np.float32) * SLOTS)[None, :], (128, 16)).copy()
    iotap = np.arange(128, dtype=np.float32).reshape(128, 1)
    cmask = np.zeros((8, 128), dtype=np.float32)
    cmask[:core, :] = 1.0
    rope = np.zeros((128, 4), dtype=np.float32)
    for bl in range(2):
        bg = 2 * core + bl
        theta = 10000.0 ** (-np.arange(0, 64, 2, dtype=np.float64) / 64.0)
        ang = bg * theta
        r = np.concatenate([np.cos(ang) + np.sin(ang),
                            np.cos(ang) - np.sin(ang)])
        rv = np.tile(r, 2)
        rope[:, 2 * bl] = (rv * 0.125).astype(np.float32)
        rope[:, 2 * bl + 1] = rv.astype(np.float32)
    return {"identbf": identbf, "identf": identf, "cvec": cvec,
            "iotap": iotap, "cmask": cmask, "rope": rope}


def host_weights(Wq, Wk, Wv, gw, w1, w2):
    def wt(W):
        WT = np.ascontiguousarray(W.T)
        return WT.reshape(8, 128, 8, 128).transpose(2, 1, 0, 3).reshape(
            8, 128, 1024)
    wqkv = np.ascontiguousarray(
        np.concatenate([wt(Wq), wt(Wk), wt(Wv)], axis=0))
    gwt = np.ascontiguousarray(
        gw.T.reshape(8, 128, 16).transpose(1, 0, 2).reshape(128, 128))
    w1r = np.ascontiguousarray(
        w1.reshape(16, 8, 128, 16, 256).transpose(0, 3, 2, 1, 4)
        .reshape(256, 128, 2048))
    w2r = np.ascontiguousarray(
        w2.reshape(16, 16, 2, 128, 1024).transpose(0, 1, 3, 2, 4)
        .reshape(256, 128, 2048))
    return {"wqkv": wqkv, "gw_in": gwt, "w1r": w1r, "w2r": w2r}


def make_in_maps(x_full, Wq, Wk, Wv, gw, w1, w2):
    wts = host_weights(Wq, Wk, Wv, gw, w1, w2)
    in_maps = []
    for core in range(8):
        m = dict(wts)
        m.update(host_consts(core))
        m["x"] = np.ascontiguousarray(
            x_full[2 * core:2 * core + 2].reshape(4096, 1024))
        in_maps.append(m)
    return in_maps



_CACHED_NC = None


def kernel(L0_attn_x, L0_attn_Wq, L0_attn_Wk, L0_attn_Wv, L0_ffn_gate_w,
           L0_ffn_w1, L0_ffn_w2):
    global _CACHED_NC
    x = np.asarray(L0_attn_x, dtype=np.float32)
    Wq = np.asarray(L0_attn_Wq, dtype=np.float32)
    Wk = np.asarray(L0_attn_Wk, dtype=np.float32)
    Wv = np.asarray(L0_attn_Wv, dtype=np.float32)
    gw = np.asarray(L0_ffn_gate_w, dtype=np.float32)
    w1 = np.asarray(L0_ffn_w1, dtype=np.float32)
    w2 = np.asarray(L0_ffn_w2, dtype=np.float32)

    if _CACHED_NC is None:
        _CACHED_NC = build_nc("full")
    in_maps = make_in_maps(x, Wq, Wk, Wv, gw, w1, w2)
    res = run_bass_kernel_spmd(_CACHED_NC, in_maps, core_ids=list(range(8)))
    out = np.concatenate(
        [np.asarray(res.results[i]["out"]).reshape(2, 2048, 1024)
         for i in range(8)], axis=0)
    return out


# version 08
# speedup vs baseline: 1.1564x; 1.0036x over previous
import numpy as np
import ml_dtypes
from contextlib import ExitStack

import concourse.bass as bass
import concourse.mybir as mybir
from concourse.bass import IndirectOffsetOnAxis
from concourse.bass_utils import run_bass_kernel_spmd

F32 = mybir.dt.float32
BF16 = mybir.dt.bfloat16
U32 = mybir.dt.uint32
AF = mybir.ActivationFunctionType
ALU = mybir.AluOpType
AX = mybir.AxisListType

NCHUNK = 32
NEXP = 16
SLOTS = 384
NSLOT = NEXP * SLOTS
CAP = 2560.0
EPS = 1e-5
OOBB = 8192.0
CH = 1024
PARANOID = 0


class KB:
    def __init__(self, nc):
        self.nc = nc
        self.cnt = {}
        self.sems = {}
        self.ops = {k: [] for k in ("tensor", "vector", "scalar", "gpsimd", "sync")}
        self.sem_stack = ExitStack()

    def sem(self, name):
        if name not in self.sems:
            self.sems[name] = self.sem_stack.enter_context(self.nc.semaphore(name))
        return self.sems[name]

    def get(self, name):
        return self.cnt.get(name, 0)

    def op(self, eng, fn, waits=(), inc=None, inc_n=1):
        wait_h = [(self.sem(s), int(v)) for (s, v) in waits]
        ev = None
        sem_h = None
        if inc:
            sem_h = self.sem(inc)
            ev = self.cnt[inc] = self.cnt.get(inc, 0) + inc_n

        def play(e, fn=fn, wait_h=wait_h, sem_h=sem_h, inc_n=inc_n):
            for h, v in wait_h:
                e.wait_ge(h, v)
            r = fn(e)
            if sem_h is not None:
                r.then_inc(sem_h, inc_n)
        self.ops[eng].append(play)
        return ev

    def play(self, block):
        for eng in ("tensor", "vector", "scalar", "gpsimd", "sync"):
            ops = self.ops[eng]
            self.ops[eng] = []
            if not ops:
                continue
            sec = getattr(block, eng)

            def body(e, _ops=ops):
                for f in _ops:
                    f(e)
            sec(body)

    def barrier(self, name, extra_waits=()):
        target = self.get(name) + 5
        for eng in self.ops:
            w = list(extra_waits) if eng == "sync" else []
            self.op(eng, lambda e: e.nop(), waits=w, inc=name)
        for eng in self.ops:
            self.op(eng, lambda e: e.nop(), waits=[(name, target)])


def alloc(ctx, nc, kind, name, shape, dtype):
    if kind == "sbuf":
        return ctx.enter_context(nc.sbuf_tensor(name, shape, dtype))
    return ctx.enter_context(nc.psum_tensor(name, shape, dtype))



def alloc_ffn_sbuf(ctx, nc, tagc):
    t = {}
    sb = lambda n, s, d: alloc(ctx, nc, "sbuf", n + tagc, s, d)
    t["xstage"] = [sb(f"xstage{i}", [128, 1024], F32) for i in range(2)]
    t["bn6"] = [sb(f"bn6{i}", [128, 12], F32) for i in range(2)]
    t["mv"] = [sb(f"mv{i}", [128, 2], F32) for i in range(2)]
    t["sqv"] = [sb(f"sqv{i}", [128, 1], F32) for i in range(2)]
    t["rstd"] = [sb(f"rstd{i}", [128, 1], F32) for i in range(2)]
    t["lnbf"] = sb("lnbf", [128, NCHUNK * CH], BF16)
    t["lnt_sc"] = [sb(f"lntsc{i}", [128, 1024], BF16) for i in range(2)]
    t["lg_sc"] = [sb(f"lgsc{i}", [16, 128], F32) for i in range(2)]
    t["nm"] = [sb(f"nm{i}", [128, 1], F32) for i in range(2)]
    t["et"] = [sb(f"et{i}", [128, 16], F32) for i in range(2)]
    t["S"] = [sb(f"S{i}", [128, 1], F32) for i in range(2)]
    t["p_big"] = sb("p_big", [128, 32], F32)
    t["D0"] = [sb(f"D0{i}", [128, 16], F32) for i in range(2)]
    t["Csc"] = [sb(f"Csc{i}", [128, 16], F32) for i in range(2)]
    t["Dm"] = [sb(f"Dm{i}", [128, 16], F32) for i in range(2)]
    t["D_big"] = sb("D_big", [128, NCHUNK * 16], F32)
    t["route"] = sb("route", [16, 2 * 4096], F32)
    t["D_T"] = t["route"][:, 0:4096]
    t["incl"] = t["route"][:, 4096:8192]
    t["ranksT"] = t["route"][:, 0:4096]
    t["hist_all"] = sb("hist_all", [8, 16], F32)
    t["offs_bc"] = sb("offs_bc", [128, 16], F32)
    t["slot_big"] = sb("slot_big", [128, 32], F32)
    t["fits_big"] = sb("fits_big", [128, 32], F32)
    t["coef_big"] = sb("coef_big", [128, 32], F32)
    t["g_big"] = sb("g_big", [128, 32], U32)
    for nm_ in ("sc16a", "sc16b", "sc16c"):
        t[nm_] = [sb(f"{nm_}{i}", [128, 16], F32) for i in range(2)]
    for nm_ in ("sc1a", "sc1b", "sc1c", "sc1d", "sc1e", "sc1f", "sc1g", "sc1h",
                "oobv"):
        t[nm_] = [sb(f"{nm_}{i}", [128, 1], F32) for i in range(2)]
    t["xet"] = [sb(f"xet{i}", [128, 8 * 384], BF16) for i in range(3)]
    t["w1b"] = [sb(f"w1b{i}", [128, 2048], BF16) for i in range(4)]
    t["w2b"] = [sb(f"w2b{i}", [128, 2048], BF16) for i in range(4)]
    t["ht"] = [sb(f"ht{i}", [128, 384], BF16) for i in range(2)]
    t["yev"] = [sb(f"yev{i}", [128, 3072], F32) for i in range(2)]
    t["outg"] = [sb(f"outg{i}", [128, 1024], F32) for i in range(2)]
    t["xstage2"] = t["xstage"]
    t["xnew"] = [sb(f"xnew{i}", [128, 1024], F32) for i in range(2)]
    return t


def emit_ffn_p1(kb, c, t, xin, W1R, W2R):
    pf_w1 = []
    pf_w2 = []
    for gs in range(4):
        pf_w1.append(kb.op("gpsimd", lambda e, gs=gs: e.dma_start(
            t["w1b"][gs][:], W1R[gs:gs + 1], max_dma_last_dim=8192),
            inc="f_w1", inc_n=16))
        pf_w2.append(kb.op("gpsimd", lambda e, gs=gs: e.dma_start(
            t["w2b"][gs][:], W2R[gs:gs + 1], max_dma_last_dim=8192),
            inc="f_w2", inc_n=16))
    ev_ln = [None] * NCHUNK
    ev_lnT = [None] * NCHUNK
    ev_gate = [None] * NCHUNK
    ev_lg = [None] * NCHUNK
    ev_D = [None] * NCHUNK
    ev_DT = [None] * NCHUNK

    pend_B = None

    def emit_B(args):
        j, q, e3 = args
        e4 = kb.op("scalar", lambda e, q=q: e.activation(
            t["rstd"][q][:], t["sqv"][q][:], AF.Exp, scale=-0.5),
            waits=[("f_sch", e3)], inc="f_sch")
        ev_ln[j] = kb.op("vector", lambda e, q=q, j=j: e.tensor_scalar(
            t["lnbf"][:, j * CH:(j + 1) * CH], t["xstage"][q][:],
            t["mv"][q][:, 0:1], t["rstd"][q][:], ALU.subtract, ALU.mult),
            waits=[("f_sch", e4)], inc="f_ln")

    for j in range(NCHUNK):
        q = j % 2
        w = [("f_ln", ev_ln[j - 2])] if j >= 2 else []
        ev_dx = kb.op("sync", lambda e, q=q, j=j: e.dma_start(
            t["xstage"][q][:], xin[j * 128:(j + 1) * 128, :]),
            waits=w, inc="f_dx", inc_n=16)
        kb.op("vector", lambda e, q=q: e.bn_stats(
            t["bn6"][q][:, 0:6], t["xstage"][q][:, 0:512]),
            waits=[("f_dx", ev_dx)])
        e1 = kb.op("vector", lambda e, q=q: e.bn_stats(
            t["bn6"][q][:, 6:12], t["xstage"][q][:, 512:1024]), inc="f_vch")
        e2 = kb.op("vector", lambda e, q=q: e.bn_aggr(
            t["mv"][q][:], t["bn6"][q][:]), waits=[("f_vch", e1)], inc="f_vch")
        e3 = kb.op("scalar", lambda e, q=q: e.activation(
            t["sqv"][q][:], t["mv"][q][:, 1:2], AF.Ln,
            bias=c["epsv"][:, 0:1]),
            waits=[("f_vch", e2)], inc="f_sch")
        if pend_B is not None:
            emit_B(pend_B)
        pend_B = (j, q, e3)
    emit_B(pend_B)

    for j in range(NCHUNK):
        q = j % 2
        ev = None
        for ci in range(8):
            w = []
            if ci == 0:
                w.append(("f_ln", ev_ln[j]))
                if j >= 2:
                    w.append(("f_lnT", ev_lnT[j - 2]))
            ev = kb.op("tensor", lambda e, q=q, j=j, ci=ci: e.transpose(
                t["psumT"][q][:, ci * 128:(ci + 1) * 128],
                t["lnbf"][:, j * CH + ci * 128: j * CH + (ci + 1) * 128],
                c["identbf"][:]),
                waits=w, inc="f_tT" if ci == 7 else None)
        w = [("f_tT", ev)]
        if j >= 2:
            w.append(("f_gate", ev_gate[j - 2]))
        ev_lnT[j] = kb.op("scalar", lambda e, q=q: e.copy(
            t["lnt_sc"][q][:], t["psumT"][q][:]), waits=w, inc="f_lnT")

        ev = None
        for ci in range(8):
            w = []
            if ci == 0:
                w.append(("f_lnT", ev_lnT[j]))
                if j >= 2:
                    w.append(("f_lg", ev_lg[j - 2]))
            ev = kb.op("tensor", lambda e, q=q, ci=ci: e.matmul(
                t["psumG"][q][:], c["gw_sb"][:, ci * 16:(ci + 1) * 16],
                t["lnt_sc"][q][:, ci * 128:(ci + 1) * 128],
                start=(ci == 0), stop=(ci == 7)),
                waits=w, inc="f_gate" if ci == 7 else None)
        ev_gate[j] = ev
        ev_lg[j] = kb.op("scalar", lambda e, q=q: e.copy(
            t["lg_sc"][q][:], t["psumG"][q][:]),
            waits=[("f_gate", ev_gate[j])], inc="f_lg")
        w = [("f_lg", ev_lg[j])]
        if j >= 2:
            w.append(("f_D", ev_D[j - 2]))
        ev_lT = kb.op("tensor", lambda e, q=q: e.transpose(
            t["psumL"][q][:], t["lg_sc"][q][:], c["identf"][0:16, 0:16]),
            waits=w, inc="f_lT")

        e5 = kb.op("vector", lambda e, q=q: e.reduce_max(
            t["nm"][q][:], t["psumL"][q][:], axis=AX.X, negate=True),
            waits=[("f_lT", ev_lT)], inc="f_vch")
        e6 = kb.op("scalar", lambda e, q=q: e.activation(
            t["et"][q][:], t["psumL"][q][:], AF.Exp, bias=t["nm"][q][:, 0:1],
            accum_out=t["S"][q][:]), waits=[("f_vch", e5)], inc="f_sch")
        kb.op("vector", lambda e, q=q, j=j: e.reciprocal(
            t["p_big"][:, j:j + 1], t["S"][q][:]), waits=[("f_sch", e6)])
        e7 = kb.op("vector", lambda e, q=q: e.tensor_scalar(
            t["D0"][q][:], t["psumL"][q][:], t["nm"][q][:, 0:1], 0.0,
            ALU.add, ALU.is_ge), inc="f_vch")
        e8 = kb.op("vector", lambda e, q=q: e.tensor_tensor_scan(
            t["Csc"][q][:], t["D0"][q][:], t["D0"][q][:], 0.0,
            ALU.add, ALU.bypass), waits=[("f_vch", e7)], inc="f_vch")
        e9 = kb.op("vector", lambda e, q=q: e.tensor_scalar(
            t["Dm"][q][:], t["Csc"][q][:], 1.0, None, ALU.is_le),
            waits=[("f_vch", e8)], inc="f_vch")
        ev_D[j] = kb.op("vector", lambda e, q=q, j=j: e.tensor_tensor(
            t["D_big"][:, j * 16:(j + 1) * 16], t["D0"][q][:], t["Dm"][q][:],
            ALU.mult), waits=[("f_vch", e9)], inc="f_D")

        w = [("f_D", ev_D[j])]
        if j >= 2:
            w.append(("f_DT", ev_DT[j - 2]))
        evt = kb.op("tensor", lambda e, q=q, j=j: e.transpose(
            t["psumD"][q][:], t["D_big"][:, j * 16:(j + 1) * 16],
            c["identf"][:]), waits=w, inc="f_tD")
        ev_DT[j] = kb.op("scalar", lambda e, q=q, j=j: e.copy(
            t["D_T"][:, j * 128:(j + 1) * 128], t["psumD"][q][:]),
            waits=[("f_tD", evt)], inc="f_DT")

    return {"ev_D": ev_D, "ev_DT": ev_DT, "pf_w1": pf_w1, "pf_w2": pf_w2}


def emit_ffn_p2(kb, c, t, ev1, XeD, ccin, ccout):
    ev_scan = kb.op("vector", lambda e: e.tensor_tensor_scan(
        t["incl"], t["D_T"], t["D_T"], 0.0, ALU.add, ALU.bypass),
        waits=[("f_DT", ev1["ev_DT"][NCHUNK - 1])], inc="f_vch")
    ev_rk = kb.op("vector", lambda e: e.tensor_tensor(
        t["ranksT"], t["incl"], t["D_T"], ALU.subtract),
        waits=[("f_vch", ev_scan)], inc="f_rk")
    ev_h = kb.op("sync", lambda e: e.dma_start(
        ccin, t["incl"][:, 4095:4096]), waits=[("f_rk", ev_rk)],
        inc="f_hist", inc_n=16)
    ev_cc = kb.op("gpsimd", lambda e: e.collective_compute(
        "AllGather", ALU.bypass, replica_groups=[list(range(8))],
        ins=[ccin], outs=[ccout]), waits=[("f_hist", ev_h)], inc="f_cc")
    ev_hb = kb.op("scalar", lambda e: e.dma_start(
        t["hist_all"][:], ccout), waits=[("f_cc", ev_cc)],
        inc="f_ccb", inc_n=16)

    ev_sg = [None] * NCHUNK
    ev_scat = [None] * NCHUNK
    ev_D = ev1["ev_D"]
    for j in range(NCHUNK):
        q = j % 2
        w = [("f_rk", ev_rk)]
        if j >= 2:
            w.append(("f_slot", ev_sg[j - 2]))
        ev_rtp = kb.op("tensor", lambda e, q=q, j=j: e.transpose(
            t["psumR"][q][:], t["ranksT"][:, j * 128:(j + 1) * 128],
            c["identf"][0:16, 0:16]), waits=w, inc="f_rtp")

        Dj = t["D_big"][:, j * 16:(j + 1) * 16]
        e1a = kb.op("vector", lambda e, q=q, Dj=Dj: e.tensor_tensor(
            t["sc16a"][q][:], t["psumR"][q][:], Dj, ALU.mult),
            waits=[("f_rtp", ev_rtp), ("f_D", ev_D[j])], inc="f_vch")
        e1 = kb.op("vector", lambda e, q=q, j=j: e.reduce_sum(
            t["slot_big"][:, j:j + 1], t["sc16a"][q][:], axis=AX.X),
            waits=[("f_vch", e1a)], inc="f_vch")
        e2 = kb.op("vector", lambda e, j=j: e.tensor_scalar(
            t["fits_big"][:, j:j + 1], t["slot_big"][:, j:j + 1],
            float(SLOTS), None, ALU.is_lt), waits=[("f_vch", e1)], inc="f_vch")
        e3a = kb.op("vector", lambda e, q=q, Dj=Dj: e.tensor_tensor(
            t["sc16b"][q][:], c["cvec384"][:], Dj, ALU.mult),
            waits=[("f_vch", e2)], inc="f_vch")
        e3 = kb.op("vector", lambda e, q=q: e.reduce_sum(
            t["sc1a"][q][:], t["sc16b"][q][:], axis=AX.X),
            waits=[("f_vch", e3a)], inc="f_vch")
        e4 = kb.op("vector", lambda e, q=q, j=j: e.tensor_tensor(
            t["sc1b"][q][:], t["sc1a"][q][:], t["slot_big"][:, j:j + 1],
            ALU.add), waits=[("f_vch", e3)], inc="f_vch")
        e5 = kb.op("vector", lambda e, q=q, j=j: e.tensor_scalar(
            t["oobv"][q][:], c["iotap"][:], OOBB + 128.0 * j, None, ALU.add),
            waits=[("f_vch", e4)], inc="f_vch")
        e6 = kb.op("vector", lambda e, q=q: e.tensor_tensor(
            t["sc1c"][q][:], t["sc1b"][q][:], t["oobv"][q][:], ALU.subtract),
            waits=[("f_vch", e5)], inc="f_vch")
        e7 = kb.op("vector", lambda e, q=q, j=j: e.scalar_tensor_tensor(
            t["sc1d"][q][:], t["sc1c"][q][:], t["fits_big"][:, j:j + 1],
            t["oobv"][q][:], ALU.mult, ALU.add), waits=[("f_vch", e6)],
            inc="f_vch")
        ev_sg[j] = kb.op("vector", lambda e, q=q, j=j: e.tensor_copy(
            t["g_big"][:, j:j + 1], t["sc1d"][q][:]),
            waits=[("f_vch", e7)], inc="f_slot")

        ev_scat[j] = kb.op("gpsimd", lambda e, j=j: e.indirect_dma_start(
            XeD, IndirectOffsetOnAxis(t["g_big"][:, j:j + 1], 0),
            t["lnbf"][:, j * CH:(j + 1) * CH], None,
            bounds_check=kb.rbc, oob_is_err=False),
            waits=[("f_slot", ev_sg[j])], inc="f_scat", inc_n=16)

    ev_om = kb.op("tensor", lambda e: e.matmul(
        t["psumO"][:], c["coremask_bc"][:], t["hist_all"][:],
        start=True, stop=True),
        waits=[("f_ccb", ev_hb), ("f_slot", ev_sg[NCHUNK - 1])], inc="f_om")
    ev_offs = kb.op("scalar", lambda e: e.copy(
        t["offs_bc"][:], t["psumO"][:]), waits=[("f_om", ev_om)],
        inc="f_offs")

    return {"ev_scat": ev_scat, "ev_offs": ev_offs, "ev_D": ev_D}


def emit_ffn_p34(kb, c, t, ev2, W1R, W2R, XeD, YD, xin, xout, ev1):
    ev_coef = [None] * NCHUNK
    for j in range(NCHUNK):
        q = j % 2
        Dj = t["D_big"][:, j * 16:(j + 1) * 16]
        w = [("f_offs", ev2["ev_offs"])] if j == 0 else []
        e1a = kb.op("vector", lambda e, q=q, Dj=Dj: e.tensor_tensor(
            t["sc16c"][q][:], t["offs_bc"][:], Dj, ALU.mult),
            waits=w, inc="f_vch")
        e1 = kb.op("vector", lambda e, q=q: e.reduce_sum(
            t["sc1e"][q][:], t["sc16c"][q][:], axis=AX.X),
            waits=[("f_vch", e1a)], inc="f_vch")
        e2 = kb.op("vector", lambda e, q=q, j=j: e.tensor_tensor(
            t["sc1f"][q][:], t["sc1e"][q][:], t["slot_big"][:, j:j + 1],
            ALU.add), waits=[("f_vch", e1)], inc="f_vch")
        e3 = kb.op("vector", lambda e, q=q: e.tensor_scalar(
            t["sc1g"][q][:], t["sc1f"][q][:], CAP, None, ALU.is_lt),
            waits=[("f_vch", e2)], inc="f_vch")
        e4 = kb.op("vector", lambda e, q=q, j=j: e.tensor_tensor(
            t["sc1h"][q][:], t["sc1g"][q][:], t["fits_big"][:, j:j + 1],
            ALU.mult), waits=[("f_vch", e3)], inc="f_vch")
        ev_coef[j] = kb.op("vector", lambda e, q=q, j=j: e.tensor_scalar(
            t["coef_big"][:, j:j + 1], t["sc1h"][q][:],
            t["p_big"][:, j:j + 1], 0.5, ALU.mult, ALU.mult),
            waits=[("f_vch", e4)], inc="f_coef")

    scat_join = ev2["ev_scat"][NCHUNK - 1]
    if PARANOID & 1:
        kb.barrier("pb1", extra_waits=[("f_scat", kb.get("f_scat"))])
        scat_join = 0
    ev_xet = [None] * NEXP
    ev_w1 = [None] * (NEXP * 16)
    ev_w2 = [None] * (NEXP * 16)
    ev_w1[0:4] = ev1["pf_w1"]
    ev_w2[0:4] = ev1["pf_w2"]
    ev_yev = [None] * NEXP
    ev_ydma = [None] * NEXP
    b_hmm = kb.get("f_hmm")
    b_h = kb.get("f_h")
    b_htu = kb.get("f_htu")
    ev_hmm = [b_hmm + 1 + i for i in range(NEXP * 32)]
    ev_h = [b_h + 1 + i for i in range(NEXP * 32)]
    ev_hu = [b_htu + 1 + i for i in range(NEXP * 32)]

    for ei in range(NEXP):
        q = ei % 2
        qx = ei % 3
        ev = None
        for ci in range(8):
            w = []
            if ci == 0:
                w.append(("f_scat", scat_join))
                if ei >= 3:
                    w.append(("f_hmm", ev_hmm[(ei - 3) * 32 + 31]))
            ev = kb.op("sync", lambda e, qx=qx, ei=ei, ci=ci: e.dma_start(
                t["xet"][qx][:, ci * 384:(ci + 1) * 384],
                XeD[ei * SLOTS:(ei + 1) * SLOTS, ci * 128:(ci + 1) * 128],
                transpose=True), waits=w, inc="f_xet", inc_n=16)
        ev_xet[ei] = ev

        for sg in range(16):
            gs = ei * 16 + sg
            if gs < 4:
                continue
            qq = gs % 4
            w = [("f_hmm", ev_hmm[(gs - 4) * 2 + 1])]
            ev_w1[gs] = kb.op("gpsimd", lambda e, qq=qq, gs=gs:
                e.dma_start(t["w1b"][qq][:], W1R[gs:gs + 1],
                            max_dma_last_dim=8192),
                waits=w, inc="f_w1", inc_n=16)
            w = [("f_htu", ev_hu[(gs - 4) * 2 + 1])]
            ev_w2[gs] = kb.op("gpsimd", lambda e, qq=qq, gs=gs:
                e.dma_start(t["w2b"][qq][:], W2R[gs:gs + 1],
                            max_dma_last_dim=8192),
                waits=w, inc="f_w2", inc_n=16)

        pend_w2 = None

        def emit_w2(args, ei=ei):
            gs_, gft_, hq_, qq_, fj_, ft_ = args
            ev = None
            for st in range(3):
                for dh in range(2):
                    w = []
                    if st == 0 and dh == 0:
                        w.append(("f_h", ev_h[gft_]))
                        w.append(("f_w2", ev_w2[gs_]))
                        if ft_ == 0 and ei >= 1:
                            w.append(("f_yev", ev_yev[ei - 1]))
                    ev = kb.op("tensor",
                        lambda e, hq=hq_, qq=qq_, st=st, dh=dh, fj=fj_,
                        ft=ft_: e.matmul(
                            t["psumY"][:, st * 1024 + dh * 512:
                                       st * 1024 + dh * 512 + 512],
                            t["ht"][hq][:, st * 128:(st + 1) * 128],
                            t["w2b"][qq][:, fj * 1024 + dh * 512:
                                          fj * 1024 + dh * 512 + 512],
                            start=(ft == 0), stop=(ft == 31)),
                        waits=w,
                        inc="f_htu" if (st == 2 and dh == 1) else None)
            assert ev == ev_hu[gft_], (ev, ev_hu[gft_])

        for sg in range(16):
            gs = ei * 16 + sg
            qq = gs % 4
            for fj in range(2):
                ft = sg * 2 + fj
                gft = ei * 32 + ft
                hq = ft % 2
                ev = None
                for ci in range(8):
                    w = []
                    if ci == 0:
                        w.append(("f_w1", ev_w1[gs]))
                        if ft == 0:
                            w.append(("f_xet", ev_xet[ei]))
                            if ei == 0:
                                w.append(("f_offs", ev2["ev_offs"]))
                        if gft >= 2:
                            w.append(("f_h", ev_h[gft - 2]))
                    ev = kb.op("tensor",
                        lambda e, qx=qx, qq=qq, hq=hq, ci=ci, fj=fj: e.matmul(
                            t["psumH"][hq][:],
                            t["w1b"][qq][:, ci * 256 + fj * 128:
                                          ci * 256 + fj * 128 + 128],
                            t["xet"][qx][:, ci * 384:(ci + 1) * 384],
                            start=(ci == 0), stop=(ci == 7)),
                        waits=w, inc="f_hmm" if ci == 7 else None)
                assert ev == ev_hmm[gft], (ev, ev_hmm[gft])
                w = [("f_hmm", ev_hmm[gft])]
                if gft >= 2:
                    w.append(("f_htu", ev_hu[gft - 2]))
                evg = kb.op("scalar", lambda e, hq=hq: e.activation(
                    t["ht"][hq][:], t["psumH"][hq][:], AF.Gelu_apprx_tanh),
                    waits=w, inc="f_h")
                assert evg == ev_h[gft], (evg, ev_h[gft])
                if pend_w2 is not None:
                    emit_w2(pend_w2)
                pend_w2 = (gs, gft, hq, qq, fj, ft)
        emit_w2(pend_w2)

        w = [("f_htu", ev_hu[ei * 32 + 31])]
        if ei >= 1:
            w.append(("f_y", ev_ydma[ei - 1]))
        ev_yev[ei] = kb.op("vector", lambda e, q=q: e.tensor_copy(
            t["yev"][q][:], t["psumY"][:]), waits=w, inc="f_yev")
        ev_ydma[ei] = kb.op("gpsimd", lambda e, q=q, ei=ei: e.dma_start(
            YD[ei * SLOTS:(ei + 1) * SLOTS, :].rearrange(
                "(st sp) d -> sp st d", sp=128),
            t["yev"][q][:].rearrange("p (st d) -> p st d", st=3)),
            waits=[("f_yev", ev_yev[ei])],
            inc="f_y", inc_n=16)

    ev_ms = None
    for q in range(2):
        ev_ms = kb.op("vector", lambda e, q=q: e.memset(
            t["outg"][q][:], 0.0), inc="f_vch")
    y_join = ev_ydma[NEXP - 1]
    if PARANOID & 2:
        kb.barrier("pb2", extra_waits=[("f_y", y_join)])
        y_join = 0
    ev_comb = [None] * NCHUNK
    ev_out = [None] * NCHUNK
    for j in range(NCHUNK):
        q = j % 2
        w = [("f_y", y_join)]
        if j >= 2:
            w.append(("f_comb", ev_comb[j - 2]))
        else:
            w.append(("f_vch", ev_ms))
        ev_g = kb.op("gpsimd", lambda e, q=q, j=j: e.indirect_dma_start(
            t["outg"][q][:], None,
            YD, IndirectOffsetOnAxis(t["g_big"][:, j:j + 1], 0),
            bounds_check=kb.rbc, oob_is_err=False),
            waits=w, inc="f_g", inc_n=16)
        w = [("f_comb", ev_comb[j - 2])] if j >= 2 else []
        ev_x2 = kb.op("sync", lambda e, q=q, j=j: e.dma_start(
            t["xstage2"][q][:], xin[j * 128:(j + 1) * 128, :]),
            waits=w, inc="f_x2", inc_n=16)
        w = [("f_g", ev_g), ("f_x2", ev_x2), ("f_coef", ev_coef[j])]
        if j >= 2:
            w.append(("f_out", ev_out[j - 2]))
        ev_comb[j] = kb.op("vector", lambda e, q=q, j=j:
            e.scalar_tensor_tensor(
                t["xnew"][q][:], t["outg"][q][:], t["coef_big"][:, j:j + 1],
                t["xstage2"][q][:], ALU.mult, ALU.add),
            waits=w, inc="f_comb")
        ev_out[j] = kb.op("sync", lambda e, q=q, j=j: e.dma_start(
            xout[j * 128:(j + 1) * 128, :], t["xnew"][q][:]),
            waits=[("f_comb", ev_comb[j])], inc="f_out", inc_n=16)
    return ev_out[NCHUNK - 1]


def emit_ffn_call(kb, block, ctx_consts, c, nc, xin, xout, W1R, W2R, XeD, YD,
                  ccin, ccout, tagc):
    with ExitStack() as sctx:
        t = alloc_ffn_sbuf(sctx, nc, tagc)
        with ExitStack() as pctx:
            ps = lambda n, s, d: alloc(pctx, nc, "psum", n + tagc, s, d)
            t["psumT"] = [ps(f"psumT{i}", [128, 1024], BF16) for i in range(2)]
            t["psumG"] = [ps(f"psumG{i}", [16, 128], F32) for i in range(2)]
            t["psumL"] = [ps(f"psumL{i}", [128, 16], F32) for i in range(2)]
            t["psumD"] = [ps(f"psumD{i}", [16, 128], F32) for i in range(2)]
            ev1 = emit_ffn_p1(kb, c, t, xin, W1R, W2R)
            kb.play(block)
        with ExitStack() as pctx:
            ps = lambda n, s, d: alloc(pctx, nc, "psum", n + tagc, s, d)
            t["psumR"] = [ps(f"psumR{i}", [128, 16], F32) for i in range(2)]
            t["psumO"] = ps("psumO", [128, 16], F32)
            ev2 = emit_ffn_p2(kb, c, t, ev1, XeD, ccin, ccout)
            kb.play(block)
        with ExitStack() as pctx:
            ps = lambda n, s, d: alloc(pctx, nc, "psum", n + tagc, s, d)
            t["psumH"] = [ps(f"psumH{i}", [128, 384], F32) for i in range(2)]
            t["psumY"] = ps("psumY", [128, 3072], F32)
            ev_last = emit_ffn_p34(kb, c, t, ev2, W1R, W2R, XeD, YD, xin,
                                    xout, ev1)
            kb.play(block)
    return ev_last




def alloc_attn_sbuf(ctx, nc):
    t = {}
    sb = lambda n, s, d: alloc(ctx, nc, "sbuf", "a_" + n, s, d)
    t["xb"] = sb("xb", [128, 16 * 1024], F32)
    t["bn6"] = [sb(f"bn6{i}", [128, 12], F32) for i in range(2)]
    t["mv"] = [sb(f"mv{i}", [128, 2], F32) for i in range(2)]
    t["sqv"] = [sb(f"sqv{i}", [128, 1], F32) for i in range(2)]
    t["rstd"] = [sb(f"rstd{i}", [128, 1], F32) for i in range(2)]
    t["aln"] = [sb(f"aln{i}", [128, 1024], BF16) for i in range(2)]
    t["lnT"] = sb("lnT", [128, 16384], BF16)
    t["wq"] = [sb(f"wq{i}", [128, 1024], BF16) for i in range(2)]
    t["wk"] = [sb(f"wk{i}", [128, 1024], BF16) for i in range(2)]
    t["wv"] = [sb(f"wv{i}", [128, 1024], BF16) for i in range(2)]
    t["qpair"] = sb("qpair", [128, 2048], BF16)
    t["kpair"] = sb("kpair", [128, 2048], BF16)
    t["vaug"] = [sb(f"vaug{i}", [128, 16 * 65], BF16) for i in range(2)]
    t["expT"] = sb("expT", [128, 16 * 512], BF16)
    t["avst"] = [sb(f"avst{i}", [65, 512], F32) for i in range(2)]
    t["rec"] = [sb(f"rec{i}", [128, 1], F32) for i in range(2)]
    return t


def emit_attn(kb, c, t, xmir):
    lnT_v = t["lnT"][:].rearrange("p (c s) -> p c s", c=8)
    psumTA_v = t["psumTA"][:].rearrange("p (c s) -> p c s", c=8)

    last_nrm = None
    prev_ep_last = [None]
    prev_av = [None]
    last_ave = None
    last_av = None
    last_xbo = None
    last_xp_2 = {0: None, 1: None}
    wq_user = {0: None, 1: None}
    wk_user = {0: None, 1: None}
    wv_user = {0: None, 1: None}

    for b in range(2):
        base_tok = b * 2048
        ev_xb = [None] * 16
        ev_lnT = [None] * 16
        ev_tT = [None] * 16
        for i in range(16):
            w = []
            if b == 1 and i == 0:
                w.append(("a_xbo", last_xbo))
            ev_xb[i] = kb.op("sync", lambda e, i=i, base_tok=base_tok:
                e.dma_start(t["xb"][:, i * 1024:(i + 1) * 1024],
                            xmir[base_tok + i * 128: base_tok + (i + 1) * 128, :]),
                waits=w, inc="a_dxb", inc_n=16)
            q = i % 2
            kb.op("vector", lambda e, q=q, i=i: e.bn_stats(
                t["bn6"][q][:, 0:6], t["xb"][:, i * 1024:i * 1024 + 512]),
                waits=[("a_dxb", ev_xb[i])])
            e1 = kb.op("vector", lambda e, q=q, i=i: e.bn_stats(
                t["bn6"][q][:, 6:12], t["xb"][:, i * 1024 + 512:(i + 1) * 1024]),
                inc="a_vch")
            e2 = kb.op("vector", lambda e, q=q: e.bn_aggr(
                t["mv"][q][:], t["bn6"][q][:]), waits=[("a_vch", e1)],
                inc="a_vch")
            e3 = kb.op("scalar", lambda e, q=q: e.activation(
                t["sqv"][q][:], t["mv"][q][:, 1:2], AF.Ln,
                bias=c["epsv"][:, 0:1]),
                waits=[("a_vch", e2)], inc="a_sch")
            e4 = kb.op("scalar", lambda e, q=q: e.activation(
                t["rstd"][q][:], t["sqv"][q][:], AF.Exp, scale=-0.5),
                waits=[("a_sch", e3)], inc="a_sch")
            w = [("a_sch", e4)]
            if i >= 2:
                w.append(("a_tT", ev_tT[i - 2]))
            ev_ln = kb.op("vector", lambda e, q=q, i=i: e.tensor_scalar(
                t["aln"][q][:], t["xb"][:, i * 1024:(i + 1) * 1024],
                t["mv"][q][:, 0:1], t["rstd"][q][:], ALU.subtract, ALU.mult),
                waits=w, inc="a_ln")
            ev = None
            for ci in range(8):
                w = []
                if ci == 0:
                    w.append(("a_ln", ev_ln))
                    if i >= 2:
                        w.append(("a_lnT", ev_lnT[i - 2]))
                    if b == 1 and i == 0 and wv_user[0] is not None:
                        w.append(("a_vmm", max(x for x in wv_user.values()
                                               if x is not None)))
                ev = kb.op("tensor", lambda e, q=q, ci=ci: e.transpose(
                    t["psumTA"][:, ci * 128:(ci + 1) * 128],
                    t["aln"][q][:, ci * 128:(ci + 1) * 128],
                    c["identbf"][:]),
                    waits=w, inc="a_tT" if ci == 7 else None)
            ev_tT[i] = ev
            ev_lnT[i] = kb.op("scalar", lambda e, i=i: e.copy(
                lnT_v[:, :, i * 128:(i + 1) * 128], psumTA_v),
                waits=[("a_tT", ev_tT[i])], inc="a_lnT")

        mh = [None, None]
        for h in range(2):
            w = []
            if h == 0 and last_av is not None:
                w.append(("a_av", last_av))
            mh[h] = kb.op("vector", lambda e, h=h: e.memset(
                t["vaug"][h][:], 1.0), waits=w, inc="a_vch")

        for dt in range(8):
            qd = dt % 2
            for m, nmw, user in ((0, "wq", wq_user), (1, "wk", wk_user),
                                 (2, "wv", wv_user)):
                w = []
                if user[qd] is not None:
                    w.append((("a_qmm", "a_kmm", "a_vmm")[m], user[qd]))
                kb.op("gpsimd", lambda e, qd=qd, dt=dt, m=m, nmw=nmw:
                    e.dma_start(t[nmw][qd][:],
                                c["WQKV"][m * 8 + dt:m * 8 + dt + 1],
                                max_dma_last_dim=8192),
                    waits=w, inc=f"a_dw{m}", inc_n=16)
            ev_dw = [kb.get(f"a_dw{m}") for m in range(3)]

            ev = None
            for g in range(4):
                for ci in range(8):
                    w = []
                    if g == 0 and ci == 0:
                        w.append(("a_dw0", ev_dw[0]))
                        w.append(("a_lnT", ev_lnT[15]))
                        if last_nrm is not None:
                            w.append(("a_nrm", last_nrm))
                        if prev_ep_last[0] is not None:
                            w.append(("a_exp", prev_ep_last[0]))
                        if last_ave is not None:
                            w.append(("a_ave", last_ave))
                    ev = kb.op("tensor", lambda e, qd=qd, g=g, ci=ci: e.matmul(
                        t["psumA"][:, g * 512:(g + 1) * 512],
                        t["wq"][qd][:, ci * 128:(ci + 1) * 128],
                        t["lnT"][:, ci * 2048 + g * 512:
                                 ci * 2048 + (g + 1) * 512],
                        start=(ci == 0), stop=(ci == 7)),
                        waits=w, inc="a_qmm" if (g == 3 and ci == 7) else None)
            ev_qmm = ev
            wq_user[qd] = ev_qmm
            ev = None
            for g in range(4):
                ev = kb.op("vector", lambda e, g=g, b=b: e.tensor_scalar(
                    t["qpair"][:, g * 512:(g + 1) * 512],
                    t["psumA"][:, g * 512:(g + 1) * 512],
                    c["rope_sb"][:, 2 * b:2 * b + 1], None, ALU.mult),
                    waits=([("a_qmm", ev_qmm)] if g == 0 else ()),
                    inc="a_qe" if g == 3 else None)
            ev_qe = ev
            ev = None
            for g in range(4):
                for ci in range(8):
                    w = []
                    if g == 0 and ci == 0:
                        w.append(("a_dw1", ev_dw[1]))
                        w.append(("a_qe", ev_qe))
                    ev = kb.op("tensor", lambda e, qd=qd, g=g, ci=ci: e.matmul(
                        t["psumA"][:, g * 512:(g + 1) * 512],
                        t["wk"][qd][:, ci * 128:(ci + 1) * 128],
                        t["lnT"][:, ci * 2048 + g * 512:
                                 ci * 2048 + (g + 1) * 512],
                        start=(ci == 0), stop=(ci == 7)),
                        waits=w, inc="a_kmm" if (g == 3 and ci == 7) else None)
            ev_kmm = ev
            wk_user[qd] = ev_kmm
            ev = None
            for g in range(4):
                ev = kb.op("vector", lambda e, g=g, b=b: e.tensor_scalar(
                    t["kpair"][:, g * 512:(g + 1) * 512],
                    t["psumA"][:, g * 512:(g + 1) * 512],
                    c["rope_sb"][:, 2 * b + 1:2 * b + 2], None, ALU.mult),
                    waits=([("a_kmm", ev_kmm)] if g == 0 else ()),
                    inc="a_ke" if g == 3 else None)
            ev_ke = ev

            ev_vve = [None] * 16
            for i in range(16):
                pb = i % 2
                ev = None
                for ci in range(8):
                    w = []
                    if ci == 0:
                        w.append(("a_dw2", ev_dw[2]))
                        if i == 0:
                            w.append(("a_vch", mh[1]))
                            if last_ave is not None:
                                w.append(("a_ave", last_ave))
                            if last_nrm is not None:
                                w.append(("a_nrm", last_nrm))
                        if i >= 2:
                            w.append(("a_vve", ev_vve[i - 2]))
                    ev = kb.op("tensor", lambda e, qd=qd, pb=pb, i=i, ci=ci:
                        e.matmul(
                            t["psumB"][:, pb * 512:pb * 512 + 128],
                            lnT_v[:, ci, i * 128:(i + 1) * 128],
                            t["wv"][qd][:, ci * 128:(ci + 1) * 128],
                            start=(ci == 0), stop=(ci == 7)),
                        waits=w, inc="a_vmm" if ci == 7 else None)
                ev_vmm_i = ev
                wv_user[qd] = ev_vmm_i
                for h in range(2):
                    ev_vve[i] = kb.op("vector", lambda e, pb=pb, i=i, h=h:
                        e.tensor_copy(
                            t["vaug"][h][:, i * 65:i * 65 + 64],
                            t["psumB"][:, pb * 512 + h * 64:
                                       pb * 512 + (h + 1) * 64]),
                        waits=([("a_vmm", ev_vmm_i)] if h == 0 else ()),
                        inc="a_vve")
            ev_ve = ev_vve[15]

            for h in range(2):
                hb = h * 64
                for g in range(4):
                    ev_sc = [None] * 16
                    ev_ep = [None] * 8
                    av_ev = [None] * 16

                    def emit_sc(kc, h=h, g=g, hb=hb, ev_sc=ev_sc, ev_ep=ev_ep):
                        bnk = kc % 4
                        w = []
                        if kc == 0 and g == 0 and h == 0:
                            w.append(("a_ke", ev_ke))
                        if kc >= 4:
                            w.append(("a_exp", ev_ep[(kc - 4) // 2]))
                        elif prev_ep_last[0] is not None:
                            w.append(("a_exp", prev_ep_last[0]))
                        ev_sc[kc] = kb.op("tensor",
                            lambda e, hb=hb, g=g, kc=kc, bnk=bnk: e.matmul(
                                t["psumA"][:, bnk * 512:(bnk + 1) * 512],
                                t["kpair"][hb:hb + 64, kc * 128:(kc + 1) * 128],
                                t["qpair"][hb:hb + 64, g * 512:(g + 1) * 512],
                                start=True, stop=True),
                            waits=w, inc="a_sc")

                    def emit_ep(p, ev_sc=ev_sc, ev_ep=ev_ep):
                        w = [("a_sc", ev_sc[2 * p + 1])]
                        if prev_av[0] is not None:
                            w.append(("a_av", prev_av[0][2 * p + 1]))
                        half = p % 2
                        ev_ep[p] = kb.op("scalar",
                            lambda e, p=p, half=half: e.activation(
                                t["expT"][:, p * 1024:(p + 1) * 1024],
                                t["psumA"][:, half * 1024:(half + 1) * 1024],
                                AF.Exp),
                            waits=w, inc="a_exp")

                    def emit_av(kc, h=h, g=g, ev_ep=ev_ep, av_ev=av_ev,
                                w_first=()):
                        w = [("a_exp", ev_ep[kc // 2])] + list(w_first)
                        av_ev[kc] = kb.op("tensor", lambda e, h=h, kc=kc:
                            e.matmul(
                                t["psumB"][0:65, 0:512],
                                t["vaug"][h][:, kc * 65:(kc + 1) * 65],
                                t["expT"][:, kc * 512:(kc + 1) * 512],
                                start=(kc == 0), stop=(kc == 15)),
                            waits=w, inc="a_av")

                    for kc in range(4):
                        emit_sc(kc)
                    for p in range(8):
                        emit_ep(p)
                        if p < 6:
                            emit_sc(2 * p + 4)
                            emit_sc(2 * p + 5)
                        for kc in (2 * p, 2 * p + 1):
                            wf = []
                            if kc == 0:
                                if last_ave is not None:
                                    wf.append(("a_ave", last_ave))
                                if g == 0 and h == 0:
                                    wf.append(("a_vve", ev_ve))
                            emit_av(kc, w_first=wf)
                    prev_ep_last[0] = ev_ep[7]
                    prev_av[0] = av_ev
                    last_av = av_ev[15]

                    qg = g % 2
                    w = [("a_av", av_ev[15])]
                    if last_xp_2[qg] is not None:
                        w.append(("a_xp", last_xp_2[qg]))
                    ev_ave = kb.op("vector", lambda e, qg=qg: e.tensor_copy(
                        t["avst"][qg][:], t["psumB"][0:65, 0:512]),
                        waits=w, inc="a_ave")
                    last_ave = ev_ave
                    ev = None
                    for tt in range(4):
                        w = []
                        if tt == 0:
                            w.append(("a_ave", ev_ave))
                            if last_nrm is not None:
                                w.append(("a_nrm", last_nrm))
                        ev = kb.op("tensor", lambda e, qg=qg, tt=tt:
                            e.transpose(
                                t["psumB"][:, 512 + tt * 65:
                                           512 + (tt + 1) * 65],
                                t["avst"][qg][:, tt * 128:(tt + 1) * 128],
                                c["identf"][0:65, 0:65]),
                            waits=w, inc="a_xp" if tt == 3 else None)
                    ev_xp = ev
                    last_xp_2[qg] = ev_xp
                    for tt in range(4):
                        rq = tt % 2
                        e1 = kb.op("vector", lambda e, rq=rq, tt=tt:
                            e.reciprocal(
                                t["rec"][rq][:],
                                t["psumB"][:, 512 + tt * 65 + 64:
                                           512 + tt * 65 + 65]),
                            waits=([("a_xp", ev_xp)] if tt == 0 else ()),
                            inc="a_vch")
                        col = (g * 4 + tt) * 1024 + (dt * 2 + h) * 64
                        last_nrm = kb.op("vector",
                            lambda e, rq=rq, tt=tt, col=col:
                            e.scalar_tensor_tensor(
                                t["xb"][:, col:col + 64],
                                t["psumB"][:, 512 + tt * 65:
                                           512 + tt * 65 + 64],
                                t["rec"][rq][:],
                                t["xb"][:, col:col + 64],
                                ALU.mult, ALU.add),
                            waits=[("a_vch", e1)], inc="a_nrm")

        ev = None
        for i in range(16):
            w = [("a_nrm", last_nrm)] if i == 0 else []
            ev = kb.op("sync", lambda e, i=i, base_tok=base_tok: e.dma_start(
                xmir[base_tok + i * 128: base_tok + (i + 1) * 128, :],
                t["xb"][:, i * 1024:(i + 1) * 1024]),
                waits=w, inc="a_xbo", inc_n=16)
        last_xbo = ev
    return last_xbo


def emit_attn_call(kb, block, c, nc, xmir):
    with ExitStack() as sctx:
        t = alloc_attn_sbuf(sctx, nc)
        with ExitStack() as pctx:
            t["psumA"] = alloc(pctx, nc, "psum", "a_psumA", [128, 2048], F32)
            t["psumB"] = alloc(pctx, nc, "psum", "a_psumB", [128, 1024], F32)
            t["psumTA"] = alloc(pctx, nc, "psum", "a_psumTA", [128, 1024], BF16)
            ev = emit_attn(kb, c, t, xmir)
            kb.play(block)
    return ev


def build_nc(mode="full"):
    nc = bass.Bass("TRN2", target_bir_lowering=False, num_swdge_queues=4)
    kb = KB(nc)

    X = nc.dram_tensor("x", [4096, 1024], F32, kind="ExternalInput").ap()
    OUT = nc.dram_tensor("out", [4096, 1024], F32, kind="ExternalOutput").ap()
    xmir = nc.dram_tensor("xmir", [4096, 1024], F32).ap()
    XeD = nc.dram_tensor("XeD", [NSLOT, 1024], BF16).ap()
    YD = nc.dram_tensor("YD", [NSLOT, 1024], F32).ap()
    ccin = nc.dram_tensor("ccin", [1, 16], F32).ap().opt()
    ccout = nc.dram_tensor("ccout", [8, 16], F32).ap().opt()
    GWT = nc.dram_tensor("gwt", [128, 128], F32).ap()
    CVEC = nc.dram_tensor("cvec", [128, 16], F32, kind="ExternalInput").ap()
    IOTAP = nc.dram_tensor("iotap", [128, 1], F32, kind="ExternalInput").ap()
    CMASK = nc.dram_tensor("cmask", [8, 128], F32, kind="ExternalInput").ap()
    ROPE = nc.dram_tensor("rope", [128, 4], F32, kind="ExternalInput").ap()
    IDB = nc.dram_tensor("identbf", [128, 128], BF16, kind="ExternalInput").ap()
    IDF = nc.dram_tensor("identf", [128, 128], F32, kind="ExternalInput").ap()
    GWI = nc.dram_tensor("gw_in", [128, 128], F32, kind="ExternalInput").ap()
    WQKV = nc.dram_tensor("wqkv", [24, 128, 1024], F32, kind="ExternalInput").ap()
    W1R = nc.dram_tensor("w1r", [256, 128, 2048], F32, kind="ExternalInput").ap()
    W2R = nc.dram_tensor("w2r", [256, 128, 2048], F32, kind="ExternalInput").ap()

    with ExitStack() as cctx, nc.Block() as block:
        c = {}
        sb = lambda n, s, d: alloc(cctx, nc, "sbuf", "c_" + n, s, d)
        c["identbf"] = sb("identbf", [128, 128], BF16)
        c["identf"] = sb("identf", [128, 128], F32)
        c["rope_sb"] = sb("rope", [128, 4], F32)
        c["cvec384"] = sb("cvec", [128, 16], F32)
        c["iotap"] = sb("iotap", [128, 1], F32)
        c["coremask_bc"] = sb("cmask", [8, 128], F32)
        c["gw_sb"] = sb("gw", [128, 128], BF16)
        c["epsv"] = sb("epsv", [128, 1], F32)
        c["WQKV"] = WQKV
        kb.op("vector", lambda e: e.memset(c["epsv"][:], EPS), inc="f_vch")

        for dst, src in ((c["identbf"], IDB), (c["identf"], IDF),
                         (c["rope_sb"], ROPE), (c["cvec384"], CVEC),
                         (c["iotap"], IOTAP), (c["coremask_bc"], CMASK)):
            kb.op("sync", lambda e, dst=dst, src=src: e.dma_start(dst[:], src),
                  inc="c_dma", inc_n=16)
        kb.op("gpsimd", lambda e: setattr(kb, "rbc", e.to_reg(NSLOT - 1)))
        kb.op("gpsimd", lambda e: e.dma_start(c["gw_sb"][:], GWI),
              inc="c_dma", inc_n=16)
        kb.barrier("ph", extra_waits=[("c_dma", kb.get("c_dma"))])
        kb.play(block)

        if mode == "attn":
            kb.op("sync", lambda e: e.dma_start(xmir, X), inc="c_dma", inc_n=16)
            kb.barrier("ph", extra_waits=[("c_dma", kb.get("c_dma"))])
            ev = emit_attn_call(kb, block, c, nc, xmir)
            kb.barrier("ph", extra_waits=[("a_xbo", kb.get("a_xbo"))])
            kb.op("sync", lambda e: e.dma_start(OUT, xmir),
                  inc="f_out", inc_n=16)
            kb.op("sync", lambda e: e.nop(),
                  waits=[("f_out", kb.get("f_out"))])
            kb.play(block)
            return nc

        xo1 = OUT if mode == "ffn" else xmir
        emit_ffn_call(kb, block, None, c, nc, X, xo1, W1R, W2R, XeD, YD,
                      ccin, ccout, "_f1")
        if mode == "ffn":
            kb.op("sync", lambda e: e.nop(),
                  waits=[("f_out", kb.get("f_out"))])
            kb.play(block)
            return nc

        kb.barrier("ph", extra_waits=[("f_out", kb.get("f_out"))])
        kb.play(block)
        emit_attn_call(kb, block, c, nc, xmir)
        kb.barrier("ph", extra_waits=[("a_xbo", kb.get("a_xbo"))])
        kb.play(block)
        emit_ffn_call(kb, block, None, c, nc, xmir, OUT, W1R, W2R, XeD, YD,
                      ccin, ccout, "_f2")
        kb.op("sync", lambda e: e.nop(), waits=[("f_out", kb.get("f_out"))])
        kb.play(block)
    return nc



def host_consts(core):
    identbf = np.eye(128, dtype=ml_dtypes.bfloat16)
    identf = np.eye(128, dtype=np.float32)
    cvec = np.broadcast_to(
        (np.arange(16, dtype=np.float32) * SLOTS)[None, :], (128, 16)).copy()
    iotap = np.arange(128, dtype=np.float32).reshape(128, 1)
    cmask = np.zeros((8, 128), dtype=np.float32)
    cmask[:core, :] = 1.0
    rope = np.zeros((128, 4), dtype=np.float32)
    for bl in range(2):
        bg = 2 * core + bl
        theta = 10000.0 ** (-np.arange(0, 64, 2, dtype=np.float64) / 64.0)
        ang = bg * theta
        r = np.concatenate([np.cos(ang) + np.sin(ang),
                            np.cos(ang) - np.sin(ang)])
        rv = np.tile(r, 2)
        rope[:, 2 * bl] = (rv * 0.125).astype(np.float32)
        rope[:, 2 * bl + 1] = rv.astype(np.float32)
    return {"identbf": identbf, "identf": identf, "cvec": cvec,
            "iotap": iotap, "cmask": cmask, "rope": rope}


def host_weights(Wq, Wk, Wv, gw, w1, w2):
    def wt(W):
        WT = np.ascontiguousarray(W.T)
        return WT.reshape(8, 128, 8, 128).transpose(2, 1, 0, 3).reshape(
            8, 128, 1024)
    wqkv = np.ascontiguousarray(
        np.concatenate([wt(Wq), wt(Wk), wt(Wv)], axis=0))
    gwt = np.ascontiguousarray(
        gw.T.reshape(8, 128, 16).transpose(1, 0, 2).reshape(128, 128))
    w1r = np.ascontiguousarray(
        w1.reshape(16, 8, 128, 16, 256).transpose(0, 3, 2, 1, 4)
        .reshape(256, 128, 2048))
    w2r = np.ascontiguousarray(
        w2.reshape(16, 16, 2, 128, 1024).transpose(0, 1, 3, 2, 4)
        .reshape(256, 128, 2048))
    return {"wqkv": wqkv, "gw_in": gwt, "w1r": w1r, "w2r": w2r}


def make_in_maps(x_full, Wq, Wk, Wv, gw, w1, w2):
    wts = host_weights(Wq, Wk, Wv, gw, w1, w2)
    in_maps = []
    for core in range(8):
        m = dict(wts)
        m.update(host_consts(core))
        m["x"] = np.ascontiguousarray(
            x_full[2 * core:2 * core + 2].reshape(4096, 1024))
        in_maps.append(m)
    return in_maps



_CACHED_NC = None


def kernel(L0_attn_x, L0_attn_Wq, L0_attn_Wk, L0_attn_Wv, L0_ffn_gate_w,
           L0_ffn_w1, L0_ffn_w2):
    global _CACHED_NC
    x = np.asarray(L0_attn_x, dtype=np.float32)
    Wq = np.asarray(L0_attn_Wq, dtype=np.float32)
    Wk = np.asarray(L0_attn_Wk, dtype=np.float32)
    Wv = np.asarray(L0_attn_Wv, dtype=np.float32)
    gw = np.asarray(L0_ffn_gate_w, dtype=np.float32)
    w1 = np.asarray(L0_ffn_w1, dtype=np.float32)
    w2 = np.asarray(L0_ffn_w2, dtype=np.float32)

    if _CACHED_NC is None:
        _CACHED_NC = build_nc("full")
    in_maps = make_in_maps(x, Wq, Wk, Wv, gw, w1, w2)
    res = run_bass_kernel_spmd(_CACHED_NC, in_maps, core_ids=list(range(8)))
    out = np.concatenate(
        [np.asarray(res.results[i]["out"]).reshape(2, 2048, 1024)
         for i in range(8)], axis=0)
    return out


# version 09
# speedup vs baseline: 1.1609x; 1.0039x over previous
import numpy as np
import ml_dtypes
from contextlib import ExitStack

import concourse.bass as bass
import concourse.mybir as mybir
from concourse.bass import IndirectOffsetOnAxis
from concourse.bass_utils import run_bass_kernel_spmd

F32 = mybir.dt.float32
BF16 = mybir.dt.bfloat16
U32 = mybir.dt.uint32
AF = mybir.ActivationFunctionType
ALU = mybir.AluOpType
AX = mybir.AxisListType

NCHUNK = 32
NEXP = 16
SLOTS = 384
NSLOT = NEXP * SLOTS
CAP = 2560.0
EPS = 1e-5
OOBB = 8192.0
CH = 1024
PARANOID = 0


class KB:
    def __init__(self, nc):
        self.nc = nc
        self.cnt = {}
        self.sems = {}
        self.ops = {k: [] for k in ("tensor", "vector", "scalar", "gpsimd", "sync")}
        self.sem_stack = ExitStack()

    def sem(self, name):
        if name not in self.sems:
            self.sems[name] = self.sem_stack.enter_context(self.nc.semaphore(name))
        return self.sems[name]

    def get(self, name):
        return self.cnt.get(name, 0)

    def op(self, eng, fn, waits=(), inc=None, inc_n=1):
        wait_h = [(self.sem(s), int(v)) for (s, v) in waits]
        ev = None
        sem_h = None
        if inc:
            sem_h = self.sem(inc)
            ev = self.cnt[inc] = self.cnt.get(inc, 0) + inc_n

        def play(e, fn=fn, wait_h=wait_h, sem_h=sem_h, inc_n=inc_n):
            for h, v in wait_h:
                e.wait_ge(h, v)
            r = fn(e)
            if sem_h is not None:
                r.then_inc(sem_h, inc_n)
        self.ops[eng].append(play)
        return ev

    def play(self, block):
        for eng in ("tensor", "vector", "scalar", "gpsimd", "sync"):
            ops = self.ops[eng]
            self.ops[eng] = []
            if not ops:
                continue
            sec = getattr(block, eng)

            def body(e, _ops=ops):
                for f in _ops:
                    f(e)
            sec(body)

    def barrier(self, name, extra_waits=()):
        target = self.get(name) + 5
        for eng in self.ops:
            w = list(extra_waits) if eng == "sync" else []
            self.op(eng, lambda e: e.nop(), waits=w, inc=name)
        for eng in self.ops:
            self.op(eng, lambda e: e.nop(), waits=[(name, target)])


def alloc(ctx, nc, kind, name, shape, dtype):
    if kind == "sbuf":
        return ctx.enter_context(nc.sbuf_tensor(name, shape, dtype))
    return ctx.enter_context(nc.psum_tensor(name, shape, dtype))



def alloc_ffn_sbuf(ctx, nc, tagc):
    t = {}
    sb = lambda n, s, d: alloc(ctx, nc, "sbuf", n + tagc, s, d)
    t["xstage"] = [sb(f"xstage{i}", [128, 1024], F32) for i in range(3)]
    t["bn6"] = [sb(f"bn6{i}", [128, 12], F32) for i in range(3)]
    t["mv"] = [sb(f"mv{i}", [128, 2], F32) for i in range(3)]
    t["sqv"] = [sb(f"sqv{i}", [128, 1], F32) for i in range(3)]
    t["rstd"] = [sb(f"rstd{i}", [128, 1], F32) for i in range(3)]
    t["lnbf"] = sb("lnbf", [128, NCHUNK * CH], BF16)
    t["lnt_sc"] = [sb(f"lntsc{i}", [128, 1024], BF16) for i in range(2)]
    t["lg_sc"] = [sb(f"lgsc{i}", [16, 128], F32) for i in range(2)]
    t["nm"] = [sb(f"nm{i}", [128, 1], F32) for i in range(2)]
    t["et"] = [sb(f"et{i}", [128, 16], F32) for i in range(2)]
    t["S"] = [sb(f"S{i}", [128, 1], F32) for i in range(2)]
    t["p_big"] = sb("p_big", [128, 32], F32)
    t["D0"] = [sb(f"D0{i}", [128, 16], F32) for i in range(2)]
    t["Csc"] = [sb(f"Csc{i}", [128, 16], F32) for i in range(2)]
    t["Dm"] = [sb(f"Dm{i}", [128, 16], F32) for i in range(2)]
    t["D_big"] = sb("D_big", [128, NCHUNK * 16], F32)
    t["route"] = sb("route", [16, 2 * 4096], F32)
    t["D_T"] = t["route"][:, 0:4096]
    t["incl"] = t["route"][:, 4096:8192]
    t["ranksT"] = t["route"][:, 0:4096]
    t["hist_all"] = sb("hist_all", [8, 16], F32)
    t["offs_bc"] = sb("offs_bc", [128, 16], F32)
    t["slot_big"] = sb("slot_big", [128, 32], F32)
    t["fits_big"] = sb("fits_big", [128, 32], F32)
    t["coef_big"] = sb("coef_big", [128, 32], F32)
    t["g_big"] = sb("g_big", [128, 32], U32)
    for nm_ in ("sc16a", "sc16b", "sc16c"):
        t[nm_] = [sb(f"{nm_}{i}", [128, 16], F32) for i in range(2)]
    for nm_ in ("sc1a", "sc1b", "sc1c", "sc1d", "sc1e", "sc1f", "sc1g", "sc1h",
                "oobv"):
        t[nm_] = [sb(f"{nm_}{i}", [128, 1], F32) for i in range(2)]
    t["xet"] = [sb(f"xet{i}", [128, 8 * 384], BF16) for i in range(2)]
    t["w1b"] = [sb(f"w1b{i}", [128, 2048], BF16) for i in range(4)]
    t["w2b"] = [sb(f"w2b{i}", [128, 2048], BF16) for i in range(4)]
    t["ht"] = [sb(f"ht{i}", [128, 384], BF16) for i in range(2)]
    t["yev"] = [sb(f"yev{i}", [128, 3072], F32) for i in range(2)]
    t["outg"] = [sb(f"outg{i}", [128, 1024], F32) for i in range(2)]
    t["xstage2"] = t["xstage"]
    t["xnew"] = [sb(f"xnew{i}", [128, 1024], F32) for i in range(2)]
    return t


def emit_ffn_p1(kb, c, t, xin, W1R, W2R):
    pf_w1 = []
    pf_w2 = []
    for gs in range(4):
        pf_w1.append(kb.op("gpsimd", lambda e, gs=gs: e.dma_start(
            t["w1b"][gs][:], W1R[gs:gs + 1], max_dma_last_dim=8192),
            inc="f_w1", inc_n=16))
        pf_w2.append(kb.op("gpsimd", lambda e, gs=gs: e.dma_start(
            t["w2b"][gs][:], W2R[gs:gs + 1], max_dma_last_dim=8192),
            inc="f_w2", inc_n=16))
    ev_ln = [None] * NCHUNK
    ev_lnT = [None] * NCHUNK
    ev_gate = [None] * NCHUNK
    ev_lg = [None] * NCHUNK
    ev_D = [None] * NCHUNK
    ev_DT = [None] * NCHUNK

    pend_B = []

    def emit_B(args):
        j, q, e3 = args
        e4 = kb.op("scalar", lambda e, q=q: e.activation(
            t["rstd"][q][:], t["sqv"][q][:], AF.Exp, scale=-0.5),
            waits=[("f_sch", e3)], inc="f_sch")
        ev_ln[j] = kb.op("vector", lambda e, q=q, j=j: e.tensor_scalar(
            t["lnbf"][:, j * CH:(j + 1) * CH], t["xstage"][q][:],
            t["mv"][q][:, 0:1], t["rstd"][q][:], ALU.subtract, ALU.mult),
            waits=[("f_sch", e4)], inc="f_ln")

    for j in range(NCHUNK):
        q = j % 3
        w = [("f_ln", ev_ln[j - 3])] if j >= 3 else []
        ev_dx = kb.op("sync", lambda e, q=q, j=j: e.dma_start(
            t["xstage"][q][:], xin[j * 128:(j + 1) * 128, :]),
            waits=w, inc="f_dx", inc_n=16)
        kb.op("vector", lambda e, q=q: e.bn_stats(
            t["bn6"][q][:, 0:6], t["xstage"][q][:, 0:512]),
            waits=[("f_dx", ev_dx)])
        e1 = kb.op("vector", lambda e, q=q: e.bn_stats(
            t["bn6"][q][:, 6:12], t["xstage"][q][:, 512:1024]), inc="f_vch")
        e2 = kb.op("vector", lambda e, q=q: e.bn_aggr(
            t["mv"][q][:], t["bn6"][q][:]), waits=[("f_vch", e1)], inc="f_vch")
        e3 = kb.op("scalar", lambda e, q=q: e.activation(
            t["sqv"][q][:], t["mv"][q][:, 1:2], AF.Ln,
            bias=c["epsv"][:, 0:1]),
            waits=[("f_vch", e2)], inc="f_sch")
        pend_B.append((j, q, e3))
        if len(pend_B) > 2:
            emit_B(pend_B.pop(0))
    while pend_B:
        emit_B(pend_B.pop(0))

    for j in range(NCHUNK):
        q = j % 2
        ev = None
        for ci in range(8):
            w = []
            if ci == 0:
                w.append(("f_ln", ev_ln[j]))
                if j >= 2:
                    w.append(("f_lnT", ev_lnT[j - 2]))
            ev = kb.op("tensor", lambda e, q=q, j=j, ci=ci: e.transpose(
                t["psumT"][q][:, ci * 128:(ci + 1) * 128],
                t["lnbf"][:, j * CH + ci * 128: j * CH + (ci + 1) * 128],
                c["identbf"][:]),
                waits=w, inc="f_tT" if ci == 7 else None)
        w = [("f_tT", ev)]
        if j >= 2:
            w.append(("f_gate", ev_gate[j - 2]))
        ev_lnT[j] = kb.op("scalar", lambda e, q=q: e.copy(
            t["lnt_sc"][q][:], t["psumT"][q][:]), waits=w, inc="f_lnT")

        ev = None
        for ci in range(8):
            w = []
            if ci == 0:
                w.append(("f_lnT", ev_lnT[j]))
                if j >= 2:
                    w.append(("f_lg", ev_lg[j - 2]))
            ev = kb.op("tensor", lambda e, q=q, ci=ci: e.matmul(
                t["psumG"][q][:], c["gw_sb"][:, ci * 16:(ci + 1) * 16],
                t["lnt_sc"][q][:, ci * 128:(ci + 1) * 128],
                start=(ci == 0), stop=(ci == 7)),
                waits=w, inc="f_gate" if ci == 7 else None)
        ev_gate[j] = ev
        ev_lg[j] = kb.op("scalar", lambda e, q=q: e.copy(
            t["lg_sc"][q][:], t["psumG"][q][:]),
            waits=[("f_gate", ev_gate[j])], inc="f_lg")
        w = [("f_lg", ev_lg[j])]
        if j >= 2:
            w.append(("f_D", ev_D[j - 2]))
        ev_lT = kb.op("tensor", lambda e, q=q: e.transpose(
            t["psumL"][q][:], t["lg_sc"][q][:], c["identf"][0:16, 0:16]),
            waits=w, inc="f_lT")

        e5 = kb.op("vector", lambda e, q=q: e.reduce_max(
            t["nm"][q][:], t["psumL"][q][:], axis=AX.X, negate=True),
            waits=[("f_lT", ev_lT)], inc="f_vch")
        e6 = kb.op("scalar", lambda e, q=q: e.activation(
            t["et"][q][:], t["psumL"][q][:], AF.Exp, bias=t["nm"][q][:, 0:1],
            accum_out=t["S"][q][:]), waits=[("f_vch", e5)], inc="f_sch")
        kb.op("vector", lambda e, q=q, j=j: e.reciprocal(
            t["p_big"][:, j:j + 1], t["S"][q][:]), waits=[("f_sch", e6)])
        e7 = kb.op("vector", lambda e, q=q: e.tensor_scalar(
            t["D0"][q][:], t["psumL"][q][:], t["nm"][q][:, 0:1], 0.0,
            ALU.add, ALU.is_ge), inc="f_vch")
        e8 = kb.op("vector", lambda e, q=q: e.tensor_tensor_scan(
            t["Csc"][q][:], t["D0"][q][:], t["D0"][q][:], 0.0,
            ALU.add, ALU.bypass), waits=[("f_vch", e7)], inc="f_vch")
        e9 = kb.op("vector", lambda e, q=q: e.tensor_scalar(
            t["Dm"][q][:], t["Csc"][q][:], 1.0, None, ALU.is_le),
            waits=[("f_vch", e8)], inc="f_vch")
        ev_D[j] = kb.op("vector", lambda e, q=q, j=j: e.tensor_tensor(
            t["D_big"][:, j * 16:(j + 1) * 16], t["D0"][q][:], t["Dm"][q][:],
            ALU.mult), waits=[("f_vch", e9)], inc="f_D")

        w = [("f_D", ev_D[j])]
        if j >= 2:
            w.append(("f_DT", ev_DT[j - 2]))
        evt = kb.op("tensor", lambda e, q=q, j=j: e.transpose(
            t["psumD"][q][:], t["D_big"][:, j * 16:(j + 1) * 16],
            c["identf"][:]), waits=w, inc="f_tD")
        ev_DT[j] = kb.op("scalar", lambda e, q=q, j=j: e.copy(
            t["D_T"][:, j * 128:(j + 1) * 128], t["psumD"][q][:]),
            waits=[("f_tD", evt)], inc="f_DT")

    return {"ev_D": ev_D, "ev_DT": ev_DT, "pf_w1": pf_w1, "pf_w2": pf_w2}


def emit_ffn_p2(kb, c, t, ev1, XeD, ccin, ccout):
    ev_scan = kb.op("vector", lambda e: e.tensor_tensor_scan(
        t["incl"], t["D_T"], t["D_T"], 0.0, ALU.add, ALU.bypass),
        waits=[("f_DT", ev1["ev_DT"][NCHUNK - 1])], inc="f_vch")
    ev_rk = kb.op("vector", lambda e: e.tensor_tensor(
        t["ranksT"], t["incl"], t["D_T"], ALU.subtract),
        waits=[("f_vch", ev_scan)], inc="f_rk")
    ev_h = kb.op("sync", lambda e: e.dma_start(
        ccin, t["incl"][:, 4095:4096]), waits=[("f_rk", ev_rk)],
        inc="f_hist", inc_n=16)
    ev_cc = kb.op("gpsimd", lambda e: e.collective_compute(
        "AllGather", ALU.bypass, replica_groups=[list(range(8))],
        ins=[ccin], outs=[ccout]), waits=[("f_hist", ev_h)], inc="f_cc")
    ev_hb = kb.op("scalar", lambda e: e.dma_start(
        t["hist_all"][:], ccout), waits=[("f_cc", ev_cc)],
        inc="f_ccb", inc_n=16)

    ev_sg = [None] * NCHUNK
    ev_scat = [None] * NCHUNK
    ev_D = ev1["ev_D"]
    for j in range(NCHUNK):
        q = j % 2
        w = [("f_rk", ev_rk)]
        if j >= 2:
            w.append(("f_slot", ev_sg[j - 2]))
        ev_rtp = kb.op("tensor", lambda e, q=q, j=j: e.transpose(
            t["psumR"][q][:], t["ranksT"][:, j * 128:(j + 1) * 128],
            c["identf"][0:16, 0:16]), waits=w, inc="f_rtp")

        Dj = t["D_big"][:, j * 16:(j + 1) * 16]
        e1a = kb.op("vector", lambda e, q=q, Dj=Dj: e.tensor_tensor(
            t["sc16a"][q][:], t["psumR"][q][:], Dj, ALU.mult),
            waits=[("f_rtp", ev_rtp), ("f_D", ev_D[j])], inc="f_vch")
        e1 = kb.op("vector", lambda e, q=q, j=j: e.reduce_sum(
            t["slot_big"][:, j:j + 1], t["sc16a"][q][:], axis=AX.X),
            waits=[("f_vch", e1a)], inc="f_vch")
        e2 = kb.op("vector", lambda e, j=j: e.tensor_scalar(
            t["fits_big"][:, j:j + 1], t["slot_big"][:, j:j + 1],
            float(SLOTS), None, ALU.is_lt), waits=[("f_vch", e1)], inc="f_vch")
        e3a = kb.op("vector", lambda e, q=q, Dj=Dj: e.tensor_tensor(
            t["sc16b"][q][:], c["cvec384"][:], Dj, ALU.mult),
            waits=[("f_vch", e2)], inc="f_vch")
        e3 = kb.op("vector", lambda e, q=q: e.reduce_sum(
            t["sc1a"][q][:], t["sc16b"][q][:], axis=AX.X),
            waits=[("f_vch", e3a)], inc="f_vch")
        e4 = kb.op("vector", lambda e, q=q, j=j: e.tensor_tensor(
            t["sc1b"][q][:], t["sc1a"][q][:], t["slot_big"][:, j:j + 1],
            ALU.add), waits=[("f_vch", e3)], inc="f_vch")
        e5 = kb.op("vector", lambda e, q=q, j=j: e.tensor_scalar(
            t["oobv"][q][:], c["iotap"][:], OOBB + 128.0 * j, None, ALU.add),
            waits=[("f_vch", e4)], inc="f_vch")
        e6 = kb.op("vector", lambda e, q=q: e.tensor_tensor(
            t["sc1c"][q][:], t["sc1b"][q][:], t["oobv"][q][:], ALU.subtract),
            waits=[("f_vch", e5)], inc="f_vch")
        e7 = kb.op("vector", lambda e, q=q, j=j: e.scalar_tensor_tensor(
            t["sc1d"][q][:], t["sc1c"][q][:], t["fits_big"][:, j:j + 1],
            t["oobv"][q][:], ALU.mult, ALU.add), waits=[("f_vch", e6)],
            inc="f_vch")
        ev_sg[j] = kb.op("vector", lambda e, q=q, j=j: e.tensor_copy(
            t["g_big"][:, j:j + 1], t["sc1d"][q][:]),
            waits=[("f_vch", e7)], inc="f_slot")

        ev_scat[j] = kb.op("gpsimd", lambda e, j=j: e.indirect_dma_start(
            XeD, IndirectOffsetOnAxis(t["g_big"][:, j:j + 1], 0),
            t["lnbf"][:, j * CH:(j + 1) * CH], None,
            bounds_check=kb.rbc, oob_is_err=False),
            waits=[("f_slot", ev_sg[j])], inc="f_scat", inc_n=16)

    ev_om = kb.op("tensor", lambda e: e.matmul(
        t["psumO"][:], c["coremask_bc"][:], t["hist_all"][:],
        start=True, stop=True),
        waits=[("f_ccb", ev_hb), ("f_slot", ev_sg[NCHUNK - 1])], inc="f_om")
    ev_offs = kb.op("scalar", lambda e: e.copy(
        t["offs_bc"][:], t["psumO"][:]), waits=[("f_om", ev_om)],
        inc="f_offs")

    return {"ev_scat": ev_scat, "ev_offs": ev_offs, "ev_D": ev_D}


def emit_ffn_p34(kb, c, t, ev2, W1R, W2R, XeD, YD, xin, xout, ev1):
    ev_coef = [None] * NCHUNK
    for j in range(NCHUNK):
        q = j % 2
        Dj = t["D_big"][:, j * 16:(j + 1) * 16]
        w = [("f_offs", ev2["ev_offs"])] if j == 0 else []
        e1a = kb.op("vector", lambda e, q=q, Dj=Dj: e.tensor_tensor(
            t["sc16c"][q][:], t["offs_bc"][:], Dj, ALU.mult),
            waits=w, inc="f_vch")
        e1 = kb.op("vector", lambda e, q=q: e.reduce_sum(
            t["sc1e"][q][:], t["sc16c"][q][:], axis=AX.X),
            waits=[("f_vch", e1a)], inc="f_vch")
        e2 = kb.op("vector", lambda e, q=q, j=j: e.tensor_tensor(
            t["sc1f"][q][:], t["sc1e"][q][:], t["slot_big"][:, j:j + 1],
            ALU.add), waits=[("f_vch", e1)], inc="f_vch")
        e3 = kb.op("vector", lambda e, q=q: e.tensor_scalar(
            t["sc1g"][q][:], t["sc1f"][q][:], CAP, None, ALU.is_lt),
            waits=[("f_vch", e2)], inc="f_vch")
        e4 = kb.op("vector", lambda e, q=q, j=j: e.tensor_tensor(
            t["sc1h"][q][:], t["sc1g"][q][:], t["fits_big"][:, j:j + 1],
            ALU.mult), waits=[("f_vch", e3)], inc="f_vch")
        ev_coef[j] = kb.op("vector", lambda e, q=q, j=j: e.tensor_scalar(
            t["coef_big"][:, j:j + 1], t["sc1h"][q][:],
            t["p_big"][:, j:j + 1], 0.5, ALU.mult, ALU.mult),
            waits=[("f_vch", e4)], inc="f_coef")

    scat_join = ev2["ev_scat"][NCHUNK - 1]
    if PARANOID & 1:
        kb.barrier("pb1", extra_waits=[("f_scat", kb.get("f_scat"))])
        scat_join = 0
    ev_xet = [None] * NEXP
    ev_w1 = [None] * (NEXP * 16)
    ev_w2 = [None] * (NEXP * 16)
    ev_w1[0:4] = ev1["pf_w1"]
    ev_w2[0:4] = ev1["pf_w2"]
    ev_yev = [None] * NEXP
    ev_ydma = [None] * NEXP
    b_hmm = kb.get("f_hmm")
    b_h = kb.get("f_h")
    b_htu = kb.get("f_htu")
    ev_hmm = [b_hmm + 1 + i for i in range(NEXP * 32)]
    ev_h = [b_h + 1 + i for i in range(NEXP * 32)]
    ev_hu = [b_htu + 1 + i for i in range(NEXP * 32)]

    for ei in range(NEXP):
        q = ei % 2
        ev = None
        for ci in range(8):
            w = []
            if ci == 0:
                w.append(("f_scat", scat_join))
                if ei >= 2:
                    w.append(("f_hmm", ev_hmm[(ei - 2) * 32 + 31]))
            ev = kb.op("sync", lambda e, q=q, ei=ei, ci=ci: e.dma_start(
                t["xet"][q][:, ci * 384:(ci + 1) * 384],
                XeD[ei * SLOTS:(ei + 1) * SLOTS, ci * 128:(ci + 1) * 128],
                transpose=True), waits=w, inc="f_xet", inc_n=16)
        ev_xet[ei] = ev

        for sg in range(16):
            gs = ei * 16 + sg
            if gs < 4:
                continue
            qq = gs % 4
            w = [("f_hmm", ev_hmm[(gs - 4) * 2 + 1])]
            ev_w1[gs] = kb.op("gpsimd", lambda e, qq=qq, gs=gs:
                e.dma_start(t["w1b"][qq][:], W1R[gs:gs + 1],
                            max_dma_last_dim=8192),
                waits=w, inc="f_w1", inc_n=16)
            w = [("f_htu", ev_hu[(gs - 4) * 2 + 1])]
            ev_w2[gs] = kb.op("gpsimd", lambda e, qq=qq, gs=gs:
                e.dma_start(t["w2b"][qq][:], W2R[gs:gs + 1],
                            max_dma_last_dim=8192),
                waits=w, inc="f_w2", inc_n=16)

        pend_w2 = None

        def emit_w2(args, ei=ei):
            gs_, gft_, hq_, qq_, fj_, ft_ = args
            ev = None
            for st in range(3):
                for dh in range(2):
                    w = []
                    if st == 0 and dh == 0:
                        w.append(("f_h", ev_h[gft_]))
                        w.append(("f_w2", ev_w2[gs_]))
                        if ft_ == 0 and ei >= 1:
                            w.append(("f_yev", ev_yev[ei - 1]))
                    ev = kb.op("tensor",
                        lambda e, hq=hq_, qq=qq_, st=st, dh=dh, fj=fj_,
                        ft=ft_: e.matmul(
                            t["psumY"][:, st * 1024 + dh * 512:
                                       st * 1024 + dh * 512 + 512],
                            t["ht"][hq][:, st * 128:(st + 1) * 128],
                            t["w2b"][qq][:, fj * 1024 + dh * 512:
                                          fj * 1024 + dh * 512 + 512],
                            start=(ft == 0), stop=(ft == 31)),
                        waits=w,
                        inc="f_htu" if (st == 2 and dh == 1) else None)
            assert ev == ev_hu[gft_], (ev, ev_hu[gft_])

        for sg in range(16):
            gs = ei * 16 + sg
            qq = gs % 4
            for fj in range(2):
                ft = sg * 2 + fj
                gft = ei * 32 + ft
                hq = ft % 2
                ev = None
                for ci in range(8):
                    w = []
                    if ci == 0:
                        w.append(("f_w1", ev_w1[gs]))
                        if ft == 0:
                            w.append(("f_xet", ev_xet[ei]))
                            if ei == 0:
                                w.append(("f_offs", ev2["ev_offs"]))
                        if gft >= 2:
                            w.append(("f_h", ev_h[gft - 2]))
                    ev = kb.op("tensor",
                        lambda e, q=q, qq=qq, hq=hq, ci=ci, fj=fj: e.matmul(
                            t["psumH"][hq][:],
                            t["w1b"][qq][:, ci * 256 + fj * 128:
                                          ci * 256 + fj * 128 + 128],
                            t["xet"][q][:, ci * 384:(ci + 1) * 384],
                            start=(ci == 0), stop=(ci == 7)),
                        waits=w, inc="f_hmm" if ci == 7 else None)
                assert ev == ev_hmm[gft], (ev, ev_hmm[gft])
                w = [("f_hmm", ev_hmm[gft])]
                if gft >= 2:
                    w.append(("f_htu", ev_hu[gft - 2]))
                evg = kb.op("scalar", lambda e, hq=hq: e.activation(
                    t["ht"][hq][:], t["psumH"][hq][:], AF.Gelu_apprx_tanh),
                    waits=w, inc="f_h")
                assert evg == ev_h[gft], (evg, ev_h[gft])
                if pend_w2 is not None:
                    emit_w2(pend_w2)
                pend_w2 = (gs, gft, hq, qq, fj, ft)
        emit_w2(pend_w2)

        w = [("f_htu", ev_hu[ei * 32 + 31])]
        if ei >= 1:
            w.append(("f_y", ev_ydma[ei - 1]))
        ev_yev[ei] = kb.op("vector", lambda e, q=q: e.tensor_copy(
            t["yev"][q][:], t["psumY"][:]), waits=w, inc="f_yev")
        ev_ydma[ei] = kb.op("gpsimd", lambda e, q=q, ei=ei: e.dma_start(
            YD[ei * SLOTS:(ei + 1) * SLOTS, :].rearrange(
                "(st sp) d -> sp st d", sp=128),
            t["yev"][q][:].rearrange("p (st d) -> p st d", st=3)),
            waits=[("f_yev", ev_yev[ei])],
            inc="f_y", inc_n=16)

    ev_ms = None
    for q in range(2):
        ev_ms = kb.op("vector", lambda e, q=q: e.memset(
            t["outg"][q][:], 0.0), inc="f_vch")
    y_join = ev_ydma[NEXP - 1]
    if PARANOID & 2:
        kb.barrier("pb2", extra_waits=[("f_y", y_join)])
        y_join = 0
    ev_comb = [None] * NCHUNK
    ev_out = [None] * NCHUNK
    for j in range(NCHUNK):
        q = j % 2
        w = [("f_y", y_join)]
        if j >= 2:
            w.append(("f_comb", ev_comb[j - 2]))
        else:
            w.append(("f_vch", ev_ms))
        ev_g = kb.op("gpsimd", lambda e, q=q, j=j: e.indirect_dma_start(
            t["outg"][q][:], None,
            YD, IndirectOffsetOnAxis(t["g_big"][:, j:j + 1], 0),
            bounds_check=kb.rbc, oob_is_err=False),
            waits=w, inc="f_g", inc_n=16)
        w = [("f_comb", ev_comb[j - 2])] if j >= 2 else []
        ev_x2 = kb.op("sync", lambda e, q=q, j=j: e.dma_start(
            t["xstage2"][q][:], xin[j * 128:(j + 1) * 128, :]),
            waits=w, inc="f_x2", inc_n=16)
        w = [("f_g", ev_g), ("f_x2", ev_x2), ("f_coef", ev_coef[j])]
        if j >= 2:
            w.append(("f_out", ev_out[j - 2]))
        ev_comb[j] = kb.op("vector", lambda e, q=q, j=j:
            e.scalar_tensor_tensor(
                t["xnew"][q][:], t["outg"][q][:], t["coef_big"][:, j:j + 1],
                t["xstage2"][q][:], ALU.mult, ALU.add),
            waits=w, inc="f_comb")
        ev_out[j] = kb.op("sync", lambda e, q=q, j=j: e.dma_start(
            xout[j * 128:(j + 1) * 128, :], t["xnew"][q][:]),
            waits=[("f_comb", ev_comb[j])], inc="f_out", inc_n=16)
    return ev_out[NCHUNK - 1]


def emit_ffn_call(kb, block, ctx_consts, c, nc, xin, xout, W1R, W2R, XeD, YD,
                  ccin, ccout, tagc):
    with ExitStack() as sctx:
        t = alloc_ffn_sbuf(sctx, nc, tagc)
        with ExitStack() as pctx:
            ps = lambda n, s, d: alloc(pctx, nc, "psum", n + tagc, s, d)
            t["psumT"] = [ps(f"psumT{i}", [128, 1024], BF16) for i in range(2)]
            t["psumG"] = [ps(f"psumG{i}", [16, 128], F32) for i in range(2)]
            t["psumL"] = [ps(f"psumL{i}", [128, 16], F32) for i in range(2)]
            t["psumD"] = [ps(f"psumD{i}", [16, 128], F32) for i in range(2)]
            ev1 = emit_ffn_p1(kb, c, t, xin, W1R, W2R)
            kb.play(block)
        with ExitStack() as pctx:
            ps = lambda n, s, d: alloc(pctx, nc, "psum", n + tagc, s, d)
            t["psumR"] = [ps(f"psumR{i}", [128, 16], F32) for i in range(2)]
            t["psumO"] = ps("psumO", [128, 16], F32)
            ev2 = emit_ffn_p2(kb, c, t, ev1, XeD, ccin, ccout)
            kb.play(block)
        with ExitStack() as pctx:
            ps = lambda n, s, d: alloc(pctx, nc, "psum", n + tagc, s, d)
            t["psumH"] = [ps(f"psumH{i}", [128, 384], F32) for i in range(2)]
            t["psumY"] = ps("psumY", [128, 3072], F32)
            ev_last = emit_ffn_p34(kb, c, t, ev2, W1R, W2R, XeD, YD, xin,
                                    xout, ev1)
            kb.play(block)
    return ev_last




def alloc_attn_sbuf(ctx, nc):
    t = {}
    sb = lambda n, s, d: alloc(ctx, nc, "sbuf", "a_" + n, s, d)
    t["xb"] = sb("xb", [128, 16 * 1024], F32)
    t["bn6"] = [sb(f"bn6{i}", [128, 12], F32) for i in range(2)]
    t["mv"] = [sb(f"mv{i}", [128, 2], F32) for i in range(2)]
    t["sqv"] = [sb(f"sqv{i}", [128, 1], F32) for i in range(2)]
    t["rstd"] = [sb(f"rstd{i}", [128, 1], F32) for i in range(2)]
    t["aln"] = [sb(f"aln{i}", [128, 1024], BF16) for i in range(2)]
    t["lnT"] = sb("lnT", [128, 16384], BF16)
    t["wq"] = [sb(f"wq{i}", [128, 1024], BF16) for i in range(2)]
    t["wk"] = [sb(f"wk{i}", [128, 1024], BF16) for i in range(2)]
    t["wv"] = [sb(f"wv{i}", [128, 1024], BF16) for i in range(2)]
    t["qpair"] = sb("qpair", [128, 2048], BF16)
    t["kpair"] = sb("kpair", [128, 2048], BF16)
    t["vaug"] = [sb(f"vaug{i}", [128, 16 * 65], BF16) for i in range(2)]
    t["expT"] = sb("expT", [128, 16 * 512], BF16)
    t["avst"] = [sb(f"avst{i}", [65, 512], F32) for i in range(2)]
    t["rec"] = [sb(f"rec{i}", [128, 1], F32) for i in range(2)]
    return t


def emit_attn(kb, c, t, xmir):
    lnT_v = t["lnT"][:].rearrange("p (c s) -> p c s", c=8)
    psumTA_v = t["psumTA"][:].rearrange("p (c s) -> p c s", c=8)

    last_nrm = None
    prev_ep_last = [None]
    prev_av = [None]
    last_ave = None
    last_av = None
    last_xbo = None
    last_xp_2 = {0: None, 1: None}
    wq_user = {0: None, 1: None}
    wk_user = {0: None, 1: None}
    wv_user = {0: None, 1: None}

    for b in range(2):
        base_tok = b * 2048
        ev_xb = [None] * 16
        ev_lnT = [None] * 16
        ev_tT = [None] * 16
        for i in range(16):
            w = []
            if b == 1 and i == 0:
                w.append(("a_xbo", last_xbo))
            ev_xb[i] = kb.op("sync", lambda e, i=i, base_tok=base_tok:
                e.dma_start(t["xb"][:, i * 1024:(i + 1) * 1024],
                            xmir[base_tok + i * 128: base_tok + (i + 1) * 128, :]),
                waits=w, inc="a_dxb", inc_n=16)
            q = i % 2
            kb.op("vector", lambda e, q=q, i=i: e.bn_stats(
                t["bn6"][q][:, 0:6], t["xb"][:, i * 1024:i * 1024 + 512]),
                waits=[("a_dxb", ev_xb[i])])
            e1 = kb.op("vector", lambda e, q=q, i=i: e.bn_stats(
                t["bn6"][q][:, 6:12], t["xb"][:, i * 1024 + 512:(i + 1) * 1024]),
                inc="a_vch")
            e2 = kb.op("vector", lambda e, q=q: e.bn_aggr(
                t["mv"][q][:], t["bn6"][q][:]), waits=[("a_vch", e1)],
                inc="a_vch")
            e3 = kb.op("scalar", lambda e, q=q: e.activation(
                t["sqv"][q][:], t["mv"][q][:, 1:2], AF.Ln,
                bias=c["epsv"][:, 0:1]),
                waits=[("a_vch", e2)], inc="a_sch")
            e4 = kb.op("scalar", lambda e, q=q: e.activation(
                t["rstd"][q][:], t["sqv"][q][:], AF.Exp, scale=-0.5),
                waits=[("a_sch", e3)], inc="a_sch")
            w = [("a_sch", e4)]
            if i >= 2:
                w.append(("a_tT", ev_tT[i - 2]))
            ev_ln = kb.op("vector", lambda e, q=q, i=i: e.tensor_scalar(
                t["aln"][q][:], t["xb"][:, i * 1024:(i + 1) * 1024],
                t["mv"][q][:, 0:1], t["rstd"][q][:], ALU.subtract, ALU.mult),
                waits=w, inc="a_ln")
            ev = None
            for ci in range(8):
                w = []
                if ci == 0:
                    w.append(("a_ln", ev_ln))
                    if i >= 2:
                        w.append(("a_lnT", ev_lnT[i - 2]))
                    if b == 1 and i == 0 and wv_user[0] is not None:
                        w.append(("a_vmm", max(x for x in wv_user.values()
                                               if x is not None)))
                ev = kb.op("tensor", lambda e, q=q, ci=ci: e.transpose(
                    t["psumTA"][:, ci * 128:(ci + 1) * 128],
                    t["aln"][q][:, ci * 128:(ci + 1) * 128],
                    c["identbf"][:]),
                    waits=w, inc="a_tT" if ci == 7 else None)
            ev_tT[i] = ev
            ev_lnT[i] = kb.op("scalar", lambda e, i=i: e.copy(
                lnT_v[:, :, i * 128:(i + 1) * 128], psumTA_v),
                waits=[("a_tT", ev_tT[i])], inc="a_lnT")

        mh = [None, None]
        for h in range(2):
            w = []
            if h == 0 and last_av is not None:
                w.append(("a_av", last_av))
            mh[h] = kb.op("vector", lambda e, h=h: e.memset(
                t["vaug"][h][:], 1.0), waits=w, inc="a_vch")

        for dt in range(8):
            qd = dt % 2
            for m, nmw, user in ((0, "wq", wq_user), (1, "wk", wk_user),
                                 (2, "wv", wv_user)):
                w = []
                if user[qd] is not None:
                    w.append((("a_qmm", "a_kmm", "a_vmm")[m], user[qd]))
                kb.op("gpsimd", lambda e, qd=qd, dt=dt, m=m, nmw=nmw:
                    e.dma_start(t[nmw][qd][:],
                                c["WQKV"][m * 8 + dt:m * 8 + dt + 1],
                                max_dma_last_dim=8192),
                    waits=w, inc=f"a_dw{m}", inc_n=16)
            ev_dw = [kb.get(f"a_dw{m}") for m in range(3)]

            ev = None
            for g in range(4):
                for ci in range(8):
                    w = []
                    if g == 0 and ci == 0:
                        w.append(("a_dw0", ev_dw[0]))
                        w.append(("a_lnT", ev_lnT[15]))
                        if last_nrm is not None:
                            w.append(("a_nrm", last_nrm))
                        if prev_ep_last[0] is not None:
                            w.append(("a_exp", prev_ep_last[0]))
                        if last_ave is not None:
                            w.append(("a_ave", last_ave))
                    ev = kb.op("tensor", lambda e, qd=qd, g=g, ci=ci: e.matmul(
                        t["psumA"][:, g * 512:(g + 1) * 512],
                        t["wq"][qd][:, ci * 128:(ci + 1) * 128],
                        t["lnT"][:, ci * 2048 + g * 512:
                                 ci * 2048 + (g + 1) * 512],
                        start=(ci == 0), stop=(ci == 7)),
                        waits=w, inc="a_qmm" if (g == 3 and ci == 7) else None)
            ev_qmm = ev
            wq_user[qd] = ev_qmm
            ev = None
            for g in range(4):
                ev = kb.op("vector", lambda e, g=g, b=b: e.tensor_scalar(
                    t["qpair"][:, g * 512:(g + 1) * 512],
                    t["psumA"][:, g * 512:(g + 1) * 512],
                    c["rope_sb"][:, 2 * b:2 * b + 1], None, ALU.mult),
                    waits=([("a_qmm", ev_qmm)] if g == 0 else ()),
                    inc="a_qe" if g == 3 else None)
            ev_qe = ev
            ev = None
            for g in range(4):
                for ci in range(8):
                    w = []
                    if g == 0 and ci == 0:
                        w.append(("a_dw1", ev_dw[1]))
                        w.append(("a_qe", ev_qe))
                    ev = kb.op("tensor", lambda e, qd=qd, g=g, ci=ci: e.matmul(
                        t["psumA"][:, g * 512:(g + 1) * 512],
                        t["wk"][qd][:, ci * 128:(ci + 1) * 128],
                        t["lnT"][:, ci * 2048 + g * 512:
                                 ci * 2048 + (g + 1) * 512],
                        start=(ci == 0), stop=(ci == 7)),
                        waits=w, inc="a_kmm" if (g == 3 and ci == 7) else None)
            ev_kmm = ev
            wk_user[qd] = ev_kmm
            ev = None
            for g in range(4):
                ev = kb.op("vector", lambda e, g=g, b=b: e.tensor_scalar(
                    t["kpair"][:, g * 512:(g + 1) * 512],
                    t["psumA"][:, g * 512:(g + 1) * 512],
                    c["rope_sb"][:, 2 * b + 1:2 * b + 2], None, ALU.mult),
                    waits=([("a_kmm", ev_kmm)] if g == 0 else ()),
                    inc="a_ke" if g == 3 else None)
            ev_ke = ev

            ev_vve = [None] * 16
            for i in range(16):
                pb = i % 2
                ev = None
                for ci in range(8):
                    w = []
                    if ci == 0:
                        w.append(("a_dw2", ev_dw[2]))
                        if i == 0:
                            w.append(("a_vch", mh[1]))
                            if last_ave is not None:
                                w.append(("a_ave", last_ave))
                            if last_nrm is not None:
                                w.append(("a_nrm", last_nrm))
                        if i >= 2:
                            w.append(("a_vve", ev_vve[i - 2]))
                    ev = kb.op("tensor", lambda e, qd=qd, pb=pb, i=i, ci=ci:
                        e.matmul(
                            t["psumB"][:, pb * 512:pb * 512 + 128],
                            lnT_v[:, ci, i * 128:(i + 1) * 128],
                            t["wv"][qd][:, ci * 128:(ci + 1) * 128],
                            start=(ci == 0), stop=(ci == 7)),
                        waits=w, inc="a_vmm" if ci == 7 else None)
                ev_vmm_i = ev
                wv_user[qd] = ev_vmm_i
                for h in range(2):
                    ev_vve[i] = kb.op("vector", lambda e, pb=pb, i=i, h=h:
                        e.tensor_copy(
                            t["vaug"][h][:, i * 65:i * 65 + 64],
                            t["psumB"][:, pb * 512 + h * 64:
                                       pb * 512 + (h + 1) * 64]),
                        waits=([("a_vmm", ev_vmm_i)] if h == 0 else ()),
                        inc="a_vve")
            ev_ve = ev_vve[15]

            for h in range(2):
                hb = h * 64
                for g in range(4):
                    ev_sc = [None] * 16
                    ev_ep = [None] * 8
                    av_ev = [None] * 16

                    def emit_sc(kc, h=h, g=g, hb=hb, ev_sc=ev_sc, ev_ep=ev_ep):
                        bnk = kc % 4
                        w = []
                        if kc == 0 and g == 0 and h == 0:
                            w.append(("a_ke", ev_ke))
                        if kc >= 4:
                            w.append(("a_exp", ev_ep[(kc - 4) // 2]))
                        elif prev_ep_last[0] is not None:
                            w.append(("a_exp", prev_ep_last[0]))
                        ev_sc[kc] = kb.op("tensor",
                            lambda e, hb=hb, g=g, kc=kc, bnk=bnk: e.matmul(
                                t["psumA"][:, bnk * 512:(bnk + 1) * 512],
                                t["kpair"][hb:hb + 64, kc * 128:(kc + 1) * 128],
                                t["qpair"][hb:hb + 64, g * 512:(g + 1) * 512],
                                start=True, stop=True),
                            waits=w, inc="a_sc")

                    def emit_ep(p, ev_sc=ev_sc, ev_ep=ev_ep):
                        w = [("a_sc", ev_sc[2 * p + 1])]
                        if prev_av[0] is not None:
                            w.append(("a_av", prev_av[0][2 * p + 1]))
                        half = p % 2
                        ev_ep[p] = kb.op("scalar",
                            lambda e, p=p, half=half: e.activation(
                                t["expT"][:, p * 1024:(p + 1) * 1024],
                                t["psumA"][:, half * 1024:(half + 1) * 1024],
                                AF.Exp),
                            waits=w, inc="a_exp")

                    def emit_av(kc, h=h, g=g, ev_ep=ev_ep, av_ev=av_ev,
                                w_first=()):
                        w = [("a_exp", ev_ep[kc // 2])] + list(w_first)
                        av_ev[kc] = kb.op("tensor", lambda e, h=h, kc=kc:
                            e.matmul(
                                t["psumB"][0:65, 0:512],
                                t["vaug"][h][:, kc * 65:(kc + 1) * 65],
                                t["expT"][:, kc * 512:(kc + 1) * 512],
                                start=(kc == 0), stop=(kc == 15)),
                            waits=w, inc="a_av")

                    for kc in range(4):
                        emit_sc(kc)
                    for p in range(8):
                        emit_ep(p)
                        if p < 6:
                            emit_sc(2 * p + 4)
                            emit_sc(2 * p + 5)
                        for kc in (2 * p, 2 * p + 1):
                            wf = []
                            if kc == 0:
                                if last_ave is not None:
                                    wf.append(("a_ave", last_ave))
                                if g == 0 and h == 0:
                                    wf.append(("a_vve", ev_ve))
                            emit_av(kc, w_first=wf)
                    prev_ep_last[0] = ev_ep[7]
                    prev_av[0] = av_ev
                    last_av = av_ev[15]

                    qg = g % 2
                    w = [("a_av", av_ev[15])]
                    if last_xp_2[qg] is not None:
                        w.append(("a_xp", last_xp_2[qg]))
                    ev_ave = kb.op("vector", lambda e, qg=qg: e.tensor_copy(
                        t["avst"][qg][:], t["psumB"][0:65, 0:512]),
                        waits=w, inc="a_ave")
                    last_ave = ev_ave
                    ev = None
                    for tt in range(4):
                        w = []
                        if tt == 0:
                            w.append(("a_ave", ev_ave))
                            if last_nrm is not None:
                                w.append(("a_nrm", last_nrm))
                        ev = kb.op("tensor", lambda e, qg=qg, tt=tt:
                            e.transpose(
                                t["psumB"][:, 512 + tt * 65:
                                           512 + (tt + 1) * 65],
                                t["avst"][qg][:, tt * 128:(tt + 1) * 128],
                                c["identf"][0:65, 0:65]),
                            waits=w, inc="a_xp" if tt == 3 else None)
                    ev_xp = ev
                    last_xp_2[qg] = ev_xp
                    for tt in range(4):
                        rq = tt % 2
                        e1 = kb.op("vector", lambda e, rq=rq, tt=tt:
                            e.reciprocal(
                                t["rec"][rq][:],
                                t["psumB"][:, 512 + tt * 65 + 64:
                                           512 + tt * 65 + 65]),
                            waits=([("a_xp", ev_xp)] if tt == 0 else ()),
                            inc="a_vch")
                        col = (g * 4 + tt) * 1024 + (dt * 2 + h) * 64
                        last_nrm = kb.op("vector",
                            lambda e, rq=rq, tt=tt, col=col:
                            e.scalar_tensor_tensor(
                                t["xb"][:, col:col + 64],
                                t["psumB"][:, 512 + tt * 65:
                                           512 + tt * 65 + 64],
                                t["rec"][rq][:],
                                t["xb"][:, col:col + 64],
                                ALU.mult, ALU.add),
                            waits=[("a_vch", e1)], inc="a_nrm")

        ev = None
        for i in range(16):
            w = [("a_nrm", last_nrm)] if i == 0 else []
            ev = kb.op("sync", lambda e, i=i, base_tok=base_tok: e.dma_start(
                xmir[base_tok + i * 128: base_tok + (i + 1) * 128, :],
                t["xb"][:, i * 1024:(i + 1) * 1024]),
                waits=w, inc="a_xbo", inc_n=16)
        last_xbo = ev
    return last_xbo


def emit_attn_call(kb, block, c, nc, xmir):
    with ExitStack() as sctx:
        t = alloc_attn_sbuf(sctx, nc)
        with ExitStack() as pctx:
            t["psumA"] = alloc(pctx, nc, "psum", "a_psumA", [128, 2048], F32)
            t["psumB"] = alloc(pctx, nc, "psum", "a_psumB", [128, 1024], F32)
            t["psumTA"] = alloc(pctx, nc, "psum", "a_psumTA", [128, 1024], BF16)
            ev = emit_attn(kb, c, t, xmir)
            kb.play(block)
    return ev


def build_nc(mode="full"):
    nc = bass.Bass("TRN2", target_bir_lowering=False, num_swdge_queues=4)
    kb = KB(nc)

    X = nc.dram_tensor("x", [4096, 1024], F32, kind="ExternalInput").ap()
    OUT = nc.dram_tensor("out", [4096, 1024], F32, kind="ExternalOutput").ap()
    xmir = nc.dram_tensor("xmir", [4096, 1024], F32).ap()
    XeD = nc.dram_tensor("XeD", [NSLOT, 1024], BF16).ap()
    YD = nc.dram_tensor("YD", [NSLOT, 1024], F32).ap()
    ccin = nc.dram_tensor("ccin", [1, 16], F32).ap().opt()
    ccout = nc.dram_tensor("ccout", [8, 16], F32).ap().opt()
    GWT = nc.dram_tensor("gwt", [128, 128], F32).ap()
    CVEC = nc.dram_tensor("cvec", [128, 16], F32, kind="ExternalInput").ap()
    IOTAP = nc.dram_tensor("iotap", [128, 1], F32, kind="ExternalInput").ap()
    CMASK = nc.dram_tensor("cmask", [8, 128], F32, kind="ExternalInput").ap()
    ROPE = nc.dram_tensor("rope", [128, 4], F32, kind="ExternalInput").ap()
    IDB = nc.dram_tensor("identbf", [128, 128], BF16, kind="ExternalInput").ap()
    IDF = nc.dram_tensor("identf", [128, 128], F32, kind="ExternalInput").ap()
    GWI = nc.dram_tensor("gw_in", [128, 128], F32, kind="ExternalInput").ap()
    WQKV = nc.dram_tensor("wqkv", [24, 128, 1024], F32, kind="ExternalInput").ap()
    W1R = nc.dram_tensor("w1r", [256, 128, 2048], F32, kind="ExternalInput").ap()
    W2R = nc.dram_tensor("w2r", [256, 128, 2048], F32, kind="ExternalInput").ap()

    with ExitStack() as cctx, nc.Block() as block:
        c = {}
        sb = lambda n, s, d: alloc(cctx, nc, "sbuf", "c_" + n, s, d)
        c["identbf"] = sb("identbf", [128, 128], BF16)
        c["identf"] = sb("identf", [128, 128], F32)
        c["rope_sb"] = sb("rope", [128, 4], F32)
        c["cvec384"] = sb("cvec", [128, 16], F32)
        c["iotap"] = sb("iotap", [128, 1], F32)
        c["coremask_bc"] = sb("cmask", [8, 128], F32)
        c["gw_sb"] = sb("gw", [128, 128], BF16)
        c["epsv"] = sb("epsv", [128, 1], F32)
        c["WQKV"] = WQKV
        kb.op("vector", lambda e: e.memset(c["epsv"][:], EPS), inc="f_vch")

        for dst, src in ((c["identbf"], IDB), (c["identf"], IDF),
                         (c["rope_sb"], ROPE), (c["cvec384"], CVEC),
                         (c["iotap"], IOTAP), (c["coremask_bc"], CMASK)):
            kb.op("sync", lambda e, dst=dst, src=src: e.dma_start(dst[:], src),
                  inc="c_dma", inc_n=16)
        kb.op("gpsimd", lambda e: setattr(kb, "rbc", e.to_reg(NSLOT - 1)))
        kb.op("gpsimd", lambda e: e.dma_start(c["gw_sb"][:], GWI),
              inc="c_dma", inc_n=16)
        kb.barrier("ph", extra_waits=[("c_dma", kb.get("c_dma"))])
        kb.play(block)

        if mode == "attn":
            kb.op("sync", lambda e: e.dma_start(xmir, X), inc="c_dma", inc_n=16)
            kb.barrier("ph", extra_waits=[("c_dma", kb.get("c_dma"))])
            ev = emit_attn_call(kb, block, c, nc, xmir)
            kb.barrier("ph", extra_waits=[("a_xbo", kb.get("a_xbo"))])
            kb.op("sync", lambda e: e.dma_start(OUT, xmir),
                  inc="f_out", inc_n=16)
            kb.op("sync", lambda e: e.nop(),
                  waits=[("f_out", kb.get("f_out"))])
            kb.play(block)
            return nc

        xo1 = OUT if mode == "ffn" else xmir
        emit_ffn_call(kb, block, None, c, nc, X, xo1, W1R, W2R, XeD, YD,
                      ccin, ccout, "_f1")
        if mode == "ffn":
            kb.op("sync", lambda e: e.nop(),
                  waits=[("f_out", kb.get("f_out"))])
            kb.play(block)
            return nc

        kb.barrier("ph", extra_waits=[("f_out", kb.get("f_out"))])
        kb.play(block)
        emit_attn_call(kb, block, c, nc, xmir)
        kb.barrier("ph", extra_waits=[("a_xbo", kb.get("a_xbo"))])
        kb.play(block)
        emit_ffn_call(kb, block, None, c, nc, xmir, OUT, W1R, W2R, XeD, YD,
                      ccin, ccout, "_f2")
        kb.op("sync", lambda e: e.nop(), waits=[("f_out", kb.get("f_out"))])
        kb.play(block)
    return nc



def host_consts(core):
    identbf = np.eye(128, dtype=ml_dtypes.bfloat16)
    identf = np.eye(128, dtype=np.float32)
    cvec = np.broadcast_to(
        (np.arange(16, dtype=np.float32) * SLOTS)[None, :], (128, 16)).copy()
    iotap = np.arange(128, dtype=np.float32).reshape(128, 1)
    cmask = np.zeros((8, 128), dtype=np.float32)
    cmask[:core, :] = 1.0
    rope = np.zeros((128, 4), dtype=np.float32)
    for bl in range(2):
        bg = 2 * core + bl
        theta = 10000.0 ** (-np.arange(0, 64, 2, dtype=np.float64) / 64.0)
        ang = bg * theta
        r = np.concatenate([np.cos(ang) + np.sin(ang),
                            np.cos(ang) - np.sin(ang)])
        rv = np.tile(r, 2)
        rope[:, 2 * bl] = (rv * 0.125).astype(np.float32)
        rope[:, 2 * bl + 1] = rv.astype(np.float32)
    return {"identbf": identbf, "identf": identf, "cvec": cvec,
            "iotap": iotap, "cmask": cmask, "rope": rope}


def host_weights(Wq, Wk, Wv, gw, w1, w2):
    def wt(W):
        WT = np.ascontiguousarray(W.T)
        return WT.reshape(8, 128, 8, 128).transpose(2, 1, 0, 3).reshape(
            8, 128, 1024)
    wqkv = np.ascontiguousarray(
        np.concatenate([wt(Wq), wt(Wk), wt(Wv)], axis=0))
    gwt = np.ascontiguousarray(
        gw.T.reshape(8, 128, 16).transpose(1, 0, 2).reshape(128, 128))
    w1r = np.ascontiguousarray(
        w1.reshape(16, 8, 128, 16, 256).transpose(0, 3, 2, 1, 4)
        .reshape(256, 128, 2048))
    w2r = np.ascontiguousarray(
        w2.reshape(16, 16, 2, 128, 1024).transpose(0, 1, 3, 2, 4)
        .reshape(256, 128, 2048))
    return {"wqkv": wqkv, "gw_in": gwt, "w1r": w1r, "w2r": w2r}


def make_in_maps(x_full, Wq, Wk, Wv, gw, w1, w2):
    wts = host_weights(Wq, Wk, Wv, gw, w1, w2)
    in_maps = []
    for core in range(8):
        m = dict(wts)
        m.update(host_consts(core))
        m["x"] = np.ascontiguousarray(
            x_full[2 * core:2 * core + 2].reshape(4096, 1024))
        in_maps.append(m)
    return in_maps



_CACHED_NC = None


def kernel(L0_attn_x, L0_attn_Wq, L0_attn_Wk, L0_attn_Wv, L0_ffn_gate_w,
           L0_ffn_w1, L0_ffn_w2):
    global _CACHED_NC
    x = np.asarray(L0_attn_x, dtype=np.float32)
    Wq = np.asarray(L0_attn_Wq, dtype=np.float32)
    Wk = np.asarray(L0_attn_Wk, dtype=np.float32)
    Wv = np.asarray(L0_attn_Wv, dtype=np.float32)
    gw = np.asarray(L0_ffn_gate_w, dtype=np.float32)
    w1 = np.asarray(L0_ffn_w1, dtype=np.float32)
    w2 = np.asarray(L0_ffn_w2, dtype=np.float32)

    if _CACHED_NC is None:
        _CACHED_NC = build_nc("full")
    in_maps = make_in_maps(x, Wq, Wk, Wv, gw, w1, w2)
    res = run_bass_kernel_spmd(_CACHED_NC, in_maps, core_ids=list(range(8)))
    out = np.concatenate(
        [np.asarray(res.results[i]["out"]).reshape(2, 2048, 1024)
         for i in range(8)], axis=0)
    return out
